# version 08 (final)
# baseline (speedup 1.0000x reference)
import contextlib
import math
import numpy as np

import concourse.bass as bass
import concourse.mybir as mybir
from concourse.bass_utils import run_bass_kernel_spmd

F32 = mybir.dt.float32
BF16 = mybir.dt.bfloat16
U8 = mybir.dt.uint8
AF = mybir.ActivationFunctionType
OP = mybir.AluOpType
AX = mybir.AxisListType

T = 4096
D = 2048
ND = D // 128
NTT = T // 512
INV_SQRT_DK = 1.0 / math.sqrt(128.0)


def blk(w):
    O, I = w.shape
    return np.ascontiguousarray(
        w.reshape(O // 128, 128, I // 128, 128).transpose(2, 0, 3, 1))

def vecT(b):
    return np.ascontiguousarray(b.reshape(-1, 128).T)

def host_consts():
    import ml_dtypes
    ident = np.eye(128, dtype=np.float32)
    ma = np.zeros((4, 128, 512), dtype=np.float32)
    pp = np.arange(128)[:, None]
    ff = np.arange(512)[None, :]
    for d in range(4):
        ma[d] = (pp <= ff - d * 128)
    mc = (np.arange(64)[:, None] <= np.arange(64)[None, :]).astype(np.float32)
    tk = np.arange(128)[:, None]; tq = np.arange(128)[None, :]
    mc2 = ((tk // 64 < tq // 64) | ((tk // 64 == tq // 64) & (tk <= tq))).astype(np.float32)
    return {
        "c_ident_f": ident,
        "c_ident_b": ident.astype(ml_dtypes.bfloat16),
        "c_mask_a": ma.astype(ml_dtypes.bfloat16),
        "c_mask_c": mc2.astype(ml_dtypes.bfloat16),
    }


class Prog:
    def __init__(self, nc, block, stack):
        self.nc = nc
        self.block = block
        self.stack = stack
        self._sems = {}

    def sem(self, name):
        if name not in self._sems:
            h = self.stack.enter_context(self.nc.semaphore(name))
            self._sems[name] = [h, 0]
        return self._sems[name][0]

    def inc(self, name, k=1):
        self.sem(name)
        s = self._sems[name]
        s[1] += k
        return s[0], s[1]

    def cnt(self, name):
        self.sem(name)
        return self._sems[name][1]

    def sb(self, name, shape, dtype):
        return self.stack.enter_context(self.nc.sbuf_tensor(name, shape, dtype))

    def ps(self, name, shape, dtype=F32):
        return self.stack.enter_context(self.nc.psum_tensor(name, shape, dtype))

    def mark(self, ins, eng):
        h, n = self.inc("prog_" + eng)
        ins.then_inc(h, 1)
        return n

    def wait(self, e, eng, n=None):
        if n is None:
            n = self.cnt("prog_" + eng)
        if n > 0:
            e.wait_ge(self.sem("prog_" + eng), n)


def dma(eng, p, sname, out, in_, **kw):
    ins = eng.dma_start(out=out, in_=in_, **kw)
    h, n = p.inc(sname, 16)
    ins.then_inc(h, 16)
    return n


def emit_ln_tile(p, consts, x_sb, h_view, pb_row0, pb_row1, pb_bc, scr):
    nc, block = p.nc, p.block
    tmp, row_m, row_v, mb = (scr["tmp"], scr["row_m"], scr["row_v"], scr["mb"])

    pe_before = p.cnt("prog_pe")
    dve_before = p.cnt("prog_dve")
    act0 = p.cnt("prog_act")
    pe0 = p.cnt("prog_pe")
    dve0 = p.cnt("prog_dve")
    tmpb = scr["tmpb"]
    @block.scalar
    def _(s):
        p.wait(s, "pe", pe_before)
        p.wait(s, "dve", dve_before)
        for c in range(ND):
            if c >= 2:
                s.wait_ge(p.sem("prog_pe"), pe0 + (c - 1))
            ins = s.activation(out=tmpb[:, c % 2, :], in_=x_sb[:, c, :], func=AF.Square)
            p.mark(ins, "act")
    @block.vector
    def _(v):
        for c in range(ND):
            ins = v.tensor_copy(h_view(c), x_sb[:, c, :])
            p.mark(ins, "dve")
    @block.tensor
    def _(t):
        for c in range(ND):
            t.wait_ge(p.sem("prog_dve"), dve0 + c + 1)
            t.matmul(pb_row0[:], consts["ones_col_b"][:], h_view(c),
                     start=(c == 0), stop=(c == ND - 1))
            t.wait_ge(p.sem("prog_act"), act0 + c + 1)
            ins = t.matmul(pb_row1[:], consts["ones_col_b"][:], tmpb[:, c % 2, :],
                           start=(c == 0), stop=(c == ND - 1))
            p.mark(ins, "pe")
    pe_stats = p.cnt("prog_pe")

    @block.vector
    def _(v):
        p.wait(v, "pe", pe_stats)
        ins = v.tensor_scalar(row_m[:], pb_row0[:], 1.0 / D, None, OP.mult)
        nn = p.mark(ins, "dve")
        v.wait_ge(p.sem("prog_dve"), nn)
        ins = v.tensor_scalar(row_v[:], pb_row1[:], 1.0 / D, None, OP.mult)
        nn = p.mark(ins, "dve")
        v.wait_ge(p.sem("prog_dve"), nn)
        ins = v.tensor_tensor(tmp[0:1, :], row_m[:], row_m[:], OP.mult)
        nn = p.mark(ins, "dve")
        v.wait_ge(p.sem("prog_dve"), nn)
        ins = v.tensor_sub(row_v[:], row_v[:], tmp[0:1, :])
        p.mark(ins, "dve")
    dve_var = p.cnt("prog_dve")

    @block.scalar
    def _(s):
        p.wait(s, "dve", dve_var)
        ins = s.activation(out=row_v[:], in_=row_v[:], func=AF.Ln, bias=consts["eps_row"][:])
        nn = p.mark(ins, "act")
        s.wait_ge(p.sem("prog_act"), nn)
        ins = s.activation(out=row_v[:], in_=row_v[:], func=AF.Exp, scale=-0.5)
        p.mark(ins, "act")
    act_rstd = p.cnt("prog_act")

    @block.tensor
    def _(t):
        p.wait(t, "act", act_rstd)
        ins = t.matmul(pb_bc[:], consts["ones_row_f"][:], row_m[:], start=True, stop=True)
        p.mark(ins, "pe")
    pe_bc1 = p.cnt("prog_pe")

    @block.vector
    def _(v):
        p.wait(v, "pe", pe_bc1)
        ins = v.tensor_copy(mb[:], pb_bc[:])
        p.mark(ins, "dve")
    dve_mb = p.cnt("prog_dve")

    @block.tensor
    def _(t):
        p.wait(t, "dve", dve_mb)
        ins = t.matmul(pb_bc[:], consts["ones_row_f"][:], row_v[:], start=True, stop=True)
        p.mark(ins, "pe")
    pe_bc2 = p.cnt("prog_pe")

    @block.vector
    def _(v):
        p.wait(v, "pe", pe_bc2)
        for c in range(ND):
            ins = v.tensor_sub(tmp[:], x_sb[:, c, :], mb[:])
            nn = p.mark(ins, "dve")
            v.wait_ge(p.sem("prog_dve"), nn)
            ins = v.tensor_tensor(h_view(c), tmp[:], pb_bc[:], OP.mult)
            nn = p.mark(ins, "dve")
            v.wait_ge(p.sem("prog_dve"), nn)
    return p.cnt("prog_dve")


def emit_phase_b(p, consts, src_x, dst_x, w, s_prev=None, n_prev=0):
    nc, block = p.nc, p.block
    TTILE = 1024
    NTILE = T // TTILE

    with contextlib.ExitStack() as st:
        sb = lambda n, sh, dt: st.enter_context(nc.sbuf_tensor(n, sh, dt))
        ps = lambda n, sh: st.enter_context(nc.psum_tensor(n, sh, F32))

        x_sb = sb("b_x", [128, ND, 512], F32)
        h_sb = sb("b_h", [128, ND, TTILE], BF16)
        hf_sb = sb("b_hf", [128, 32, TTILE], BF16)
        o0_sb = sb("b_o0", [128, ND, TTILE], BF16)
        wa_sb = sb("b_wa", [128, 2, ND, 128], BF16)
        wb_sb = sb("b_wb", [128, 2, 32, 128], BF16)
        wg_sb = sb("b_wg", [128, ND, 2], F32)
        bias_sb = sb("b_bias", [128, 96], F32)
        mask_sb = sb("b_mask", [128, 2, 512], U8)
        xres_sb = sb("b_xres", [128, 2, 512], F32)
        t1_sb = sb("b_t1", [128, 512], F32)
        t2_sb = sb("b_t2", [128, 2, 512], F32)
        wgd_sb = sb("b_wgd", [128, ND], F32)
        s01_sb = sb("b_s01", [1, 32], F32)
        scr = {
            "tmp": sb("b_tmp", [128, 512], F32),
            "tmpb": sb("b_tmpb", [128, 2, 512], BF16),
            "row_m": sb("b_rm", [1, 512], F32),
            "row_v": sb("b_rv", [1, 512], F32),
            "mb": sb("b_mb", [128, 512], F32),
        }
        pb_row0 = ps("b_pr0", [1, 512])
        pb_row1 = ps("b_pr1", [1, 512])
        pb_bc = ps("b_pbc", [128, 512])
        pb_h = [ps("b_ph0", [128, 512]), ps("b_ph1", [128, 512])]
        pb_o = [ps("b_po0", [128, 512]), ps("b_po1", [128, 512])]

        s_ldx, s_ldw, s_ldr, s_out = "b_ldx", "b_hw", "b_ldr", "b_out"
        s_gw = "b_gw"

        @block.sync
        def _(sy):
            if s_prev is not None:
                sy.wait_ge(p.sem(s_prev), n_prev)
            dma(sy, p, s_ldw, wg_sb[:], w["WgT"].rearrange("(c p) e -> p c e", p=128))
            dma(sy, p, s_ldw, bias_sb[:, 0:32], w["b0a_T"][:])
            dma(sy, p, s_ldw, bias_sb[:, 32:64], w["b1a_T"][:])
            dma(sy, p, s_ldw, bias_sb[:, 64:80], w["b0b_T"][:])
            dma(sy, p, s_ldw, bias_sb[:, 80:96], w["b1b_T"][:])
        n_ldw0 = p.cnt(s_ldw)
        @block.vector
        def _(v):
            v.wait_ge(p.sem(s_ldw), n_ldw0)
            ins = v.tensor_sub(wgd_sb[:], wg_sb[:, :, 0], wg_sb[:, :, 1])
            p.mark(ins, "dve")
        dve_wgd = p.cnt("prog_dve")
        @block.tensor
        def _(t):
            p.wait(t, "dve", dve_wgd)
            ins = t.matmul(pb_row0[:, 0:16], consts["ones_col_f"][:], wgd_sb[:],
                           start=True, stop=True)
            p.mark(ins, "pe")
        pe_s01 = p.cnt("prog_pe")
        @block.vector
        def _(v):
            p.wait(v, "pe", pe_s01)
            ins = v.reduce_sum(s01_sb[:, 16:17], pb_row0[:, 0:16], axis=AX.X)
            p.mark(ins, "dve")

        def b0a(f): return bias_sb[:, f:f + 1]
        def b1a(f): return bias_sb[:, 32 + f:33 + f]
        def b0b(o): return bias_sb[:, 64 + o:65 + o]
        def b1b(o): return bias_sb[:, 80 + o:81 + o]

        wa_last = {0: 0, 1: 0}
        wb_last = {0: 0, 1: 0}
        out_cnt = {0: 0, 1: 0}
        for tile in range(NTILE):
            t0 = tile * TTILE
            for sub in range(2):
                c0 = t0 + sub * 512
                dve_before = p.cnt("prog_dve")
                pe_before = p.cnt("prog_pe")
                @block.sync
                def _(sy):
                    p.wait(sy, "dve", dve_before)
                    p.wait(sy, "pe", pe_before)
                    if s_prev is not None and tile == 0 and sub == 0:
                        sy.wait_ge(p.sem(s_prev), n_prev)
                    dma(sy, p, s_ldx, x_sb[:],
                        src_x.rearrange("(c p) t -> p c t", p=128)[:, :, c0:c0 + 512])
                n_ldx = p.cnt(s_ldx)
                @block.tensor
                def _(t):
                    t.wait_ge(p.sem(s_ldx), n_ldx)
                    p.wait(t, "act")
                    p.wait(t, "dve")
                @block.scalar
                def _(s):
                    s.wait_ge(p.sem(s_ldx), n_ldx)
                @block.vector
                def _(v):
                    v.wait_ge(p.sem(s_ldx), n_ldx)
                hv = lambda c: h_sb[:, c, sub * 512:(sub + 1) * 512]
                emit_ln_tile(p, consts, x_sb, hv, pb_row0, pb_row1, pb_bc, scr)
                dve_h = p.cnt("prog_dve")

                @block.tensor
                def _(t):
                    p.wait(t, "dve", dve_h)
                    for c in range(ND):
                        ins = t.matmul(pb_row0[:], wgd_sb[:, c:c + 1],
                                       x_sb[:, c, :],
                                       start=(c == 0), stop=(c == ND - 1))
                    p.mark(ins, "pe")
                pe_logit = p.cnt("prog_pe")
                @block.vector
                def _(v):
                    p.wait(v, "pe", pe_logit)
                    ins = v.scalar_tensor_tensor(scr["row_v"][:], scr["row_m"][:],
                                                 s01_sb[0:1, 16:17], pb_row0[:],
                                                 OP.mult, OP.subtract)
                    p.mark(ins, "dve")
                dve_diff = p.cnt("prog_dve")
                @block.tensor
                def _(t):
                    p.wait(t, "dve", dve_diff)
                    ins = t.matmul(pb_bc[:], consts["ones_row_f"][:], scr["row_v"][:],
                                   start=True, stop=True)
                    p.mark(ins, "pe")
                pe_bcm = p.cnt("prog_pe")
                @block.vector
                def _(v):
                    p.wait(v, "pe", pe_bcm)
                    ins = v.tensor_single_scalar(mask_sb[:, sub, :], pb_bc[:],
                                                 0.0, OP.is_le)
                    p.mark(ins, "dve")

            for e in range(2):
                Wa = w["W0a_blk"] if e == 0 else w["W1a_blk"]
                Wb = w["W0b_blk"] if e == 0 else w["W1b_blk"]
                peb = p.cnt("prog_pe")
                actb = p.cnt("prog_act")
                dveb = p.cnt("prog_dve")
                n_wa = {}
                for f in range(32):
                    buf = f % 2
                    s_gwb = s_gw + str(buf)
                    @block.gpsimd
                    def _(g):
                        if s_prev is not None and tile == 0 and e == 0 and f < 2:
                            g.wait_ge(p.sem(s_prev), n_prev)
                        if f >= 2:
                            g.wait_ge(p.sem("prog_pe"), peb + 2 * (f - 2) + 2)
                        elif wa_last[buf] > 0:
                            p.wait(g, "pe", wa_last[buf])
                        dma(g, p, s_gwb, wa_sb[:, buf],
                            Wa[:, f].rearrange("i p m -> p i m"))
                    n_wa[f] = p.cnt(s_gwb)
                def act_h(f, sub):
                    return (actb + 2 * f + sub + 1) if e == 0 else None
                def dve_h(f, sub):
                    return dveb + 2 * f + sub + 1
                @block.tensor
                def _(t):
                    for f in range(32):
                        buf = f % 2
                        t.wait_ge(p.sem(s_gw + str(buf)), n_wa[f])
                        if f == 0:
                            p.wait(t, "act")
                            p.wait(t, "dve")
                        for sub in range(2):
                            if f >= 1:
                                if e == 0:
                                    t.wait_ge(p.sem("prog_act"), act_h(f - 1, sub))
                                else:
                                    t.wait_ge(p.sem("prog_dve"), dve_h(f - 1, sub))
                            for c in range(ND):
                                ins = t.matmul(pb_h[sub][:], wa_sb[:, buf, c, :],
                                               h_sb[:, c, sub * 512:(sub + 1) * 512],
                                               start=(c == 0), stop=(c == ND - 1))
                            p.mark(ins, "pe")
                wa_last[0] = peb + 63
                wa_last[1] = peb + 64
                if e == 0:
                    @block.scalar
                    def _(s):
                        for f in range(32):
                            for sub in range(2):
                                s.wait_ge(p.sem("prog_pe"), peb + 2 * f + sub + 1)
                                ins = s.activation(out=hf_sb[:, f, sub * 512:(sub + 1) * 512],
                                                   in_=pb_h[sub][:], func=AF.Gelu,
                                                   bias=b0a(f))
                                p.mark(ins, "act")
                else:
                    @block.vector
                    def _(v):
                        for f in range(32):
                            for sub in range(2):
                                v.wait_ge(p.sem("prog_pe"), peb + 2 * f + sub + 1)
                                ins = v.tensor_scalar(hf_sb[:, f, sub * 512:(sub + 1) * 512],
                                                      pb_h[sub][:], b1a(f), 0.0,
                                                      OP.add, OP.max)
                                p.mark(ins, "dve")
                peb2 = p.cnt("prog_pe")
                actb2 = p.cnt("prog_act")
                dveb2 = p.cnt("prog_dve")
                n_wb = {}
                for o in range(ND):
                    buf = o % 2
                    s_gwb = s_gw + "b" + str(buf)
                    @block.gpsimd
                    def _(g):
                        if o >= 2:
                            g.wait_ge(p.sem("prog_pe"), peb2 + 2 * (o - 2) + 2)
                        elif wb_last[buf] > 0:
                            p.wait(g, "pe", wb_last[buf])
                        dma(g, p, s_gwb, wb_sb[:, buf],
                            Wb[:, o].rearrange("i p m -> p i m"))
                    n_wb[o] = p.cnt(s_gwb)
                if e == 1:
                    n_xr = {}
                    for o in range(ND):
                        @block.scalar
                        def _(s):
                            if o >= 1:
                                s.wait_ge(p.sem("prog_dve"), dveb2 + 3 * (2 * o))
                            else:
                                s.wait_ge(p.sem("prog_dve"), dveb2)
                            dma(s, p, s_ldr, xres_sb[:, 0, :],
                                src_x[o * 128:(o + 1) * 128, t0:t0 + 512])
                            dma(s, p, s_ldr, xres_sb[:, 1, :],
                                src_x[o * 128:(o + 1) * 128, t0 + 512:t0 + 1024])
                        n_xr[o] = p.cnt(s_ldr)
                @block.tensor
                def _(t):
                    for o in range(ND):
                        buf = o % 2
                        t.wait_ge(p.sem(s_gw + "b" + str(buf)), n_wb[o])
                        for sub in range(2):
                            if o >= 1:
                                if e == 0:
                                    t.wait_ge(p.sem("prog_act"), actb2 + 2 * (o - 1) + sub + 1)
                                else:
                                    t.wait_ge(p.sem("prog_dve"), dveb2 + 3 * (2 * (o - 1) + sub) + 3)
                            if o == 0 and sub == 0:
                                p.wait(t, "act")
                                p.wait(t, "dve")
                            for f in range(32):
                                ins = t.matmul(pb_o[sub][:], wb_sb[:, buf, f, :],
                                               hf_sb[:, f, sub * 512:(sub + 1) * 512],
                                               start=(f == 0), stop=(f == 31))
                            p.mark(ins, "pe")
                if e == 0:
                    @block.scalar
                    def _(s):
                        for o in range(ND):
                            for sub in range(2):
                                s.wait_ge(p.sem("prog_pe"), peb2 + 2 * o + sub + 1)
                                ins = s.activation(out=o0_sb[:, o, sub * 512:(sub + 1) * 512],
                                                   in_=pb_o[sub][:], func=AF.Identity,
                                                   bias=b0b(o))
                                p.mark(ins, "act")
                else:
                    @block.vector
                    def _(v):
                        for o in range(ND):
                            v.wait_ge(p.sem(s_ldr), n_xr[o])
                            if o == 0:
                                p.wait(v, "act")
                            for sub in range(2):
                                s_ou = s_out + str(sub)
                                n_op = out_cnt[sub]
                                v.wait_ge(p.sem("prog_pe"), peb2 + 2 * o + sub + 1)
                                if n_op > 0:
                                    v.wait_ge(p.sem(s_ou), n_op)
                                ins = v.tensor_scalar(t1_sb[:], pb_o[sub][:], b1b(o),
                                                      None, OP.add)
                                nn = p.mark(ins, "dve")
                                v.wait_ge(p.sem("prog_dve"), nn)
                                ins = v.select(t2_sb[:, sub, :], mask_sb[:, sub, :],
                                               o0_sb[:, o, sub * 512:(sub + 1) * 512],
                                               t1_sb[:], add_drain=True)
                                nn = p.mark(ins, "dve")
                                v.wait_ge(p.sem("prog_dve"), nn)
                                ins = v.tensor_add(t2_sb[:, sub, :], t2_sb[:, sub, :],
                                                   xres_sb[:, sub, :])
                                p.mark(ins, "dve")
                                out_cnt[sub] += 16
                    @block.sync
                    def _(sy):
                        for o in range(ND):
                            for sub in range(2):
                                sy.wait_ge(p.sem("prog_dve"), dveb2 + 3 * (2 * o + sub) + 3)
                                dma(sy, p, s_out + str(sub),
                                    dst_x[o * 128:(o + 1) * 128,
                                          t0 + sub * 512:t0 + (sub + 1) * 512],
                                    t2_sb[:, sub, :])
                wb_last[0] = peb2 + 31
                wb_last[1] = peb2 + 32
        n_o0 = p.cnt(s_out + "0"); n_o1 = p.cnt(s_out + "1")
        @block.sync
        def _(sy):
            sy.wait_ge(p.sem(s_out + "0"), n_o0)
            sy.wait_ge(p.sem(s_out + "1"), n_o1)
            ins = sy.engine_nop() if hasattr(sy, "engine_nop") else sy.nop()
            h, n = p.inc("b_done", 1)
            ins.then_inc(h, 1)
    return "b_done", p.cnt("b_done")



def emit_phase_a(p, consts, src_x, dst_x, ctx_hbm, w, s_prev=None, n_prev=0, dbg=None):
    nc, block = p.nc, p.block
    s_ldx, s_ldw, s_ldr, s_ldc, s_out, s_ctxw = ("a_ldx", "a_hw", "a_ldr",
                                                 "a_ldc", "a_out", "a_ctxw")
    s_gw = "a_gw"

    with contextlib.ExitStack() as stQ:
        sbQ = lambda n, sh, dt: stQ.enter_context(nc.sbuf_tensor(n, sh, dt))
        qlat = sbQ("a_qlat", [128, 8, T], BF16)
        ckv = sbQ("a_ckv", [128, 4, T], BF16)
        @block.sync
        def _(sy):
            if s_prev is not None:
                sy.wait_ge(p.sem(s_prev), n_prev)

        with contextlib.ExitStack() as stA:
            sbA = lambda n, sh, dt: stA.enter_context(nc.sbuf_tensor(n, sh, dt))
            x_sb = sbA("a_x", [128, ND, 512], F32)
            h_sb = sbA("a_h", [128, ND, 512], BF16)
            wqd_sb = sbA("a_wqd", [128, ND, 8, 128], BF16)
            wkvd_sb = sbA("a_wkvd", [128, ND, 4, 128], BF16)
            scr = {
                "tmp": sbA("a_tmp", [128, 512], F32),
            "tmpb": sbA("a_tmpb", [128, 2, 512], BF16),
                "row_m": sbA("a_rm", [1, 512], F32),
                "row_v": sbA("a_rv", [1, 512], F32),
                "mb": sbA("a_mb", [128, 512], F32),
            }
            with contextlib.ExitStack() as stP:
                psA = lambda n, sh: stP.enter_context(nc.psum_tensor(n, sh, F32))
                pb_row0 = psA("a_pr0", [1, 512])
                pb_row1 = psA("a_pr1", [1, 512])
                pb_bc = psA("a_pbc", [128, 512])
                pb_p = [psA("a_pp0", [128, 512]), psA("a_pp1", [128, 512])]

                @block.gpsimd
                def _(g):
                    for ih in range(2):
                        dma(g, p, s_gw, wqd_sb[:, ih * 8:(ih + 1) * 8],
                            w["Wqd_blk"][ih * 8:(ih + 1) * 8].rearrange("i o p m -> p i o m"))
                    dma(g, p, s_gw, wkvd_sb[:], w["Wkvd_blk"].rearrange("i o p m -> p i o m"))
                n_ldw0 = p.cnt(s_gw)

                for ts in range(NTT):
                    c0 = ts * 512
                    dve_b = p.cnt("prog_dve")
                    pe_b = p.cnt("prog_pe")
                    @block.sync
                    def _(sy):
                        p.wait(sy, "dve", dve_b)
                        p.wait(sy, "pe", pe_b)
                        dma(sy, p, s_ldx, x_sb[:],
                            src_x.rearrange("(c p) t -> p c t", p=128)[:, :, c0:c0 + 512])
                    n_ldx = p.cnt(s_ldx)
                    @block.tensor
                    def _(t):
                        t.wait_ge(p.sem(s_ldx), n_ldx)
                        if ts == 0:
                            t.wait_ge(p.sem(s_gw), n_ldw0)
                        p.wait(t, "act")
                        p.wait(t, "dve")
                    @block.scalar
                    def _(s):
                        s.wait_ge(p.sem(s_ldx), n_ldx)
                    @block.vector
                    def _(v):
                        v.wait_ge(p.sem(s_ldx), n_ldx)
                    hv = lambda c: h_sb[:, c, :]
                    dve_h = emit_ln_tile(p, consts, x_sb, hv, pb_row0, pb_row1, pb_bc, scr)

                    for proj, wsb, nO, dst in (("q", wqd_sb, 8, qlat), ("kv", wkvd_sb, 4, ckv)):
                        for o in range(nO):
                            bank = pb_p[o % 2]
                            dve_free = p.cnt("prog_dve")
                            @block.tensor
                            def _(t):
                                if proj == "q" and o == 0:
                                    p.wait(t, "dve", dve_h)
                                else:
                                    p.wait(t, "dve", dve_free)
                                for c in range(ND):
                                    ins = t.matmul(bank[:], wsb[:, c, o, :], h_sb[:, c, :],
                                                   start=(c == 0), stop=(c == ND - 1))
                                p.mark(ins, "pe")
                            pe_g = p.cnt("prog_pe")
                            @block.vector
                            def _(v):
                                p.wait(v, "pe", pe_g)
                                ins = v.tensor_copy(dst[:, o, c0:c0 + 512], bank[:])
                                p.mark(ins, "dve")

        with contextlib.ExitStack() as stH:
            sbH = lambda n, sh, dt: stH.enter_context(nc.sbuf_tensor(n, sh, dt))
            mask_a = sbH("a_mask", [128, 4, 512], BF16)
            wq_sb = sbH("a_wq", [128, 2, 8, 128], BF16)
            wk_sb = sbH("a_wk", [128, 2, 4, 128], BF16)
            wv_sb = sbH("a_wv", [128, 2, 4, 128], BF16)
            qt_sb = sbH("a_qt", [128, T], BF16)
            kt_sb = sbH("a_kt", [128, T], BF16)
            v_sb = sbH("a_v", [128, 32, 128], BF16)
            p_sb = sbH("a_p", [128, 4, 512], BF16)
            sq_sb = sbH("a_sq", [128, 2, 512], BF16)
            bc_sb = sbH("a_bc", [128, 512], F32)
            ctx_sb = sbH("a_ctx", [128, 2, 512], BF16)
            rsum_sb = sbH("a_rsum", [1, 512], F32)
            nrm_sb = sbH("a_nrm", [1, 32], F32)
            negc_sb = sbH("a_negc", [128, 1], F32)
            with contextlib.ExitStack() as stP:
                psH = lambda n, sh: stP.enter_context(nc.psum_tensor(n, sh, F32))
                pb_S = [psH("a_ps0", [128, 512]), psH("a_ps1", [128, 512]),
                        psH("a_ps2", [128, 512])]
                pb_sum = [psH("a_psum0", [1, 512]), psH("a_psum1", [1, 512])]
                pb_ctx = [psH("a_pctx0", [128, 512]), psH("a_pctx1", [128, 512])]
                pb_bc2 = psH("a_pbc2", [128, 512])

                pe_a1 = p.cnt("prog_pe"); dve_a1 = p.cnt("prog_dve"); act_a1 = p.cnt("prog_act")
                pe_a1b = p.cnt("prog_pe")
                @block.sync
                def _(sy):
                    p.wait(sy, "pe", pe_a1b)
                    p.wait(sy, "dve", dve_a1)
                    p.wait(sy, "act", act_a1)
                    dma(sy, p, s_ldw, mask_a[:], w["c_mask_a"].rearrange("d p f -> p d f"))
                n_mask = p.cnt(s_ldw)
                wql = {0: 0, 1: 0}
                @block.vector
                def _(v):
                    v.wait_ge(p.sem(s_ldw), n_mask)
                for hd in range(16):
                    buf = hd % 2
                    s_gwb = s_gw + str(buf)
                    @block.gpsimd
                    def _(g):
                        if hd < 2:
                            p.wait(g, "pe", pe_a1)
                            p.wait(g, "dve", dve_a1)
                            p.wait(g, "act", act_a1)
                        p.wait(g, "pe", wql[buf])
                        dma(g, p, s_gwb, wq_sb[:, buf],
                            w["Wqup_blk"][:, hd].rearrange("r p m -> p r m"))
                        dma(g, p, s_gwb, wk_sb[:, buf],
                            w["Wkup_blk"][:, hd].rearrange("r p m -> p r m"))
                        dma(g, p, s_gwb, wv_sb[:, buf],
                            w["Wvup_blk"][:, hd].rearrange("r p m -> p r m"))
                    n_w = p.cnt(s_gwb)

                    for ts in range(NTT):
                        for which, wsb, nR, dst in (("q", wq_sb, 8, qt_sb), ("k", wk_sb, 4, kt_sb)):
                            bank = pb_S[ts % 2]
                            dve_free = p.cnt("prog_dve")
                            act_free = p.cnt("prog_act")
                            @block.tensor
                            def _(t):
                                t.wait_ge(p.sem(s_gwb), n_w)
                                p.wait(t, "dve", dve_free)
                                p.wait(t, "act", act_free)
                                srcq = qlat if which == "q" else ckv
                                for r in range(nR):
                                    ins = t.matmul(bank[:], wsb[:, buf, r, :],
                                                   srcq[:, r, ts * 512:(ts + 1) * 512],
                                                   start=(r == 0), stop=(r == nR - 1))
                                p.mark(ins, "pe")
                            pe_g = p.cnt("prog_pe")
                            @block.vector
                            def _(v):
                                p.wait(v, "pe", pe_g)
                                ins = v.tensor_copy(dst[:, ts * 512:(ts + 1) * 512], bank[:])
                                p.mark(ins, "dve")
                    for tc in range(32):
                        bank = pb_S[tc % 2]
                        dve_free = p.cnt("prog_dve")
                        @block.tensor
                        def _(t):
                            p.wait(t, "dve", dve_free)
                            for r in range(4):
                                ins = t.matmul(bank[:, 0:128], ckv[:, r, tc * 128:(tc + 1) * 128],
                                               wv_sb[:, buf, r, :],
                                               start=(r == 0), stop=(r == 3))
                            p.mark(ins, "pe")
                        pe_g = p.cnt("prog_pe")
                        @block.vector
                        def _(v):
                            p.wait(v, "pe", pe_g)
                            ins = v.tensor_copy(v_sb[:, tc, :], bank[:, 0:128])
                            p.mark(ins, "dve")
                    wql[buf] = p.cnt("prog_pe")

                    peb = p.cnt("prog_pe")
                    actb = p.cnt("prog_act")
                    dveb = p.cnt("prog_dve")
                    srcs = [(qt_sb, ts) for ts in range(NTT)] + [(kt_sb, ts) for ts in range(NTT)]
                    @block.scalar
                    def _(s):
                        p.wait(s, "pe", peb)
                        p.wait(s, "dve", dveb)
                        for idx, (src_t, ts) in enumerate(srcs):
                            if idx >= 2:
                                s.wait_ge(p.sem("prog_pe"), peb + idx - 1)
                            ins = s.activation(out=sq_sb[:, idx % 2, :],
                                               in_=src_t[:, ts * 512:(ts + 1) * 512],
                                               func=AF.Square)
                            p.mark(ins, "act")
                    @block.tensor
                    def _(t):
                        for idx in range(16):
                            t.wait_ge(p.sem("prog_act"), actb + idx + 1)
                            if idx >= 2:
                                t.wait_ge(p.sem("prog_dve"), dveb + idx - 1)
                            ins = t.matmul(pb_sum[idx % 2][:], consts["ones_col_b"][:],
                                           sq_sb[:, idx % 2, :], start=True, stop=True)
                            p.mark(ins, "pe")
                    @block.vector
                    def _(v):
                        for idx in range(16):
                            v.wait_ge(p.sem("prog_pe"), peb + idx + 1)
                            ins = v.reduce_max(nrm_sb[0:1, idx:idx + 1],
                                               pb_sum[idx % 2][:], axis=AX.X)
                            p.mark(ins, "dve")
                    dve_n = p.cnt("prog_dve")
                    @block.vector
                    def _(v):
                        v.wait_ge(p.sem("prog_dve"), dve_n)
                        v.reduce_max(nrm_sb[0:1, 16:17], nrm_sb[0:1, 0:8], axis=AX.X)
                        ins = v.reduce_max(nrm_sb[0:1, 17:18], nrm_sb[0:1, 8:16], axis=AX.X)
                        nn = p.mark(ins, "dve")
                        v.wait_ge(p.sem("prog_dve"), nn)
                        ins = v.tensor_tensor(nrm_sb[0:1, 18:19], nrm_sb[0:1, 16:17],
                                              nrm_sb[0:1, 17:18], OP.mult)
                        p.mark(ins, "dve")
                    dve_c = p.cnt("prog_dve")
                    @block.scalar
                    def _(s):
                        p.wait(s, "dve", dve_c)
                        ins = s.activation(out=nrm_sb[0:1, 19:20], in_=nrm_sb[0:1, 18:19],
                                           func=AF.Sqrt, scale=1.0 / 128.0)
                        nn = p.mark(ins, "act")
                        s.wait_ge(p.sem("prog_act"), nn)
                        ins = s.activation(out=nrm_sb[0:1, 20:21], in_=nrm_sb[0:1, 19:20],
                                           func=AF.Copy, scale=-1.0)
                        p.mark(ins, "act")
                    act_c = p.cnt("prog_act")
                    dve_negc_f = p.cnt("prog_dve")
                    @block.tensor
                    def _(t):
                        p.wait(t, "act", act_c)
                        p.wait(t, "dve", dve_negc_f)
                        ins = t.matmul(pb_bc2[:, 0:1], consts["ones_row_f"][:],
                                       nrm_sb[0:1, 20:21], start=True, stop=True)
                        p.mark(ins, "pe")
                    pe_c = p.cnt("prog_pe")
                    @block.vector
                    def _(v):
                        p.wait(v, "pe", pe_c)
                        ins = v.tensor_copy(negc_sb[:], pb_bc2[:, 0:1])
                        p.mark(ins, "dve")
                    dve_negc = p.cnt("prog_dve")

                    for qt in range(NTT):
                        nS = 4 * qt + 4
                        q_sl = qt_sb[:, qt * 512:(qt + 1) * 512]
                        sum_bank = pb_sum[qt % 2]
                        ctx_bank = pb_ctx[qt % 2]
                        dve_qt0 = p.cnt("prog_dve")
                        pe0 = p.cnt("prog_pe")
                        act0 = p.cnt("prog_act")
                        dve0 = p.cnt("prog_dve")
                        def pe_score(c):
                            return pe0 + (1 if c == 0 else 2 * c)
                        def pe_sumpv(c):
                            return pe0 + (2 * c + 3 if c <= nS - 2 else 2 * nS)
                        def act_exp(c):
                            return act0 + c + 1
                        def dve_mask(c):
                            return dve0 + (c - (nS - 4)) + 1
                        @block.tensor
                        def _(t):
                            def emit_score(c):
                                if c == 0:
                                    p.wait(t, "dve", dve_negc)
                                    p.wait(t, "dve", dve_qt0)
                                if c >= 3:
                                    t.wait_ge(p.sem("prog_act"), act_exp(c - 3))
                                ins = t.matmul(pb_S[c % 3][:], kt_sb[:, c * 128:(c + 1) * 128],
                                               q_sl, start=True, stop=True)
                                p.mark(ins, "pe")
                            def emit_sumpv(c):
                                t.wait_ge(p.sem("prog_act"), act_exp(c))
                                if c >= nS - 4:
                                    t.wait_ge(p.sem("prog_dve"), dve_mask(c))
                                t.matmul(sum_bank[:], consts["ones_col_b"][:], p_sb[:, c % 4, :],
                                         start=(c == 0), stop=(c == nS - 1),
                                         skip_group_check=True)
                                ins = t.matmul(ctx_bank[:], v_sb[:, c, :], p_sb[:, c % 4, :],
                                               start=(c == 0), stop=(c == nS - 1),
                                               skip_group_check=True)
                                p.mark(ins, "pe")
                            emit_score(0)
                            for c in range(1, nS):
                                emit_score(c)
                                emit_sumpv(c - 1)
                            emit_sumpv(nS - 1)
                        @block.scalar
                        def _(s):
                            for c in range(nS):
                                s.wait_ge(p.sem("prog_pe"),
                                          max(pe_score(c), pe_sumpv(c - 4) if c >= 4 else 0))
                                if c >= 4 and c - 4 >= nS - 4:
                                    s.wait_ge(p.sem("prog_dve"), dve_mask(c - 4))
                                ins = s.activation(out=p_sb[:, c % 4, :], in_=pb_S[c % 3][:],
                                                   func=AF.Exp, bias=negc_sb[:],
                                                   scale=INV_SQRT_DK)
                                p.mark(ins, "act")
                        @block.vector
                        def _(v):
                            for c in range(nS - 4, nS):
                                v.wait_ge(p.sem("prog_act"), act_exp(c))
                                ins = v.tensor_tensor(p_sb[:, c % 4, :], p_sb[:, c % 4, :],
                                                      mask_a[:, c - (nS - 4), :], OP.mult)
                                p.mark(ins, "dve")
                        pe_attn = p.cnt("prog_pe")
                        @block.vector
                        def _(v):
                            p.wait(v, "pe", pe_attn)
                            ins = v.reciprocal(rsum_sb[:], sum_bank[:])
                            p.mark(ins, "dve")
                        dve_r = p.cnt("prog_dve")
                        @block.tensor
                        def _(t):
                            p.wait(t, "dve", dve_r)
                            ins = t.matmul(pb_bc2[:], consts["ones_row_f"][:], rsum_sb[:],
                                           start=True, stop=True)
                            p.mark(ins, "pe")
                        pe_bc = p.cnt("prog_pe")
                        s_cw = s_ctxw + str(qt % 2)
                        n_ctxw_prev = p.cnt(s_cw)
                        @block.vector
                        def _(v):
                            p.wait(v, "pe", pe_bc)
                            ins = v.tensor_copy(bc_sb[:], pb_bc2[:])
                            nn = p.mark(ins, "dve")
                            v.wait_ge(p.sem("prog_dve"), nn)
                            if n_ctxw_prev > 0:
                                v.wait_ge(p.sem(s_cw), n_ctxw_prev)
                            ins = v.tensor_tensor(ctx_sb[:, qt % 2, :], ctx_bank[:],
                                                  bc_sb[:], OP.mult)
                            p.mark(ins, "dve")
                        dve_ctx = p.cnt("prog_dve")
                        @block.sync
                        def _(sy):
                            p.wait(sy, "dve", dve_ctx)
                            dma(sy, p, s_cw,
                                ctx_hbm[hd * 128:(hd + 1) * 128, qt * 512:(qt + 1) * 512],
                                ctx_sb[:, qt % 2, :])

    n_cw0 = p.cnt(s_ctxw + "0")
    n_cw1 = p.cnt(s_ctxw + "1")
    with contextlib.ExitStack() as stO:
        sbO = lambda n, sh, dt: stO.enter_context(nc.sbuf_tensor(n, sh, dt))
        wo_sb = sbO("a_wo", [128, ND, ND, 128], BF16)
        bo_sb = sbO("a_bo", [128, ND], F32)
        ctxt_sb = sbO("a_ctxt", [128, 2, ND, 512], BF16)
        xres_sb = sbO("a_xres", [128, 4, 512], F32)
        t2_sb = sbO("a_t2", [128, 2, 512], F32)
        with contextlib.ExitStack() as stP:
            psO = lambda n, sh: stP.enter_context(nc.psum_tensor(n, sh, F32))
            pb_o = [psO("a_po0", [128, 512]), psO("a_po1", [128, 512])]
            pe_a2 = p.cnt("prog_pe"); dve_a2 = p.cnt("prog_dve"); act_a2 = p.cnt("prog_act")
            @block.sync
            def _(sy):
                p.wait(sy, "pe", pe_a2); p.wait(sy, "dve", dve_a2); p.wait(sy, "act", act_a2)
                sy.wait_ge(p.sem(s_ctxw + "0"), n_cw0)
                sy.wait_ge(p.sem(s_ctxw + "1"), n_cw1)
                dma(sy, p, s_ldw, bo_sb[:], w["bo_T"][:])
            n_bo_h = p.cnt(s_ldw)
            @block.gpsimd
            def _(g):
                p.wait(g, "pe", pe_a2); p.wait(g, "dve", dve_a2); p.wait(g, "act", act_a2)
                g.wait_ge(p.sem(s_ctxw + "0"), n_cw0)
                g.wait_ge(p.sem(s_ctxw + "1"), n_cw1)
                for ih in range(4):
                    dma(g, p, s_gw + "0", wo_sb[:, ih * 4:(ih + 1) * 4],
                        w["Wo_blk"][ih * 4:(ih + 1) * 4].rearrange("i o p m -> p i o m"))
            n_bo = p.cnt(s_gw + "0")
            for ts in range(NTT):
                cbuf = ts % 2
                s_lc = s_ldc + str(cbuf)
                pe_b = p.cnt("prog_pe")
                @block.sync
                def _(sy):
                    sy.wait_ge(p.sem(s_ctxw + "0"), n_cw0)
                    sy.wait_ge(p.sem(s_ctxw + "1"), n_cw1)
                    p.wait(sy, "pe", pe_b)
                    dma(sy, p, s_lc, ctxt_sb[:, cbuf],
                        ctx_hbm.rearrange("(c p) t -> p c t", p=128)[:, :, ts * 512:(ts + 1) * 512])
                n_ldc = p.cnt(s_lc)
                if dbg is not None and ts == 0:
                    @block.sync
                    def _(sy):
                        sy.wait_ge(p.sem(s_lc), n_ldc)
                        sy.wait_ge(p.sem(s_gw + "0"), n_bo)
                        dma(sy, p, "dbgd", dbg["ctxt0"], ctxt_sb[:, 0])
                        dma(sy, p, "dbgd", dbg["wo"], wo_sb[:])
                for o in range(ND):
                    dve_res = p.cnt("prog_dve")
                    s_lr = s_ldr + str(o % 4)
                    @block.scalar
                    def _(s):
                        if ts == 0 and o == 0:
                            s.wait_ge(p.sem(s_ctxw + "0"), n_cw0)
                            s.wait_ge(p.sem(s_ctxw + "1"), n_cw1)
                        p.wait(s, "dve", dve_res)
                        dma(s, p, s_lr, xres_sb[:, o % 4, :],
                            src_x[o * 128:(o + 1) * 128, ts * 512:(ts + 1) * 512])
                    n_r = p.cnt(s_lr)
                    bank = pb_o[o % 2]
                    dve_free = p.cnt("prog_dve")
                    @block.tensor
                    def _(t):
                        t.wait_ge(p.sem(s_gw + "0"), n_bo)
                        t.wait_ge(p.sem(s_ldw), n_bo_h)
                        t.wait_ge(p.sem(s_lc), n_ldc)
                        p.wait(t, "dve", dve_free)
                        for c in range(ND):
                            ins = t.matmul(bank[:], wo_sb[:, c, o, :],
                                           ctxt_sb[:, cbuf, c, :],
                                           start=(c == 0), stop=(c == ND - 1))
                        p.mark(ins, "pe")
                    pe_g = p.cnt("prog_pe")
                    s_ou = s_out + str(o % 2)
                    n_out_prev = p.cnt(s_ou)
                    @block.vector
                    def _(v):
                        p.wait(v, "pe", pe_g)
                        v.wait_ge(p.sem(s_lr), n_r)
                        if n_out_prev > 0:
                            v.wait_ge(p.sem(s_ou), n_out_prev)
                        ins = v.scalar_tensor_tensor(t2_sb[:, o % 2, :], bank[:],
                                                     bo_sb[:, o:o + 1], xres_sb[:, o % 4, :],
                                                     OP.add, OP.add)
                        p.mark(ins, "dve")
                    dve_g = p.cnt("prog_dve")
                    @block.sync
                    def _(sy):
                        p.wait(sy, "dve", dve_g)
                        dma(sy, p, s_ou,
                            dst_x[o * 128:(o + 1) * 128, ts * 512:(ts + 1) * 512],
                            t2_sb[:, o % 2, :])
    n_o0 = p.cnt(s_out + "0"); n_o1 = p.cnt(s_out + "1")
    @block.sync
    def _(sy):
        sy.wait_ge(p.sem(s_out + "0"), n_o0)
        sy.wait_ge(p.sem(s_out + "1"), n_o1)
        ins = sy.engine_nop() if hasattr(sy, "engine_nop") else sy.nop()
        h, n = p.inc("a_done", 1)
        ins.then_inc(h, 1)
    return "a_done", p.cnt("a_done")



def emit_phase_c(p, consts, src_x, dst_x, ctx_hbm, w, s_prev=None, n_prev=0):
    nc, block = p.nc, p.block
    s_ldx, s_hw, s_out, s_ow, s_ldc, s_ldr = ("c_ldx", "c_hw", "c_out",
                                              "c_ow", "c_ldc", "c_ldr")
    s_gw = "c_gw"

    with contextlib.ExitStack() as stC:
        stQ = contextlib.ExitStack()
        sbC = lambda n, sh, dt: stQ.enter_context(nc.sbuf_tensor(n, sh, dt))
        h_sb = sbC("c_h", [128, ND, T], BF16)
        fw_sb = sbC("c_fw", [128, 4, 128], BF16)
        fb_sb = sbC("c_fb", [128, 8], F32)
        mask_c = sbC("c_maskc", [128, 128], BF16)

        with contextlib.ExitStack() as stA:
            sbA = lambda n, sh, dt: stA.enter_context(nc.sbuf_tensor(n, sh, dt))
            x_sb = sbA("c_x", [128, ND, 512], F32)
            scr = {
                "tmp": sbA("c_tmp", [128, 512], F32),
            "tmpb": sbA("c_tmpb", [128, 2, 512], BF16),
                "row_m": sbA("c_rm", [1, 512], F32),
                "row_v": sbA("c_rv", [1, 512], F32),
                "mb": sbA("c_mb", [128, 512], F32),
            }
            with contextlib.ExitStack() as stP:
                psA = lambda n, sh: stP.enter_context(nc.psum_tensor(n, sh, F32))
                pb_row0 = psA("c_pr0", [1, 512])
                pb_row1 = psA("c_pr1", [1, 512])
                pb_bc = psA("c_pbc", [128, 512])

                @block.sync
                def _(sy):
                    if s_prev is not None:
                        sy.wait_ge(p.sem(s_prev), n_prev)
                    dma(sy, p, s_hw, mask_c[:], w["c_mask_c"])
                    dma(sy, p, s_hw, fb_sb[:, 0:1], w["fq1_b"])
                    dma(sy, p, s_hw, fb_sb[:, 1:2], w["fq2_b"])
                    dma(sy, p, s_hw, fb_sb[:, 2:3], w["fk1_b"])
                    dma(sy, p, s_hw, fb_sb[:, 3:4], w["fk2_b"])
                @block.gpsimd
                def _(g):
                    if s_prev is not None:
                        g.wait_ge(p.sem(s_prev), n_prev)
                    dma(g, p, "c_gf", fw_sb[:, 0, :], w["fq1_wT"])
                    dma(g, p, "c_gf", fw_sb[:, 1, :], w["fq2_wT"])
                    dma(g, p, "c_gf", fw_sb[:, 2, :], w["fk1_wT"])
                    dma(g, p, "c_gf", fw_sb[:, 3, :], w["fk2_wT"])
                n_hw0 = p.cnt(s_hw)
                n_gf = p.cnt("c_gf")
                @block.vector
                def _(v):
                    v.wait_ge(p.sem(s_hw), n_hw0)
                    ins = v.tensor_scalar(fb_sb[:, 4:5], fb_sb[:, 0:1], INV_SQRT_DK,
                                          None, OP.mult)
                    p.mark(ins, "dve")

                for ts in range(NTT):
                    c0 = ts * 512
                    dve_b = p.cnt("prog_dve")
                    pe_b = p.cnt("prog_pe")
                    @block.sync
                    def _(sy):
                        p.wait(sy, "dve", dve_b)
                        p.wait(sy, "pe", pe_b)
                        dma(sy, p, s_ldx, x_sb[:],
                            src_x.rearrange("(c p) t -> p c t", p=128)[:, :, c0:c0 + 512])
                    n_ldx = p.cnt(s_ldx)
                    @block.tensor
                    def _(t):
                        t.wait_ge(p.sem(s_ldx), n_ldx)
                        p.wait(t, "act")
                        p.wait(t, "dve")
                    @block.scalar
                    def _(s):
                        s.wait_ge(p.sem(s_ldx), n_ldx)
                    @block.vector
                    def _(v):
                        v.wait_ge(p.sem(s_ldx), n_ldx)
                    hv = lambda c: h_sb[:, c, c0:c0 + 512]
                    emit_ln_tile(p, consts, x_sb, hv, pb_row0, pb_row1, pb_bc, scr)

        with contextlib.ExitStack() as stH:
            sbH = lambda n, sh, dt: stH.enter_context(nc.sbuf_tensor(n, sh, dt))
            wq_sb = sbH("c_wq", [128, 2, ND, 128], BF16)
            wk_sb = sbH("c_wk", [128, 2, ND, 128], BF16)
            wv_sb = sbH("c_wv", [128, 2, ND, 128], BF16)
            qt_sb = sbH("c_qt", [128, T], BF16)
            kt_sb = sbH("c_kt", [128, T], BF16)
            vt_sb = sbH("c_vt", [128, T], BF16)
            qp_sb = sbH("c_qp", [128, T], BF16)
            kp_sb = kt_sb
            ktm_sb = sbH("c_ktm", [128, 32, 128], BF16)
            vtm_sb = sbH("c_vtm", [128, 32, 128], BF16)
            kv_sb = sbH("c_kv", [128, 128], BF16)
            o_sb = qt_sb
            t1_sb = sbH("c_t1", [128, 512], F32)
            sT_sb = sbH("c_sT", [128, 128], BF16)
            pe_c1 = p.cnt("prog_pe"); dve_c1 = p.cnt("prog_dve"); act_c1 = p.cnt("prog_act")
            with contextlib.ExitStack() as stP:
                psH = lambda n, sh, dt=F32: stP.enter_context(nc.psum_tensor(n, sh, dt))
                pb_p = [psH("c_pp0", [128, 512]), psH("c_pp1", [128, 512])]
                pb_st = psH("c_pst", [128, 128])
                pb_o = [psH("c_po0", [128, 512]), psH("c_po1", [128, 512])]
                pb_sT = [psH("c_pt0", [128, 128])]
                pb_tx = [psH("c_ptx0", [128, 128], BF16), psH("c_ptx1", [128, 128], BF16)]

                wql = {0: 0, 1: 0}
                for hd in range(16):
                    buf = hd % 2
                    s_gwb = s_gw + str(buf)
                    @block.gpsimd
                    def _(g):
                        if hd < 2:
                            p.wait(g, "pe", pe_c1); p.wait(g, "dve", dve_c1)
                            p.wait(g, "act", act_c1)
                        p.wait(g, "pe", wql[buf])
                        dma(g, p, s_gwb, wq_sb[:, buf],
                            w["wq_blk"][:, hd].rearrange("i p m -> p i m"))
                        dma(g, p, s_gwb, wk_sb[:, buf],
                            w["wk_blk"][:, hd].rearrange("i p m -> p i m"))
                        dma(g, p, s_gwb, wv_sb[:, buf],
                            w["wv_blk"][:, hd].rearrange("i p m -> p i m"))
                    n_w = p.cnt(s_gwb)

                    n_ow_head = p.cnt(s_ow)
                    for which, wsb, dst in (("q", wq_sb, qt_sb), ("k", wk_sb, kt_sb),
                                            ("v", wv_sb, vt_sb)):
                        for ts in range(NTT):
                            bank = pb_p[ts % 2]
                            dve_free = p.cnt("prog_dve")
                            act_free = p.cnt("prog_act")
                            @block.tensor
                            def _(t):
                                t.wait_ge(p.sem(s_gwb), n_w)
                                p.wait(t, "dve", dve_free)
                                p.wait(t, "act", act_free)
                                for c in range(ND):
                                    ins = t.matmul(bank[:], wsb[:, buf, c, :],
                                                   h_sb[:, c, ts * 512:(ts + 1) * 512],
                                                   start=(c == 0), stop=(c == ND - 1))
                                p.mark(ins, "pe")
                            pe_g = p.cnt("prog_pe")
                            @block.vector
                            def _(v):
                                p.wait(v, "pe", pe_g)
                                if which == "q" and ts == 0 and n_ow_head > 0:
                                    v.wait_ge(p.sem(s_ow), n_ow_head)
                                ins = v.tensor_copy(dst[:, ts * 512:(ts + 1) * 512], bank[:])
                                p.mark(ins, "dve")
                    wql[buf] = p.cnt("prog_pe")

                    peb3 = p.cnt("prog_pe")
                    actb3 = p.cnt("prog_act")
                    dveb3 = p.cnt("prog_dve")
                    titers = [("q", qt_sb, qp_sb, 0, 1, 4, ts) for ts in range(NTT)] + \
                             [("k", kt_sb, kp_sb, 2, 3, 2, ts) for ts in range(NTT)]
                    def tbanks(k):
                        return (pb_p[0], pb_p[1]) if k % 2 == 0 else (pb_o[0], pb_o[1])
                    @block.tensor
                    def _(t):
                        for k, (which, src_t, dst, i1, i2, bscl, ts) in enumerate(titers):
                            bA, bB = tbanks(k)
                            if k == 0:
                                p.wait(t, "dve", dveb3)
                                p.wait(t, "act", actb3)
                                if hd == 0:
                                    t.wait_ge(p.sem(s_hw), n_hw0)
                                    t.wait_ge(p.sem("c_gf"), n_gf)
                            if k >= 2:
                                t.wait_ge(p.sem("prog_act"), actb3 + k - 1)
                                t.wait_ge(p.sem("prog_dve"), dveb3 + k - 1)
                            t.matmul(bA[:], fw_sb[:, i1, :],
                                     src_t[:, ts * 512:(ts + 1) * 512],
                                     start=True, stop=True, skip_group_check=True)
                            ins = t.matmul(bB[:], fw_sb[:, i2, :],
                                           src_t[:, ts * 512:(ts + 1) * 512],
                                           start=True, stop=True, skip_group_check=True)
                            p.mark(ins, "pe")
                    @block.scalar
                    def _(s):
                        for k, (which, src_t, dst, i1, i2, bscl, ts) in enumerate(titers):
                            bA, bB = tbanks(k)
                            s.wait_ge(p.sem("prog_pe"), peb3 + k + 1)
                            if k >= 1:
                                s.wait_ge(p.sem("prog_dve"), dveb3 + k)
                            if which == "q":
                                ins = s.activation(out=t1_sb[:], in_=bA[:],
                                                   func=AF.Identity,
                                                   bias=fb_sb[:, bscl:bscl + 1],
                                                   scale=INV_SQRT_DK)
                            else:
                                ins = s.activation(out=t1_sb[:], in_=bA[:],
                                                   func=AF.Identity,
                                                   bias=fb_sb[:, bscl:bscl + 1])
                            p.mark(ins, "act")
                    @block.vector
                    def _(v):
                        for k, (which, src_t, dst, i1, i2, bscl, ts) in enumerate(titers):
                            bA, bB = tbanks(k)
                            v.wait_ge(p.sem("prog_act"), actb3 + k + 1)
                            ins = v.scalar_tensor_tensor(
                                dst[:, ts * 512:(ts + 1) * 512], bB[:],
                                fb_sb[:, i2:i2 + 1], t1_sb[:], OP.add, OP.mult)
                            p.mark(ins, "dve")
                    dve_tf = p.cnt("prog_dve")
                    peb4 = p.cnt("prog_pe")
                    dveb4 = p.cnt("prog_dve")
                    txit = [("k", kp_sb, ktm_sb, tc) for tc in range(32)] + \
                           [("v", vt_sb, vtm_sb, tc) for tc in range(32)]
                    @block.tensor
                    def _(t):
                        for k, (which, src_t, dst, tc) in enumerate(txit):
                            if k == 0:
                                p.wait(t, "dve", dveb4)
                            if k >= 2:
                                t.wait_ge(p.sem("prog_dve"), dveb4 + k - 1)
                            ins = t.transpose(pb_tx[k % 2][:],
                                              src_t[:, tc * 128:(tc + 1) * 128],
                                              consts["ident_b"][:])
                            p.mark(ins, "pe")
                    @block.vector
                    def _(v):
                        for k, (which, src_t, dst, tc) in enumerate(txit):
                            v.wait_ge(p.sem("prog_pe"), peb4 + k + 1)
                            ins = v.tensor_copy(dst[:, tc, :], pb_tx[k % 2][:])
                            p.mark(ins, "dve")

                    n_ow_prev = p.cnt(s_ow)
                    dve_o_prev = {0: None, 1: None}
                    pe_upd_prev = None
                    for j in range(32):
                        ob = j % 2
                        q_sl = qp_sb[:, j * 128:(j + 1) * 128]
                        ktm_sl = ktm_sb[:, j, :]
                        vtm_sl = vtm_sb[:, j, :]
                        if j > 0:
                            @block.vector
                            def _(v):
                                p.wait(v, "pe", pe_upd_prev)
                                ins = v.tensor_copy(kv_sb[:], pb_st[:])
                                p.mark(ins, "dve")
                        dve_kv = p.cnt("prog_dve")
                        dve_ofree = dve_o_prev[ob]
                        @block.tensor
                        def _(t):
                            if j == 0:
                                if n_ow_prev > 0:
                                    t.wait_ge(p.sem(s_ow), n_ow_prev)
                                p.wait(t, "dve", dve_tf)
                            if j > 0:
                                p.wait(t, "dve", dve_kv)
                                t.matmul(pb_o[ob][:, 0:128], kv_sb[:], q_sl,
                                         start=True, stop=False, skip_group_check=True)
                            ins = t.matmul(pb_sT[0][:], kp_sb[:, j * 128:(j + 1) * 128],
                                           q_sl, start=True, stop=True,
                                           skip_group_check=True)
                            p.mark(ins, "pe")
                        pe_sT = p.cnt("prog_pe")
                        @block.vector
                        def _(v):
                            p.wait(v, "pe", pe_sT)
                            ins = v.tensor_tensor(sT_sb[:], pb_sT[0][:],
                                                  mask_c[:], OP.mult)
                            p.mark(ins, "dve")
                        dve_sT = p.cnt("prog_dve")
                        @block.tensor
                        def _(t):
                            p.wait(t, "dve", dve_sT)
                            if dve_ofree is not None:
                                p.wait(t, "dve", dve_ofree)
                            t.matmul(pb_o[ob][:, 0:128], vtm_sl, sT_sb[:],
                                     start=(j == 0), stop=True, skip_group_check=True)
                            ins = t.matmul(pb_st[:], ktm_sl, vtm_sl,
                                           start=(j == 0), stop=(j == 31),
                                           skip_group_check=True)
                            p.mark(ins, "pe")
                        pe_upd_prev = p.cnt("prog_pe")
                        @block.vector
                        def _(v):
                            p.wait(v, "pe", pe_upd_prev)
                            ins = v.tensor_copy(o_sb[:, j * 128:(j + 1) * 128],
                                                pb_o[ob][:, 0:128])
                            p.mark(ins, "dve")
                        dve_o_prev[ob] = p.cnt("prog_dve")
                    dve_o_all = p.cnt("prog_dve")
                    @block.sync
                    def _(sy):
                        p.wait(sy, "dve", dve_o_all)
                        dma(sy, p, s_ow,
                            ctx_hbm[hd * 128:(hd + 1) * 128, :], o_sb[:])

        stQ.close()
        n_ow0 = p.cnt(s_ow)
        pe_c3 = p.cnt("prog_pe"); dve_c3 = p.cnt("prog_dve"); act_c3 = p.cnt("prog_act")
        with contextlib.ExitStack() as stO:
            sbO = lambda n, sh, dt: stO.enter_context(nc.sbuf_tensor(n, sh, dt))
            wo_sb = sbO("c_wo", [128, ND, ND, 128], BF16)
            nw_sb = sbO("c_nw", [128, 1], F32)
            ot_sb = sbO("c_ot", [128, 2, ND, 512], BF16)
            sq_sb = sbO("c_sq", [128, 512], BF16)
            bc_sb = sbO("c_bc", [128, 512], F32)
            row_sb = sbO("c_row", [1, 512], F32)
            xres_sb = sbO("c_xres", [128, 4, 512], F32)
            t2_sb = sbO("c_t2", [128, 2, 512], F32)
            with contextlib.ExitStack() as stP:
                psO = lambda n, sh: stP.enter_context(nc.psum_tensor(n, sh, F32))
                pb_oo = [psO("c_poo0", [128, 512]), psO("c_poo1", [128, 512])]
                pb_row = psO("c_prow", [1, 512])
                pb_bc2 = psO("c_pbc2", [128, 512])
                @block.sync
                def _(sy):
                    p.wait(sy, "pe", pe_c3); p.wait(sy, "dve", dve_c3); p.wait(sy, "act", act_c3)
                    sy.wait_ge(p.sem(s_ow), n_ow0)
                    dma(sy, p, s_hw, nw_sb[:], w["norm_w"])
                n_nw = p.cnt(s_hw)
                @block.gpsimd
                def _(g):
                    p.wait(g, "pe", pe_c3); p.wait(g, "dve", dve_c3); p.wait(g, "act", act_c3)
                    g.wait_ge(p.sem(s_ow), n_ow0)
                    for ih in range(4):
                        dma(g, p, s_gw + "0", wo_sb[:, ih * 4:(ih + 1) * 4],
                            w["wo_blk"][ih * 4:(ih + 1) * 4].rearrange("i o p m -> p i o m"))
                n_wo = p.cnt(s_gw + "0")
                for ts in range(NTT):
                    cbuf = ts % 2
                    s_lc = s_ldc + str(cbuf)
                    pe_b = p.cnt("prog_pe")
                    act_b = p.cnt("prog_act")
                    @block.sync
                    def _(sy):
                        p.wait(sy, "pe", pe_b)
                        p.wait(sy, "act", act_b)
                        dma(sy, p, s_lc, ot_sb[:, cbuf],
                            ctx_hbm.rearrange("(c p) t -> p c t", p=128)[:, :, ts * 512:(ts + 1) * 512])
                    n_ldc = p.cnt(s_lc)
                    peb = p.cnt("prog_pe")
                    actb = p.cnt("prog_act")
                    dveb = p.cnt("prog_dve")
                    def pe_row(hh): return peb + 2 * hh + 1
                    def pe_bcm(hh): return peb + 2 * hh + 2
                    def act_sq(hh): return actb + 3 * hh + 1
                    def act_ex(hh): return actb + 3 * hh + 3
                    def dve_cp(hh): return dveb + 2 * hh + 1
                    @block.scalar
                    def _(s):
                        s.wait_ge(p.sem(s_lc), n_ldc)
                        if ts == 0:
                            s.wait_ge(p.sem(s_hw), n_nw)
                        for hh in range(ND):
                            if hh >= 1:
                                s.wait_ge(p.sem("prog_pe"), pe_row(hh - 1))
                            ins = s.activation(out=sq_sb[:], in_=ot_sb[:, cbuf, hh, :],
                                               func=AF.Square)
                            nn = p.mark(ins, "act")
                            s.wait_ge(p.sem("prog_pe"), pe_row(hh) - 1 + 1)
                            ins = s.activation(out=row_sb[:], in_=pb_row[:], func=AF.Ln,
                                               scale=1.0 / 128.0, bias=consts["eps_row"][:])
                            nn = p.mark(ins, "act")
                            s.wait_ge(p.sem("prog_act"), nn)
                            ins = s.activation(out=row_sb[:], in_=row_sb[:], func=AF.Exp,
                                               scale=-0.5)
                            p.mark(ins, "act")
                    @block.tensor
                    def _(t):
                        if ts == 0:
                            p.wait(t, "act", actb)
                            p.wait(t, "dve", dveb)
                        for hh in range(ND):
                            t.wait_ge(p.sem("prog_act"), act_sq(hh))
                            ins = t.matmul(pb_row[:], consts["ones_col_b"][:], sq_sb[:],
                                           start=True, stop=True)
                            p.mark(ins, "pe")
                            t.wait_ge(p.sem("prog_act"), act_ex(hh))
                            if hh >= 1:
                                t.wait_ge(p.sem("prog_dve"), dve_cp(hh - 1))
                            ins = t.matmul(pb_bc2[:], consts["ones_row_f"][:], row_sb[:],
                                           start=True, stop=True)
                            p.mark(ins, "pe")
                    @block.vector
                    def _(v):
                        for hh in range(ND):
                            v.wait_ge(p.sem("prog_pe"), pe_bcm(hh))
                            ins = v.tensor_copy(bc_sb[:], pb_bc2[:])
                            nn = p.mark(ins, "dve")
                            v.wait_ge(p.sem("prog_dve"), nn)
                            ins = v.scalar_tensor_tensor(ot_sb[:, cbuf, hh, :],
                                                         ot_sb[:, cbuf, hh, :],
                                                         nw_sb[:], bc_sb[:],
                                                         OP.mult, OP.mult)
                            nn = p.mark(ins, "dve")
                            v.wait_ge(p.sem("prog_dve"), nn)
                    dve_rms = p.cnt("prog_dve")
                    for o in range(ND):
                        dve_res = p.cnt("prog_dve")
                        s_lr = s_ldr + str(o % 4)
                        @block.scalar
                        def _(s):
                            p.wait(s, "dve", dve_res)
                            dma(s, p, s_lr, xres_sb[:, o % 4, :],
                                src_x[o * 128:(o + 1) * 128, ts * 512:(ts + 1) * 512])
                        n_r = p.cnt(s_lr)
                        bank = pb_oo[o % 2]
                        dve_free = p.cnt("prog_dve")
                        @block.tensor
                        def _(t):
                            t.wait_ge(p.sem(s_gw + "0"), n_wo)
                            p.wait(t, "dve", dve_free)
                            for c in range(ND):
                                ins = t.matmul(bank[:], wo_sb[:, c, o, :],
                                               ot_sb[:, cbuf, c, :],
                                               start=(c == 0), stop=(c == ND - 1))
                            p.mark(ins, "pe")
                        pe_g = p.cnt("prog_pe")
                        s_ou = s_out + str(o % 2)
                        n_out_prev = p.cnt(s_ou)
                        @block.vector
                        def _(v):
                            p.wait(v, "pe", pe_g)
                            v.wait_ge(p.sem(s_lr), n_r)
                            if n_out_prev > 0:
                                v.wait_ge(p.sem(s_ou), n_out_prev)
                            ins = v.tensor_add(t2_sb[:, o % 2, :], bank[:],
                                               xres_sb[:, o % 4, :])
                            p.mark(ins, "dve")
                        dve_g = p.cnt("prog_dve")
                        @block.sync
                        def _(sy):
                            p.wait(sy, "dve", dve_g)
                            dma(sy, p, s_ou,
                                dst_x[o * 128:(o + 1) * 128, ts * 512:(ts + 1) * 512],
                                t2_sb[:, o % 2, :])
    n_o0 = p.cnt(s_out + "0"); n_o1 = p.cnt(s_out + "1")
    @block.sync
    def _(sy):
        sy.wait_ge(p.sem(s_out + "0"), n_o0)
        sy.wait_ge(p.sem(s_out + "1"), n_o1)
        ins = sy.engine_nop() if hasattr(sy, "engine_nop") else sy.nop()
        h, n = p.inc("c_done", 1)
        ins.then_inc(h, 1)
    return "c_done", p.cnt("c_done")


def host_prep_c(inp):
    return {
        "wq_blk": blk(np.asarray(inp["L1_attn_wq"])),
        "wk_blk": blk(np.asarray(inp["L1_attn_wk"])),
        "wv_blk": blk(np.asarray(inp["L1_attn_wv"])),
        "wo_blk": blk(np.asarray(inp["L1_attn_wo"])),
        "fq1_wT": np.ascontiguousarray(np.asarray(inp["L1_attn_fq1_w"]).T),
        "fq2_wT": np.ascontiguousarray(np.asarray(inp["L1_attn_fq2_w"]).T),
        "fk1_wT": np.ascontiguousarray(np.asarray(inp["L1_attn_fk1_w"]).T),
        "fk2_wT": np.ascontiguousarray(np.asarray(inp["L1_attn_fk2_w"]).T),
        "fq1_b": np.asarray(inp["L1_attn_fq1_b"]).reshape(128, 1).copy(),
        "fq2_b": np.asarray(inp["L1_attn_fq2_b"]).reshape(128, 1).copy(),
        "fk1_b": np.asarray(inp["L1_attn_fk1_b"]).reshape(128, 1).copy(),
        "fk2_b": np.asarray(inp["L1_attn_fk2_b"]).reshape(128, 1).copy(),
        "norm_w": np.asarray(inp["L1_attn_norm_w"]).reshape(128, 1).copy(),
    }

C_W_SHAPES = {
    "wq_blk": ([ND, ND, 128, 128], F32),
    "wk_blk": ([ND, ND, 128, 128], F32),
    "wv_blk": ([ND, ND, 128, 128], F32),
    "wo_blk": ([ND, ND, 128, 128], F32),
    "fq1_wT": ([128, 128], F32),
    "fq2_wT": ([128, 128], F32),
    "fk1_wT": ([128, 128], F32),
    "fk2_wT": ([128, 128], F32),
    "fq1_b": ([128, 1], F32),
    "fq2_b": ([128, 1], F32),
    "fk1_b": ([128, 1], F32),
    "fk2_b": ([128, 1], F32),
    "norm_w": ([128, 1], F32),
}


def build_c_test():
    nc = bass.Bass("TRN2", target_bir_lowering=False)
    stack = contextlib.ExitStack()
    xin = nc.dram_tensor("xin", [D, T], F32, kind="ExternalInput").ap()
    xout = nc.dram_tensor("xout", [D, T], F32, kind="ExternalOutput").ap()
    ctx_hbm = nc.dram_tensor("ctx_hbm", [D, T], BF16, kind="Internal").ap()
    cin = {k: nc.dram_tensor(k, sh, dt, kind="ExternalInput").ap()
           for k, (sh, dt) in CONST_SHAPES.items()}
    wc = {k: nc.dram_tensor(k, sh, dt, kind="ExternalInput").ap()
          for k, (sh, dt) in C_W_SHAPES.items()}
    wc["c_mask_c"] = cin["c_mask_c"]
    with stack:
        with nc.Block() as block:
            p = Prog(nc, block, stack)
            consts = emit_consts(p, cin)
            s_done, n_done = emit_phase_c(p, consts, xin, xout, ctx_hbm, wc)
            @block.sync
            def _(sy):
                sy.wait_ge(p.sem(s_done), n_done)
    return nc



def emit_phase_d(p, consts, src_x, dst_x, moe_acc, w, s_prev=None, n_prev=0):
    nc, block = p.nc, p.block
    s_ldx, s_hw, s_out, s_ldr, s_lda = "d_ldx", "d_hw", "d_out", "d_ldr", "d_lda"

    def vsync(v, ins):
        nn = p.mark(ins, "dve")
        v.wait_ge(p.sem("prog_dve"), nn)
        return nn

    with contextlib.ExitStack() as stD:
        sbD = lambda n, sh, dt: stD.enter_context(nc.sbuf_tensor(n, sh, dt))
        h_sb = sbD("d_h", [128, ND, T], BF16)
        w_sb = sbD("d_w", [128, 32, 12], F32)
        s1_sb = sbD("d_s1", [128, 32], F32)

        with contextlib.ExitStack() as stA:
            sbA = lambda n, sh, dt: stA.enter_context(nc.sbuf_tensor(n, sh, dt))
            x_sb = sbA("d_x", [128, ND, 512], F32)
            clf_sb = sbA("d_clf", [128, ND, 12], F32)
            se_sb = sbA("d_se", [12, 1], F32)
            eb_sb = sbA("d_eb", [128, 12], F32)
            row12 = sbA("d_r12", [1, 16], F32)
            mb12 = sbA("d_mb12", [12, 512], F32)
            lg_sb = sbA("d_lg", [12, 512], F32)
            rt_sb = sbA("d_rt", [128, 12], F32)
            pr_sb = sbA("d_pr", [128, 12], F32)
            sc1 = sbA("d_sc1", [128, 12], F32)
            sc2 = sbA("d_sc2", [128, 12], F32)
            rr_sb = sbA("d_rr2", [128, 4], F32)
            scr = {
                "tmp": sbA("d_tmp", [128, 512], F32),
            "tmpb": sbA("d_tmpb", [128, 2, 512], BF16),
                "row_m": sbA("d_rm", [1, 512], F32),
                "row_v": sbA("d_rv", [1, 512], F32),
                "mb": sbA("d_mb", [128, 512], F32),
            }
            with contextlib.ExitStack() as stP:
                psA = lambda n, sh: stP.enter_context(nc.psum_tensor(n, sh, F32))
                pb_row0 = psA("d_pr0", [1, 512])
                pb_row1 = psA("d_pr1", [1, 512])
                pb_bc = psA("d_pbc", [128, 512])
                pb_lg = psA("d_plg", [12, 512])
                pb_t = psA("d_pt", [128, 12])

                @block.sync
                def _(sy):
                    if s_prev is not None:
                        sy.wait_ge(p.sem(s_prev), n_prev)
                    dma(sy, p, s_hw, clf_sb[:], w["clfT"].rearrange("(c p) e -> p c e", p=128))
                    dma(sy, p, s_hw, eb_sb[0:1, :], w["ebias_row"])
                n_hw0 = p.cnt(s_hw)
                @block.tensor
                def _(t):
                    t.wait_ge(p.sem(s_hw), n_hw0)
                    p.wait(t, "act")
                    p.wait(t, "dve")
                    for c in range(ND):
                        t.matmul(pb_t[0:12, 0:1], clf_sb[:, c, :],
                                 consts["ones_col_f"][:], start=(c == 0),
                                 stop=(c == ND - 1), skip_group_check=True)
                    ins = t.matmul(pb_bc[:, 0:12], consts["ones_row_f"][:], eb_sb[0:1, :],
                                   start=True, stop=True, skip_group_check=True)
                    p.mark(ins, "pe")
                pe_eb = p.cnt("prog_pe")
                @block.vector
                def _(v):
                    p.wait(v, "pe", pe_eb)
                    v.tensor_copy(se_sb[:], pb_t[0:12, 0:1])
                    ins = v.tensor_copy(eb_sb[:], pb_bc[:, 0:12])
                    vsync(v, ins)
                dve_pre = p.cnt("prog_dve")

                for ts in range(NTT):
                    c0 = ts * 512
                    dve_b = p.cnt("prog_dve")
                    pe_b = p.cnt("prog_pe")
                    @block.sync
                    def _(sy):
                        p.wait(sy, "dve", dve_b)
                        p.wait(sy, "pe", pe_b)
                        dma(sy, p, s_ldx, x_sb[:],
                            src_x.rearrange("(c p) t -> p c t", p=128)[:, :, c0:c0 + 512])
                    n_ldx = p.cnt(s_ldx)
                    @block.tensor
                    def _(t):
                        t.wait_ge(p.sem(s_ldx), n_ldx)
                        p.wait(t, "act")
                        p.wait(t, "dve")
                    @block.scalar
                    def _(s):
                        s.wait_ge(p.sem(s_ldx), n_ldx)
                    @block.vector
                    def _(v):
                        v.wait_ge(p.sem(s_ldx), n_ldx)
                    hv = lambda c: h_sb[:, c, c0:c0 + 512]
                    dve_h = emit_ln_tile(p, consts, x_sb, hv, pb_row0, pb_row1, pb_bc, scr)

                    @block.tensor
                    def _(t):
                        p.wait(t, "dve", dve_h)
                        for c in range(ND):
                            ins = t.matmul(pb_lg[:], clf_sb[:, c, :], x_sb[:, c, :],
                                           start=(c == 0), stop=(c == ND - 1))
                        ins = t.matmul(pb_t[:, 0:12] if False else pb_bc[:, 0:512],
                                       consts["ones_row_f"][:], scr["row_m"][:],
                                       start=True, stop=True)
                        p.mark(ins, "pe")
                    pe_lg = p.cnt("prog_pe")
                    @block.vector
                    def _(v):
                        p.wait(v, "pe", pe_lg)
                        ins = v.tensor_copy(mb12[:], pb_bc[0:12, 0:512])
                        vsync(v, ins)
                        ins = v.tensor_scalar(mb12[:], mb12[:], se_sb[0:12, 0:1], None,
                                              OP.mult)
                        vsync(v, ins)
                        ins = v.tensor_sub(lg_sb[:], pb_lg[:], mb12[:])
                        vsync(v, ins)
                        p.mark(v.engine_nop(), "dve")
                    dve_lgc = p.cnt("prog_dve")
                    @block.tensor
                    def _(t):
                        p.wait(t, "dve", dve_lgc)
                        ins = t.matmul(pb_bc[:, 0:512], consts["ones_row_f"][:],
                                       scr["row_v"][:], start=True, stop=True)
                        p.mark(ins, "pe")
                    pe_rb = p.cnt("prog_pe")
                    @block.vector
                    def _(v):
                        p.wait(v, "pe", pe_rb)
                        ins = v.tensor_tensor(lg_sb[:], lg_sb[:], pb_bc[0:12, 0:512],
                                              OP.mult)
                        vsync(v, ins)
                    dve_lg = p.cnt("prog_dve")

                    for sub in range(4):
                        slot = ts * 4 + sub
                        dve_now = p.cnt("prog_dve")
                        @block.tensor
                        def _(t):
                            p.wait(t, "dve", max(dve_lg, dve_now))
                            ins = t.transpose(pb_t[:], lg_sb[:, sub * 128:(sub + 1) * 128],
                                              consts["ident_f"][0:12, 0:12])
                            p.mark(ins, "pe")
                        pe_tr = p.cnt("prog_pe")
                        @block.vector
                        def _(v):
                            p.wait(v, "pe", pe_tr)
                            ins = v.tensor_copy(rt_sb[:], pb_t[:])
                            vsync(v, ins)
                            ins = v.reduce_max(rr_sb[:, 0:1], rt_sb[:], axis=AX.X)
                            vsync(v, ins)
                            ins = v.tensor_scalar(rr_sb[:, 1:2], rr_sb[:, 0:1], -1.0,
                                                  None, OP.mult)
                            vsync(v, ins)
                        dve_mx = p.cnt("prog_dve")
                        @block.scalar
                        def _(s):
                            p.wait(s, "dve", dve_mx)
                            ins = s.activation(out=pr_sb[:], in_=rt_sb[:], func=AF.Exp,
                                               bias=rr_sb[:, 1:2])
                            p.mark(ins, "act")
                        act_ex = p.cnt("prog_act")
                        @block.vector
                        def _(v):
                            p.wait(v, "act", act_ex)
                            ins = v.reduce_sum(rr_sb[:, 2:3], pr_sb[:], axis=AX.X)
                            vsync(v, ins)
                            ins = v.reciprocal(rr_sb[:, 3:4], rr_sb[:, 2:3])
                            vsync(v, ins)
                            ins = v.tensor_scalar(pr_sb[:], pr_sb[:], rr_sb[:, 3:4],
                                                  None, OP.mult)
                            vsync(v, ins)
                            ins = v.tensor_add(sc1[:], pr_sb[:], eb_sb[:])
                            vsync(v, ins)
                            ins = v.reduce_max(rr_sb[:, 0:1], sc1[:], axis=AX.X)
                            vsync(v, ins)
                            ins = v.tensor_scalar(sc2[:], sc1[:], rr_sb[:, 0:1], None,
                                                  OP.is_ge)
                            vsync(v, ins)
                            ins = v.scalar_tensor_tensor(sc1[:], sc2[:], -1e30, sc1[:],
                                                         OP.mult, OP.add)
                            vsync(v, ins)
                            ins = v.reduce_max(rr_sb[:, 1:2], sc1[:], axis=AX.X)
                            vsync(v, ins)
                            ins = v.tensor_scalar(sc1[:], sc1[:], rr_sb[:, 1:2], None,
                                                  OP.is_ge)
                            vsync(v, ins)
                            ins = v.tensor_add(sc2[:], sc2[:], sc1[:])
                            vsync(v, ins)
                            ins = v.scalar_tensor_tensor(w_sb[:, slot, :], sc2[:], 1.5,
                                                         pr_sb[:], OP.mult, OP.mult)
                            vsync(v, ins)
                            ins = v.reduce_sum(s1_sb[:, slot:slot + 1],
                                               w_sb[:, slot, 8:12], axis=AX.X)
                            vsync(v, ins)
                            ins = v.tensor_scalar(s1_sb[:, slot:slot + 1],
                                                  s1_sb[:, slot:slot + 1], 1.0, None,
                                                  OP.add)
                            vsync(v, ins)

        pe_d0 = p.cnt("prog_pe"); dve_d0 = p.cnt("prog_dve"); act_d0 = p.cnt("prog_act")
        with contextlib.ExitStack() as stE:
            sbE = lambda n, sh, dt: stE.enter_context(nc.sbuf_tensor(n, sh, dt))
            wg_sb = sbE("d_wg", [128, 2, ND, 128], BF16)
            wu_sb = sbE("d_wu", [128, 2, ND, 128], BF16)
            wd_sb = sbE("d_wd", [128, 2, 8, 128], BF16)
            act_sb = sbE("d_act", [128, 8, T // 2], BF16)
            wbc_sb = sbE("d_wbc", [128, NTT, 512], BF16)
            wrow_sb = sbE("d_wrow", [1, 512], F32)
            silu_sb = sbE("d_silu", [128, 512], F32)
            t3_sb = silu_sb
            dout_sb = sbE("d_dout", [128, 6, 512], F32)
            with contextlib.ExitStack() as stP:
                psE = lambda n, sh: stP.enter_context(nc.psum_tensor(n, sh, F32))
                pb_g = [psE("d_pg0", [128, 512]), psE("d_pg1", [128, 512])]
                pb_u = [psE("d_pu0", [128, 512]), psE("d_pu1", [128, 512])]
                pb_d = [psE("d_pd0", [128, 512]), psE("d_pd1", [128, 512])]
                pb_row = psE("d_prow", [1, 512])
                pb_bc2 = psE("d_pbc2", [128, 512])

                n_acc_prev = {i: 0 for i in range(8)}
                acc_done = {}
                pref_n_wg = {}; pref_n_wu = {}
                acc_cnt = {i: 0 for i in range(8)}
                gul = {0: 0, 1: 0}
                dl = {0: 0, 1: 0}
                for e in range(8):
                    for ts in range(NTT):
                        dve_f = p.cnt("prog_dve")
                        @block.tensor
                        def _(t):
                            if e == 0 and ts == 0:
                                p.wait(t, "pe", pe_d0)
                                p.wait(t, "act", act_d0)
                            p.wait(t, "dve", dve_f)
                            for sub in range(4):
                                ins = t.matmul(pb_row[0:1, sub * 128:(sub + 1) * 128],
                                               w_sb[:, ts * 4 + sub, e:e + 1],
                                               consts["ident_f"][:],
                                               start=True, stop=True,
                                               skip_group_check=True)
                            p.mark(ins, "pe")
                        pe_r = p.cnt("prog_pe")
                        @block.vector
                        def _(v):
                            p.wait(v, "pe", pe_r)
                            ins = v.tensor_copy(wrow_sb[:], pb_row[:])
                            nn = p.mark(ins, "dve")
                            v.wait_ge(p.sem("prog_dve"), nn)
                        dve_r = p.cnt("prog_dve")
                        @block.tensor
                        def _(t):
                            p.wait(t, "dve", dve_r)
                            ins = t.matmul(pb_bc2[:], consts["ones_row_f"][:], wrow_sb[:],
                                           start=True, stop=True)
                            p.mark(ins, "pe")
                        pe_b2 = p.cnt("prog_pe")
                        @block.vector
                        def _(v):
                            p.wait(v, "pe", pe_b2)
                            ins = v.tensor_copy(wbc_sb[:, ts, :], pb_bc2[:])
                            p.mark(ins, "dve")
                    for th in range(2):
                        peb = p.cnt("prog_pe")
                        actb = p.cnt("prog_act")
                        dveb = p.cnt("prog_dve")
                        n_wg = dict(pref_n_wg); n_wu = dict(pref_n_wu)
                        pref_n_wg.clear(); pref_n_wu.clear()
                        for f in range(8):
                            if f in n_wg:
                                continue
                            buf = f % 2
                            s_gg = "d_gg" + str(buf); s_gu = "d_gu" + str(buf)
                            @block.gpsimd
                            def _(g):
                                if e == 0 and th == 0 and f < 2:
                                    p.wait(g, "pe", pe_d0); p.wait(g, "dve", dve_d0)
                                    p.wait(g, "act", act_d0)
                                if f >= 2:
                                    g.wait_ge(p.sem("prog_pe"), peb + 4 * (f - 1))
                                elif gul[buf] > 0:
                                    p.wait(g, "pe", gul[buf])
                                dma(g, p, s_gg, wg_sb[:, buf],
                                    w["gate_blk"][e][:, f].rearrange("i p m -> p i m"))
                                dma(g, p, s_gu, wu_sb[:, buf],
                                    w["up_blk"][e][:, f].rearrange("i p m -> p i m"))
                            n_wg[f] = p.cnt(s_gg); n_wu[f] = p.cnt(s_gu)
                        def pe_gu(f, tsr):
                            return peb + 4 * f + tsr + 1
                        def act_si(f, tsr):
                            return actb + 4 * f + tsr + 1
                        def dve_aw(f, tsr):
                            return dveb + 2 * (4 * f + tsr) + 2
                        @block.tensor
                        def _(t):
                            for f in range(8):
                                buf = f % 2
                                t.wait_ge(p.sem("d_gg" + str(buf)), n_wg[f])
                                t.wait_ge(p.sem("d_gu" + str(buf)), n_wu[f])
                                if f == 0 and th == 0:
                                    p.wait(t, "act")
                                    p.wait(t, "dve")
                                for tsr in range(4):
                                    ts = th * 4 + tsr
                                    pf, pt = (f, tsr - 2) if tsr >= 2 else (f - 1, tsr + 2)
                                    if pf >= 0:
                                        t.wait_ge(p.sem("prog_act"), act_si(pf, pt))
                                        t.wait_ge(p.sem("prog_dve"), dve_aw(pf, pt))
                                    elif th > 0 or e > 0:
                                        p.wait(t, "act")
                                        p.wait(t, "dve")
                                    pg = pb_g[tsr % 2]; pu = pb_u[tsr % 2]
                                    for c in range(ND):
                                        t.matmul(pg[:], wg_sb[:, buf, c, :],
                                                 h_sb[:, c, ts * 512:(ts + 1) * 512],
                                                 start=(c == 0), stop=(c == ND - 1))
                                    for c in range(ND):
                                        ins = t.matmul(pu[:], wu_sb[:, buf, c, :],
                                                       h_sb[:, c, ts * 512:(ts + 1) * 512],
                                                       start=(c == 0), stop=(c == ND - 1))
                                    p.mark(ins, "pe")
                        gul[0] = peb + 4 * 6 + 4
                        gul[1] = peb + 4 * 7 + 4
                        @block.scalar
                        def _(s):
                            for f in range(8):
                                for tsr in range(4):
                                    k = 4 * f + tsr
                                    s.wait_ge(p.sem("prog_pe"), pe_gu(f, tsr))
                                    if k >= 1:
                                        s.wait_ge(p.sem("prog_dve"), dveb + 2 * (k - 1) + 2)
                                    ins = s.activation(out=silu_sb[:], in_=pb_g[tsr % 2][:],
                                                       func=AF.Silu)
                                    p.mark(ins, "act")
                        @block.vector
                        def _(v):
                            for f in range(8):
                                for tsr in range(4):
                                    ts = th * 4 + tsr
                                    v.wait_ge(p.sem("prog_act"), act_si(f, tsr))
                                    ins = v.tensor_tensor(t3_sb[:], silu_sb[:],
                                                          pb_u[tsr % 2][:], OP.mult)
                                    nn = p.mark(ins, "dve")
                                    v.wait_ge(p.sem("prog_dve"), nn)
                                    ins = v.tensor_tensor(
                                        act_sb[:, f, tsr * 512:(tsr + 1) * 512],
                                        t3_sb[:], wbc_sb[:, ts, :], OP.mult)
                                    nn = p.mark(ins, "dve")
                                    v.wait_ge(p.sem("prog_dve"), nn)
                        peb2 = p.cnt("prog_pe")
                        dveb2 = p.cnt("prog_dve")
                        n_wd = {}
                        def emit_accums(g, o):
                            for tsr in range(4):
                                ts = th * 4 + tsr
                                ring = (4 * o + tsr) % 6
                                g.wait_ge(p.sem("prog_dve"), dveb2 + 4 * o + tsr + 1)
                                dma(g, p, s_lda + str(ring),
                                    moe_acc[o * 128:(o + 1) * 128,
                                            ts * 512:(ts + 1) * 512],
                                    dout_sb[:, ring, :],
                                    **({} if e == 0 else {"accum_op": OP.add}))
                                if e == 7:
                                    acc_done[(o, ts)] = (s_lda + str(ring),
                                                        p.cnt(s_lda + str(ring)))
                        ne, nth = (e, 1) if th == 0 else (e + 1, 0)
                        for o in range(ND):
                            buf = o % 2
                            s_gd = "d_gd" + str(buf)
                            @block.gpsimd
                            def _(g):
                                if e > 0 and th == 0 and o == 1:
                                    for i in range(8):
                                        g.wait_ge(p.sem(s_lda + str(i)), n_acc_prev[i])
                                if o >= 1:
                                    emit_accums(g, o - 1)
                                if o >= 2:
                                    g.wait_ge(p.sem("prog_pe"), peb2 + 4 * (o - 1))
                                elif dl[buf] > 0:
                                    p.wait(g, "pe", dl[buf])
                                dma(g, p, s_gd, wd_sb[:, buf],
                                    w["down_blk"][e][:, o].rearrange("i p m -> p i m"))
                                if o in (2, 3) and ne < 8:
                                    fb = o - 2
                                    g.wait_ge(p.sem("prog_pe"), gul[fb])
                                    dma(g, p, "d_gg" + str(fb), wg_sb[:, fb],
                                        w["gate_blk"][ne][:, fb].rearrange("i p m -> p i m"))
                                    dma(g, p, "d_gu" + str(fb), wu_sb[:, fb],
                                        w["up_blk"][ne][:, fb].rearrange("i p m -> p i m"))
                                    pref_n_wg[fb] = p.cnt("d_gg" + str(fb))
                                    pref_n_wu[fb] = p.cnt("d_gu" + str(fb))
                            n_wd[o] = p.cnt(s_gd)
                        @block.gpsimd
                        def _(g):
                            emit_accums(g, ND - 1)
                        def pe_dn(o, tsr):
                            return peb2 + 4 * o + tsr + 1
                        @block.tensor
                        def _(t):
                            for o in range(ND):
                                buf = o % 2
                                t.wait_ge(p.sem("d_gd" + str(buf)), n_wd[o])
                                if o == 0:
                                    p.wait(t, "dve")
                                for tsr in range(4):
                                    pf, pt = (o, tsr - 2) if tsr >= 2 else (o - 1, tsr + 2)
                                    if pf >= 0:
                                        t.wait_ge(p.sem("prog_dve"),
                                                  dveb2 + 4 * pf + pt + 1)
                                    pd = pb_d[tsr % 2]
                                    for f in range(8):
                                        ins = t.matmul(pd[:], wd_sb[:, buf, f, :],
                                                       act_sb[:, f, tsr * 512:(tsr + 1) * 512],
                                                       start=(f == 0), stop=(f == 7))
                                    p.mark(ins, "pe")
                        dl[0] = peb2 + 4 * 14 + 4
                        dl[1] = peb2 + 4 * 15 + 4
                        @block.vector
                        def _(v):
                            for o in range(ND):
                                for tsr in range(4):
                                    ring = (4 * o + tsr) % 6
                                    s_ac = s_lda + str(ring)
                                    v.wait_ge(p.sem("prog_pe"), pe_dn(o, tsr))
                                    na = acc_cnt[ring]
                                    if na > 0:
                                        v.wait_ge(p.sem(s_ac), na)
                                    ins = v.tensor_copy(dout_sb[:, ring, :], pb_d[tsr % 2][:])
                                    p.mark(ins, "dve")
                                    acc_cnt[ring] += 16
                    for i in range(8):
                        n_acc_prev[i] = p.cnt(s_lda + str(i))
                    for i in range(8):
                        n_acc_prev[i] = p.cnt(s_lda + str(i))

        pe_d1 = p.cnt("prog_pe"); dve_d1 = p.cnt("prog_dve"); act_d1 = p.cnt("prog_act")
        n_acc_fin = {i: p.cnt(s_lda + str(i)) for i in range(6)}
        with contextlib.ExitStack() as stF:
            sbF = lambda n, sh, dt: stF.enter_context(nc.sbuf_tensor(n, sh, dt))
            ma_sb = sbF("d_ma", [128, 6, 512], F32)
            xres_sb = sbF("d_xres", [128, 6, 512], F32)
            s1b_sb = sbF("d_s1b", [128, 8, 512], F32)
            srow_sb = sbF("d_srow", [1, 512], F32)
            t2_sb = sbF("d_t2", [128, 4, 512], F32)
            with contextlib.ExitStack() as stP:
                psF = lambda n, sh: stP.enter_context(nc.psum_tensor(n, sh, F32))
                pb_row2 = psF("d_prow2", [1, 512])
                pb_bc3 = psF("d_pbc3", [128, 512])
                ma_guard = {i: 0 for i in range(6)}
                peb5 = p.cnt("prog_pe")
                dveb5 = p.cnt("prog_dve")
                @block.tensor
                def _(t):
                    p.wait(t, "pe", pe_d1)
                    p.wait(t, "act", act_d1)
                    p.wait(t, "dve", dve_d1)
                    for ts in range(NTT):
                        if ts >= 1:
                            t.wait_ge(p.sem("prog_dve"), dveb5 + 2 * (ts - 1) + 1)
                        for sub in range(4):
                            ins = t.matmul(pb_row2[0:1, sub * 128:(sub + 1) * 128],
                                           s1_sb[:, ts * 4 + sub:ts * 4 + sub + 1],
                                           consts["ident_f"][:], start=True, stop=True,
                                           skip_group_check=True)
                        p.mark(ins, "pe")
                        t.wait_ge(p.sem("prog_dve"), dveb5 + 2 * ts + 1)
                        ins = t.matmul(pb_bc3[:], consts["ones_row_f"][:], srow_sb[:],
                                       start=True, stop=True)
                        p.mark(ins, "pe")
                @block.vector
                def _(v):
                    for ts in range(NTT):
                        v.wait_ge(p.sem("prog_pe"), peb5 + 2 * ts + 1)
                        ins = v.tensor_copy(srow_sb[:], pb_row2[:])
                        nn = p.mark(ins, "dve")
                        v.wait_ge(p.sem("prog_dve"), nn)
                        v.wait_ge(p.sem("prog_pe"), peb5 + 2 * ts + 2)
                        ins = v.tensor_copy(s1b_sb[:, ts, :], pb_bc3[:])
                        nn = p.mark(ins, "dve")
                        v.wait_ge(p.sem("prog_dve"), nn)
                dve_s1b = p.cnt("prog_dve")
                for ts in range(NTT):
                    for o in range(ND):
                        ring = (ts * ND + o) % 6
                        s_ma = "d_ma" + str(ring)
                        s_lr = s_ldr + str(ring)
                        g = ma_guard[ring]
                        @block.sync
                        def _(sy):
                            if ts == 0 and o == 0:
                                for i in range(6):
                                    sy.wait_ge(p.sem(s_lda + str(i)), n_acc_fin[i])
                                p.wait(sy, "pe", pe_d1)
                                p.wait(sy, "dve", dve_d1)
                                p.wait(sy, "act", act_d1)
                            if g > 0:
                                sy.wait_ge(p.sem("prog_dve"), g)
                            dma(sy, p, s_ma, ma_sb[:, ring, :],
                                moe_acc[o * 128:(o + 1) * 128, ts * 512:(ts + 1) * 512])
                        n_ma = p.cnt(s_ma)
                        @block.gpsimd
                        def _(gg):
                            if g > 0:
                                gg.wait_ge(p.sem("prog_dve"), g)
                            dma(gg, p, s_lr, xres_sb[:, ring, :],
                                src_x[o * 128:(o + 1) * 128, ts * 512:(ts + 1) * 512])
                        n_r = p.cnt(s_lr)
                        s_ou = s_out + str(o % 4)
                        n_out_prev = p.cnt(s_ou)
                        @block.vector
                        def _(v):
                            v.wait_ge(p.sem(s_ma), n_ma)
                            v.wait_ge(p.sem(s_lr), n_r)
                            p.wait(v, "dve", dve_s1b)
                            if n_out_prev > 0:
                                v.wait_ge(p.sem(s_ou), n_out_prev)
                            ins = v.tensor_tensor(t2_sb[:, o % 4, :], xres_sb[:, ring, :],
                                                  s1b_sb[:, ts, :], OP.mult)
                            nn = p.mark(ins, "dve")
                            v.wait_ge(p.sem("prog_dve"), nn)
                            ins = v.tensor_add(t2_sb[:, o % 4, :], t2_sb[:, o % 4, :],
                                               ma_sb[:, ring, :])
                            p.mark(ins, "dve")
                        dve_g = p.cnt("prog_dve")
                        ma_guard[ring] = dve_g
                        @block.scalar
                        def _(sc):
                            p.wait(sc, "dve", dve_g)
                            dma(sc, p, s_ou,
                                dst_x[o * 128:(o + 1) * 128, ts * 512:(ts + 1) * 512],
                                t2_sb[:, o % 4, :])
    n_os = {i: p.cnt(s_out + str(i)) for i in range(4)}
    @block.sync
    def _(sy):
        for i in range(4):
            sy.wait_ge(p.sem(s_out + str(i)), n_os[i])
        ins = sy.engine_nop() if hasattr(sy, "engine_nop") else sy.nop()
        h, n = p.inc("d_done", 1)
        ins.then_inc(h, 1)
    return "d_done", p.cnt("d_done")


def host_prep_d(inp):
    gate = np.asarray(inp["L1_ffn_gate_w"])
    up = np.asarray(inp["L1_ffn_up_w"])
    down = np.asarray(inp["L1_ffn_down_w"])
    return {
        "clfT": np.ascontiguousarray(np.asarray(inp["L1_ffn_classifier_w"]).T),
        "ebias_row": np.asarray(inp["L1_ffn_e_score_correction_bias"]).reshape(1, 12).copy(),
        "gate_blk": np.stack([blk(gate[e]) for e in range(8)]),
        "up_blk": np.stack([blk(up[e]) for e in range(8)]),
        "down_blk": np.stack([blk(down[e]) for e in range(8)]),
    }

D_W_SHAPES = {
    "clfT": ([D, 12], F32),
    "ebias_row": ([1, 12], F32),
    "gate_blk": ([8, ND, 8, 128, 128], F32),
    "up_blk": ([8, ND, 8, 128, 128], F32),
    "down_blk": ([8, 8, ND, 128, 128], F32),
}


def build_d_test():
    nc = bass.Bass("TRN2", target_bir_lowering=False)
    stack = contextlib.ExitStack()
    xin = nc.dram_tensor("xin", [D, T], F32, kind="ExternalInput").ap()
    xout = nc.dram_tensor("xout", [D, T], F32, kind="ExternalOutput").ap()
    moe_acc = nc.dram_tensor("moe_acc", [D, T], F32, kind="Internal").ap()
    cin = {k: nc.dram_tensor(k, sh, dt, kind="ExternalInput").ap()
           for k, (sh, dt) in CONST_SHAPES.items()}
    wd = {k: nc.dram_tensor(k, sh, dt, kind="ExternalInput").ap()
          for k, (sh, dt) in D_W_SHAPES.items()}
    with stack:
        with nc.Block() as block:
            p = Prog(nc, block, stack)
            consts = emit_consts(p, cin)
            s_done, n_done = emit_phase_d(p, consts, xin, xout, moe_acc, wd)
            @block.sync
            def _(sy):
                sy.wait_ge(p.sem(s_done), n_done)
    return nc



def emit_consts(p, cin):
    block = p.block
    consts = {
        "ones_col_f": p.sb("k_ones_cf", [128, 1], F32),
        "ones_col_b": p.sb("k_ones_cb", [128, 1], BF16),
        "ones_row_f": p.sb("k_ones_rf", [1, 128], F32),
        "ident_f": p.sb("k_id_f", [128, 128], F32),
        "ident_b": p.sb("k_id_b", [128, 128], BF16),
        "eps_row": p.sb("k_eps", [1, 1], F32),
    }
    @block.sync
    def _(sy):
        dma(sy, p, "k_const", consts["ident_f"][:], cin["c_ident_f"])
        dma(sy, p, "k_const", consts["ident_b"][:], cin["c_ident_b"])
    @block.gpsimd
    def _(g):
        g.memset(consts["eps_row"][:], 1e-5)
        g.memset(consts["ones_col_f"][:], 1.0)
        g.memset(consts["ones_col_b"][:], 1.0)
        ins = g.memset(consts["ones_row_f"][:], 1.0)
        p.mark(ins, "pool")
    n_c = p.cnt("k_const")
    n_p = p.cnt("prog_pool")
    for name in ("tensor", "scalar", "vector", "sync", "gpsimd"):
        sec = getattr(block, name)
        @sec
        def _(e):
            e.wait_ge(p.sem("k_const"), n_c)
            e.wait_ge(p.sem("prog_pool"), n_p)
    return consts



def host_prep_b(inp):
    import ml_dtypes
    return {
        "WgT": np.ascontiguousarray(np.asarray(inp["L0_ffn_Wg"]).T),
        "W0a_blk": blk(np.asarray(inp["L0_ffn_W0a"])),
        "W0b_blk": blk(np.asarray(inp["L0_ffn_W0b"])),
        "W1a_blk": blk(np.asarray(inp["L0_ffn_W1a"])),
        "W1b_blk": blk(np.asarray(inp["L0_ffn_W1b"])),
        "b0a_T": vecT(np.asarray(inp["L0_ffn_b0a"])),
        "b1a_T": vecT(np.asarray(inp["L0_ffn_b1a"])),
        "b0b_T": vecT(np.asarray(inp["L0_ffn_b0b"])),
        "b1b_T": vecT(np.asarray(inp["L0_ffn_b1b"])),
    }

B_W_SHAPES = {
    "WgT": ([D, 2], F32),
    "W0a_blk": ([ND, 32, 128, 128], F32),
    "W0b_blk": ([32, ND, 128, 128], F32),
    "W1a_blk": ([ND, 32, 128, 128], F32),
    "W1b_blk": ([32, ND, 128, 128], F32),
    "b0a_T": ([128, 32], F32),
    "b1a_T": ([128, 32], F32),
    "b0b_T": ([128, 16], F32),
    "b1b_T": ([128, 16], F32),
}

CONST_SHAPES = {
    "c_ident_f": ([128, 128], F32),
    "c_ident_b": ([128, 128], BF16),
    "c_mask_a": ([4, 128, 512], BF16),
    "c_mask_c": ([128, 128], BF16),
}


def build_b_test():
    nc = bass.Bass("TRN2", target_bir_lowering=False)
    stack = contextlib.ExitStack()
    xin = nc.dram_tensor("xin", [D, T], F32, kind="ExternalInput").ap()
    xout = nc.dram_tensor("xout", [D, T], F32, kind="ExternalOutput").ap()
    cin = {k: nc.dram_tensor(k, sh, dt, kind="ExternalInput").ap()
           for k, (sh, dt) in CONST_SHAPES.items()}
    wb = {k: nc.dram_tensor(k, sh, dt, kind="ExternalInput").ap()
          for k, (sh, dt) in B_W_SHAPES.items()}
    with stack:
        with nc.Block() as block:
            p = Prog(nc, block, stack)
            consts = emit_consts(p, cin)
            s_out, n_out = emit_phase_b(p, consts, xin, xout, wb)
            @block.sync
            def _(sy):
                sy.wait_ge(p.sem(s_out), n_out)
    return nc


def host_prep_a(inp):
    return {
        "Wqd_blk": blk(np.asarray(inp["L0_attn_Wq_down"])),
        "Wkvd_blk": blk(np.asarray(inp["L0_attn_Wkv_down"])),
        "Wqup_blk": blk(np.asarray(inp["L0_attn_Wq_up"])),
        "Wkup_blk": blk(np.asarray(inp["L0_attn_Wk_up"])),
        "Wvup_blk": blk(np.asarray(inp["L0_attn_Wv_up"])),
        "Wo_blk": blk(np.asarray(inp["L0_attn_Wo"])),
        "bo_T": vecT(np.asarray(inp["L0_attn_bo"])),
    }

A_W_SHAPES = {
    "Wqd_blk": ([16, 8, 128, 128], F32),
    "Wkvd_blk": ([16, 4, 128, 128], F32),
    "Wqup_blk": ([8, 16, 128, 128], F32),
    "Wkup_blk": ([4, 16, 128, 128], F32),
    "Wvup_blk": ([4, 16, 128, 128], F32),
    "Wo_blk": ([16, 16, 128, 128], F32),
    "bo_T": ([128, 16], F32),
}


def build_a_test():
    nc = bass.Bass("TRN2", target_bir_lowering=False)
    stack = contextlib.ExitStack()
    xin = nc.dram_tensor("xin", [D, T], F32, kind="ExternalInput").ap()
    xout = nc.dram_tensor("xout", [D, T], F32, kind="ExternalOutput").ap()
    ctx_hbm = nc.dram_tensor("ctx_hbm", [D, T], BF16, kind="Internal").ap()
    cin = {k: nc.dram_tensor(k, sh, dt, kind="ExternalInput").ap()
           for k, (sh, dt) in CONST_SHAPES.items()}
    wa = {k: nc.dram_tensor(k, sh, dt, kind="ExternalInput").ap()
          for k, (sh, dt) in A_W_SHAPES.items()}
    wa["c_mask_a"] = cin["c_mask_a"]
    with stack:
        with nc.Block() as block:
            p = Prog(nc, block, stack)
            consts = emit_consts(p, cin)
            s_out, n_out = emit_phase_a(p, consts, xin, xout, ctx_hbm, wa)
            @block.sync
            def _(sy):
                sy.wait_ge(p.sem(s_out), n_out)
    return nc



ALL_W_SHAPES = {}
ALL_W_SHAPES.update(A_W_SHAPES)
ALL_W_SHAPES.update(B_W_SHAPES)
ALL_W_SHAPES.update(C_W_SHAPES)
ALL_W_SHAPES.update(D_W_SHAPES)


def build_full():
    nc = bass.Bass("TRN2", target_bir_lowering=False)
    stack = contextlib.ExitStack()
    x0 = nc.dram_tensor("x0", [D, T], F32, kind="ExternalInput").ap()
    yT = nc.dram_tensor("yT", [D, T], F32, kind="ExternalOutput").ap()
    xc1 = nc.dram_tensor("xc1", [D, T], F32, kind="Internal").ap()
    xc2 = nc.dram_tensor("xc2", [D, T], F32, kind="Internal").ap()
    ctx_hbm = nc.dram_tensor("ctx_hbm", [D, T], BF16, kind="Internal").ap()
    moe_acc = nc.dram_tensor("moe_acc", [D, T], F32, kind="Internal").ap()
    cin = {k: nc.dram_tensor(k, sh, dt, kind="ExternalInput").ap()
           for k, (sh, dt) in CONST_SHAPES.items()}
    wall = {k: nc.dram_tensor(k, sh, dt, kind="ExternalInput").ap()
            for k, (sh, dt) in ALL_W_SHAPES.items()}
    wall["c_mask_a"] = cin["c_mask_a"]
    wall["c_mask_c"] = cin["c_mask_c"]
    with stack:
        with nc.Block() as block:
            p = Prog(nc, block, stack)
            consts = emit_consts(p, cin)
            sA, nA = emit_phase_a(p, consts, x0, xc1, ctx_hbm, wall)
            sB, nB = emit_phase_b(p, consts, xc1, xc2, wall, s_prev=sA, n_prev=nA)
            sC, nC = emit_phase_c(p, consts, xc2, xc1, ctx_hbm, wall, s_prev=sB, n_prev=nB)
            sD, nD = emit_phase_d(p, consts, xc1, yT, moe_acc, wall, s_prev=sC, n_prev=nC)
            @block.sync
            def _(sy):
                sy.wait_ge(p.sem(sD), nD)
    return nc


_NC_CACHE = [None]

def kernel(**inputs):
    if _NC_CACHE[0] is None:
        _NC_CACHE[0] = build_full()
    nc = _NC_CACHE[0]
    shared = {}
    shared.update(host_consts())
    shared.update(host_prep_a(inputs))
    shared.update(host_prep_b(inputs))
    shared.update(host_prep_c(inputs))
    shared.update(host_prep_d(inputs))
    x = np.asarray(inputs["L0_attn_x"])
    in_maps = []
    for b in range(8):
        m = {"x0": np.ascontiguousarray(x[b].T)}
        m.update(shared)
        in_maps.append(m)
    res = run_bass_kernel_spmd(nc, in_maps, core_ids=list(range(8)))
    out = np.stack([np.asarray(res.results[b]["yT"]).T for b in range(8)])
    return np.ascontiguousarray(out.astype(np.float32))


# version 09
# speedup vs baseline: 1.1579x; 1.1579x over previous
import contextlib
import math
import numpy as np

import concourse.bass as bass
import concourse.mybir as mybir
from concourse.bass_utils import run_bass_kernel_spmd

F32 = mybir.dt.float32
BF16 = mybir.dt.bfloat16
U8 = mybir.dt.uint8
AF = mybir.ActivationFunctionType
OP = mybir.AluOpType
AX = mybir.AxisListType

T = 4096
D = 2048
ND = D // 128
NTT = T // 512
INV_SQRT_DK = 1.0 / math.sqrt(128.0)


def blk(w):
    O, I = w.shape
    return np.ascontiguousarray(
        w.reshape(O // 128, 128, I // 128, 128).transpose(2, 0, 3, 1))

def vecT(b):
    return np.ascontiguousarray(b.reshape(-1, 128).T)

def host_consts():
    import ml_dtypes
    ident = np.eye(128, dtype=np.float32)
    ma = np.zeros((4, 128, 512), dtype=np.float32)
    pp = np.arange(128)[:, None]
    ff = np.arange(512)[None, :]
    for d in range(4):
        ma[d] = (pp <= ff - d * 128)
    mc = (np.arange(64)[:, None] <= np.arange(64)[None, :]).astype(np.float32)
    tk = np.arange(128)[:, None]; tq = np.arange(128)[None, :]
    mc2 = ((tk // 64 < tq // 64) | ((tk // 64 == tq // 64) & (tk <= tq))).astype(np.float32)
    return {
        "c_ident_f": ident,
        "c_ident_b": ident.astype(ml_dtypes.bfloat16),
        "c_mask_a": ma.astype(ml_dtypes.bfloat16),
        "c_mask_c": mc2.astype(ml_dtypes.bfloat16),
    }


class Prog:
    def __init__(self, nc, block, stack):
        self.nc = nc
        self.block = block
        self.stack = stack
        self._sems = {}

    def sem(self, name):
        if name not in self._sems:
            h = self.stack.enter_context(self.nc.semaphore(name))
            self._sems[name] = [h, 0]
        return self._sems[name][0]

    def inc(self, name, k=1):
        self.sem(name)
        s = self._sems[name]
        s[1] += k
        return s[0], s[1]

    def cnt(self, name):
        self.sem(name)
        return self._sems[name][1]

    def sb(self, name, shape, dtype):
        return self.stack.enter_context(self.nc.sbuf_tensor(name, shape, dtype))

    def ps(self, name, shape, dtype=F32):
        return self.stack.enter_context(self.nc.psum_tensor(name, shape, dtype))

    def mark(self, ins, eng):
        h, n = self.inc("prog_" + eng)
        ins.then_inc(h, 1)
        return n

    def wait(self, e, eng, n=None):
        if n is None:
            n = self.cnt("prog_" + eng)
        if n > 0:
            e.wait_ge(self.sem("prog_" + eng), n)


def dma(eng, p, sname, out, in_, **kw):
    ins = eng.dma_start(out=out, in_=in_, **kw)
    h, n = p.inc(sname, 16)
    ins.then_inc(h, 16)
    return n


def emit_ln_tile(p, consts, x_sb, h_view, pb_row0, pb_row1, pb_bc, scr):
    nc, block = p.nc, p.block
    tmp, row_m, row_v, mb = (scr["tmp"], scr["row_m"], scr["row_v"], scr["mb"])

    pe_before = p.cnt("prog_pe")
    dve_before = p.cnt("prog_dve")
    act0 = p.cnt("prog_act")
    pe0 = p.cnt("prog_pe")
    dve0 = p.cnt("prog_dve")
    tmpb = scr["tmpb"]
    @block.scalar
    def _(s):
        p.wait(s, "pe", pe_before)
        p.wait(s, "dve", dve_before)
        for c in range(ND):
            if c >= 2:
                s.wait_ge(p.sem("prog_pe"), pe0 + (c - 1))
            ins = s.activation(out=tmpb[:, c % 2, :], in_=x_sb[:, c, :], func=AF.Square)
            p.mark(ins, "act")
    @block.vector
    def _(v):
        for c in range(ND):
            ins = v.tensor_copy(h_view(c), x_sb[:, c, :])
            p.mark(ins, "dve")
    @block.tensor
    def _(t):
        for c in range(ND):
            t.wait_ge(p.sem("prog_dve"), dve0 + c + 1)
            t.matmul(pb_row0[:], consts["ones_col_b"][:], h_view(c),
                     start=(c == 0), stop=(c == ND - 1))
            t.wait_ge(p.sem("prog_act"), act0 + c + 1)
            ins = t.matmul(pb_row1[:], consts["ones_col_b"][:], tmpb[:, c % 2, :],
                           start=(c == 0), stop=(c == ND - 1))
            p.mark(ins, "pe")
    pe_stats = p.cnt("prog_pe")

    @block.vector
    def _(v):
        p.wait(v, "pe", pe_stats)
        ins = v.tensor_scalar(row_m[:], pb_row0[:], 1.0 / D, None, OP.mult)
        nn = p.mark(ins, "dve")
        v.wait_ge(p.sem("prog_dve"), nn)
        ins = v.tensor_scalar(row_v[:], pb_row1[:], 1.0 / D, None, OP.mult)
        nn = p.mark(ins, "dve")
        v.wait_ge(p.sem("prog_dve"), nn)
        ins = v.tensor_tensor(tmp[0:1, :], row_m[:], row_m[:], OP.mult)
        nn = p.mark(ins, "dve")
        v.wait_ge(p.sem("prog_dve"), nn)
        ins = v.tensor_sub(row_v[:], row_v[:], tmp[0:1, :])
        p.mark(ins, "dve")
    dve_var = p.cnt("prog_dve")

    @block.scalar
    def _(s):
        p.wait(s, "dve", dve_var)
        ins = s.activation(out=row_v[:], in_=row_v[:], func=AF.Ln, bias=consts["eps_row"][:])
        nn = p.mark(ins, "act")
        s.wait_ge(p.sem("prog_act"), nn)
        ins = s.activation(out=row_v[:], in_=row_v[:], func=AF.Exp, scale=-0.5)
        p.mark(ins, "act")
    act_rstd = p.cnt("prog_act")

    @block.tensor
    def _(t):
        p.wait(t, "act", act_rstd)
        ins = t.matmul(pb_bc[:], consts["ones_row_f"][:], row_m[:], start=True, stop=True)
        p.mark(ins, "pe")
    pe_bc1 = p.cnt("prog_pe")

    @block.vector
    def _(v):
        p.wait(v, "pe", pe_bc1)
        ins = v.tensor_copy(mb[:], pb_bc[:])
        p.mark(ins, "dve")
    dve_mb = p.cnt("prog_dve")

    @block.tensor
    def _(t):
        p.wait(t, "dve", dve_mb)
        ins = t.matmul(pb_bc[:], consts["ones_row_f"][:], row_v[:], start=True, stop=True)
        p.mark(ins, "pe")
    pe_bc2 = p.cnt("prog_pe")

    @block.vector
    def _(v):
        p.wait(v, "pe", pe_bc2)
        for c in range(ND):
            ins = v.tensor_sub(tmp[:], x_sb[:, c, :], mb[:])
            nn = p.mark(ins, "dve")
            v.wait_ge(p.sem("prog_dve"), nn)
            ins = v.tensor_tensor(h_view(c), tmp[:], pb_bc[:], OP.mult)
            nn = p.mark(ins, "dve")
            v.wait_ge(p.sem("prog_dve"), nn)
    return p.cnt("prog_dve")


def emit_phase_b(p, consts, src_x, dst_x, w, s_prev=None, n_prev=0):
    nc, block = p.nc, p.block
    TTILE = 1024
    NTILE = T // TTILE

    with contextlib.ExitStack() as st:
        sb = lambda n, sh, dt: st.enter_context(nc.sbuf_tensor(n, sh, dt))
        ps = lambda n, sh: st.enter_context(nc.psum_tensor(n, sh, F32))

        x_sb = sb("b_x", [128, ND, 512], F32)
        h_sb = sb("b_h", [128, ND, TTILE], BF16)
        hf_sb = sb("b_hf", [128, 32, TTILE], BF16)
        o0_sb = sb("b_o0", [128, ND, TTILE], BF16)
        wa_sb = sb("b_wa", [128, 2, ND, 128], BF16)
        wb_sb = sb("b_wb", [128, 2, 32, 128], BF16)
        wg_sb = sb("b_wg", [128, ND, 2], F32)
        bias_sb = sb("b_bias", [128, 96], F32)
        mask_sb = sb("b_mask", [128, 2, 512], U8)
        xres_sb = sb("b_xres", [128, 2, 512], F32)
        t1_sb = sb("b_t1", [128, 512], F32)
        t2_sb = sb("b_t2", [128, 2, 512], F32)
        wgd_sb = sb("b_wgd", [128, ND], F32)
        s01_sb = sb("b_s01", [1, 32], F32)
        scr = {
            "tmp": sb("b_tmp", [128, 512], F32),
            "tmpb": sb("b_tmpb", [128, 2, 512], BF16),
            "row_m": sb("b_rm", [1, 512], F32),
            "row_v": sb("b_rv", [1, 512], F32),
            "mb": sb("b_mb", [128, 512], F32),
        }
        pb_row0 = ps("b_pr0", [1, 512])
        pb_row1 = ps("b_pr1", [1, 512])
        pb_bc = ps("b_pbc", [128, 512])
        pb_h = [ps("b_ph0", [128, 512]), ps("b_ph1", [128, 512])]
        pb_o = [ps("b_po0", [128, 512]), ps("b_po1", [128, 512])]

        s_ldx, s_ldw, s_ldr, s_out = "b_ldx", "b_hw", "b_ldr", "b_out"
        s_gw = "b_gw"

        @block.sync
        def _(sy):
            if s_prev is not None:
                sy.wait_ge(p.sem(s_prev), n_prev)
            dma(sy, p, s_ldw, wg_sb[:], w["WgT"].rearrange("(c p) e -> p c e", p=128))
            dma(sy, p, s_ldw, bias_sb[:, 0:32], w["b0a_T"][:])
            dma(sy, p, s_ldw, bias_sb[:, 32:64], w["b1a_T"][:])
            dma(sy, p, s_ldw, bias_sb[:, 64:80], w["b0b_T"][:])
            dma(sy, p, s_ldw, bias_sb[:, 80:96], w["b1b_T"][:])
        n_ldw0 = p.cnt(s_ldw)
        @block.vector
        def _(v):
            v.wait_ge(p.sem(s_ldw), n_ldw0)
            ins = v.tensor_sub(wgd_sb[:], wg_sb[:, :, 0], wg_sb[:, :, 1])
            p.mark(ins, "dve")
        dve_wgd = p.cnt("prog_dve")
        @block.tensor
        def _(t):
            p.wait(t, "dve", dve_wgd)
            ins = t.matmul(pb_row0[:, 0:16], consts["ones_col_f"][:], wgd_sb[:],
                           start=True, stop=True)
            p.mark(ins, "pe")
        pe_s01 = p.cnt("prog_pe")
        @block.vector
        def _(v):
            p.wait(v, "pe", pe_s01)
            ins = v.reduce_sum(s01_sb[:, 16:17], pb_row0[:, 0:16], axis=AX.X)
            p.mark(ins, "dve")

        def b0a(f): return bias_sb[:, f:f + 1]
        def b1a(f): return bias_sb[:, 32 + f:33 + f]
        def b0b(o): return bias_sb[:, 64 + o:65 + o]
        def b1b(o): return bias_sb[:, 80 + o:81 + o]

        wa_last = {0: 0, 1: 0}
        wb_last = {0: 0, 1: 0}
        out_cnt = {0: 0, 1: 0}
        for tile in range(NTILE):
            t0 = tile * TTILE
            for sub in range(2):
                c0 = t0 + sub * 512
                dve_before = p.cnt("prog_dve")
                pe_before = p.cnt("prog_pe")
                @block.sync
                def _(sy):
                    p.wait(sy, "dve", dve_before)
                    p.wait(sy, "pe", pe_before)
                    if s_prev is not None and tile == 0 and sub == 0:
                        sy.wait_ge(p.sem(s_prev), n_prev)
                    dma(sy, p, s_ldx, x_sb[:],
                        src_x.rearrange("(c p) t -> p c t", p=128)[:, :, c0:c0 + 512])
                n_ldx = p.cnt(s_ldx)
                @block.tensor
                def _(t):
                    t.wait_ge(p.sem(s_ldx), n_ldx)
                    p.wait(t, "act")
                    p.wait(t, "dve")
                @block.scalar
                def _(s):
                    s.wait_ge(p.sem(s_ldx), n_ldx)
                @block.vector
                def _(v):
                    v.wait_ge(p.sem(s_ldx), n_ldx)
                hv = lambda c: h_sb[:, c, sub * 512:(sub + 1) * 512]
                emit_ln_tile(p, consts, x_sb, hv, pb_row0, pb_row1, pb_bc, scr)
                dve_h = p.cnt("prog_dve")

                @block.tensor
                def _(t):
                    p.wait(t, "dve", dve_h)
                    for c in range(ND):
                        ins = t.matmul(pb_row0[:], wgd_sb[:, c:c + 1],
                                       x_sb[:, c, :],
                                       start=(c == 0), stop=(c == ND - 1))
                    p.mark(ins, "pe")
                pe_logit = p.cnt("prog_pe")
                @block.vector
                def _(v):
                    p.wait(v, "pe", pe_logit)
                    ins = v.scalar_tensor_tensor(scr["row_v"][:], scr["row_m"][:],
                                                 s01_sb[0:1, 16:17], pb_row0[:],
                                                 OP.mult, OP.subtract)
                    p.mark(ins, "dve")
                dve_diff = p.cnt("prog_dve")
                @block.tensor
                def _(t):
                    p.wait(t, "dve", dve_diff)
                    ins = t.matmul(pb_bc[:], consts["ones_row_f"][:], scr["row_v"][:],
                                   start=True, stop=True)
                    p.mark(ins, "pe")
                pe_bcm = p.cnt("prog_pe")
                @block.vector
                def _(v):
                    p.wait(v, "pe", pe_bcm)
                    ins = v.tensor_single_scalar(mask_sb[:, sub, :], pb_bc[:],
                                                 0.0, OP.is_le)
                    p.mark(ins, "dve")

            for e in range(2):
                Wa = w["W0a_blk"] if e == 0 else w["W1a_blk"]
                Wb = w["W0b_blk"] if e == 0 else w["W1b_blk"]
                peb = p.cnt("prog_pe")
                actb = p.cnt("prog_act")
                dveb = p.cnt("prog_dve")
                n_wa = {}
                for f in range(32):
                    buf = f % 2
                    s_gwb = s_gw + str(buf)
                    @block.gpsimd
                    def _(g):
                        if s_prev is not None and tile == 0 and e == 0 and f < 2:
                            g.wait_ge(p.sem(s_prev), n_prev)
                        if f >= 2:
                            g.wait_ge(p.sem("prog_pe"), peb + 2 * (f - 2) + 2)
                        elif wa_last[buf] > 0:
                            p.wait(g, "pe", wa_last[buf])
                        dma(g, p, s_gwb, wa_sb[:, buf],
                            Wa[:, f].rearrange("i p m -> p i m"))
                    n_wa[f] = p.cnt(s_gwb)
                def act_h(f, sub):
                    return (actb + 2 * f + sub + 1) if e == 0 else None
                def dve_h(f, sub):
                    return dveb + 2 * f + sub + 1
                @block.tensor
                def _(t):
                    for f in range(32):
                        buf = f % 2
                        t.wait_ge(p.sem(s_gw + str(buf)), n_wa[f])
                        if f == 0:
                            p.wait(t, "act")
                            p.wait(t, "dve")
                        for sub in range(2):
                            if f >= 1:
                                if e == 0:
                                    t.wait_ge(p.sem("prog_act"), act_h(f - 1, sub))
                                else:
                                    t.wait_ge(p.sem("prog_dve"), dve_h(f - 1, sub))
                            for c in range(ND):
                                ins = t.matmul(pb_h[sub][:], wa_sb[:, buf, c, :],
                                               h_sb[:, c, sub * 512:(sub + 1) * 512],
                                               start=(c == 0), stop=(c == ND - 1))
                            p.mark(ins, "pe")
                wa_last[0] = peb + 63
                wa_last[1] = peb + 64
                if e == 0:
                    @block.scalar
                    def _(s):
                        for f in range(32):
                            for sub in range(2):
                                s.wait_ge(p.sem("prog_pe"), peb + 2 * f + sub + 1)
                                ins = s.activation(out=hf_sb[:, f, sub * 512:(sub + 1) * 512],
                                                   in_=pb_h[sub][:], func=AF.Gelu,
                                                   bias=b0a(f))
                                p.mark(ins, "act")
                else:
                    @block.vector
                    def _(v):
                        for f in range(32):
                            for sub in range(2):
                                v.wait_ge(p.sem("prog_pe"), peb + 2 * f + sub + 1)
                                ins = v.tensor_scalar(hf_sb[:, f, sub * 512:(sub + 1) * 512],
                                                      pb_h[sub][:], b1a(f), 0.0,
                                                      OP.add, OP.max)
                                p.mark(ins, "dve")
                peb2 = p.cnt("prog_pe")
                actb2 = p.cnt("prog_act")
                dveb2 = p.cnt("prog_dve")
                n_wb = {}
                for o in range(ND):
                    buf = o % 2
                    s_gwb = s_gw + "b" + str(buf)
                    @block.gpsimd
                    def _(g):
                        if o >= 2:
                            g.wait_ge(p.sem("prog_pe"), peb2 + 2 * (o - 2) + 2)
                        elif wb_last[buf] > 0:
                            p.wait(g, "pe", wb_last[buf])
                        dma(g, p, s_gwb, wb_sb[:, buf],
                            Wb[:, o].rearrange("i p m -> p i m"))
                    n_wb[o] = p.cnt(s_gwb)
                if e == 1:
                    n_xr = {}
                    for o in range(ND):
                        @block.scalar
                        def _(s):
                            if o >= 1:
                                s.wait_ge(p.sem("prog_dve"), dveb2 + 3 * (2 * o))
                            else:
                                s.wait_ge(p.sem("prog_dve"), dveb2)
                            dma(s, p, s_ldr, xres_sb[:, 0, :],
                                src_x[o * 128:(o + 1) * 128, t0:t0 + 512])
                            dma(s, p, s_ldr, xres_sb[:, 1, :],
                                src_x[o * 128:(o + 1) * 128, t0 + 512:t0 + 1024])
                        n_xr[o] = p.cnt(s_ldr)
                @block.tensor
                def _(t):
                    for o in range(ND):
                        buf = o % 2
                        t.wait_ge(p.sem(s_gw + "b" + str(buf)), n_wb[o])
                        for sub in range(2):
                            if o >= 1:
                                if e == 0:
                                    t.wait_ge(p.sem("prog_act"), actb2 + 2 * (o - 1) + sub + 1)
                                else:
                                    t.wait_ge(p.sem("prog_dve"), dveb2 + 3 * (2 * (o - 1) + sub) + 3)
                            if o == 0 and sub == 0:
                                p.wait(t, "act")
                                p.wait(t, "dve")
                            for f in range(32):
                                ins = t.matmul(pb_o[sub][:], wb_sb[:, buf, f, :],
                                               hf_sb[:, f, sub * 512:(sub + 1) * 512],
                                               start=(f == 0), stop=(f == 31))
                            p.mark(ins, "pe")
                if e == 0:
                    @block.scalar
                    def _(s):
                        for o in range(ND):
                            for sub in range(2):
                                s.wait_ge(p.sem("prog_pe"), peb2 + 2 * o + sub + 1)
                                ins = s.activation(out=o0_sb[:, o, sub * 512:(sub + 1) * 512],
                                                   in_=pb_o[sub][:], func=AF.Identity,
                                                   bias=b0b(o))
                                p.mark(ins, "act")
                else:
                    @block.vector
                    def _(v):
                        for o in range(ND):
                            v.wait_ge(p.sem(s_ldr), n_xr[o])
                            if o == 0:
                                p.wait(v, "act")
                            for sub in range(2):
                                s_ou = s_out + str(sub)
                                n_op = out_cnt[sub]
                                v.wait_ge(p.sem("prog_pe"), peb2 + 2 * o + sub + 1)
                                if n_op > 0:
                                    v.wait_ge(p.sem(s_ou), n_op)
                                ins = v.tensor_scalar(t1_sb[:], pb_o[sub][:], b1b(o),
                                                      None, OP.add)
                                nn = p.mark(ins, "dve")
                                v.wait_ge(p.sem("prog_dve"), nn)
                                ins = v.select(t2_sb[:, sub, :], mask_sb[:, sub, :],
                                               o0_sb[:, o, sub * 512:(sub + 1) * 512],
                                               t1_sb[:], add_drain=True)
                                nn = p.mark(ins, "dve")
                                v.wait_ge(p.sem("prog_dve"), nn)
                                ins = v.tensor_add(t2_sb[:, sub, :], t2_sb[:, sub, :],
                                                   xres_sb[:, sub, :])
                                p.mark(ins, "dve")
                                out_cnt[sub] += 16
                    @block.sync
                    def _(sy):
                        for o in range(ND):
                            for sub in range(2):
                                sy.wait_ge(p.sem("prog_dve"), dveb2 + 3 * (2 * o + sub) + 3)
                                dma(sy, p, s_out + str(sub),
                                    dst_x[o * 128:(o + 1) * 128,
                                          t0 + sub * 512:t0 + (sub + 1) * 512],
                                    t2_sb[:, sub, :])
                wb_last[0] = peb2 + 31
                wb_last[1] = peb2 + 32
        n_o0 = p.cnt(s_out + "0"); n_o1 = p.cnt(s_out + "1")
        @block.sync
        def _(sy):
            sy.wait_ge(p.sem(s_out + "0"), n_o0)
            sy.wait_ge(p.sem(s_out + "1"), n_o1)
            ins = sy.engine_nop() if hasattr(sy, "engine_nop") else sy.nop()
            h, n = p.inc("b_done", 1)
            ins.then_inc(h, 1)
    return "b_done", p.cnt("b_done")



def emit_phase_a(p, consts, src_x, dst_x, ctx_hbm, w, s_prev=None, n_prev=0, dbg=None):
    nc, block = p.nc, p.block
    s_ldx, s_ldw, s_ldr, s_ldc, s_out, s_ctxw = ("a_ldx", "a_hw", "a_ldr",
                                                 "a_ldc", "a_out", "a_ctxw")
    s_gw = "a_gw"

    with contextlib.ExitStack() as stQ:
        sbQ = lambda n, sh, dt: stQ.enter_context(nc.sbuf_tensor(n, sh, dt))
        qlat = sbQ("a_qlat", [128, 8, T], BF16)
        ckv = sbQ("a_ckv", [128, 4, T], BF16)
        @block.sync
        def _(sy):
            if s_prev is not None:
                sy.wait_ge(p.sem(s_prev), n_prev)

        with contextlib.ExitStack() as stA:
            sbA = lambda n, sh, dt: stA.enter_context(nc.sbuf_tensor(n, sh, dt))
            x_sb = sbA("a_x", [128, ND, 512], F32)
            h_sb = sbA("a_h", [128, ND, 512], BF16)
            wqd_sb = sbA("a_wqd", [128, ND, 8, 128], BF16)
            wkvd_sb = sbA("a_wkvd", [128, ND, 4, 128], BF16)
            scr = {
                "tmp": sbA("a_tmp", [128, 512], F32),
            "tmpb": sbA("a_tmpb", [128, 2, 512], BF16),
                "row_m": sbA("a_rm", [1, 512], F32),
                "row_v": sbA("a_rv", [1, 512], F32),
                "mb": sbA("a_mb", [128, 512], F32),
            }
            with contextlib.ExitStack() as stP:
                psA = lambda n, sh: stP.enter_context(nc.psum_tensor(n, sh, F32))
                pb_row0 = psA("a_pr0", [1, 512])
                pb_row1 = psA("a_pr1", [1, 512])
                pb_bc = psA("a_pbc", [128, 512])
                pb_p = [psA("a_pp0", [128, 512]), psA("a_pp1", [128, 512])]

                @block.gpsimd
                def _(g):
                    for ih in range(2):
                        dma(g, p, s_gw, wqd_sb[:, ih * 8:(ih + 1) * 8],
                            w["Wqd_blk"][ih * 8:(ih + 1) * 8].rearrange("i o p m -> p i o m"))
                    dma(g, p, s_gw, wkvd_sb[:], w["Wkvd_blk"].rearrange("i o p m -> p i o m"))
                n_ldw0 = p.cnt(s_gw)

                for ts in range(NTT):
                    c0 = ts * 512
                    dve_b = p.cnt("prog_dve")
                    pe_b = p.cnt("prog_pe")
                    @block.sync
                    def _(sy):
                        p.wait(sy, "dve", dve_b)
                        p.wait(sy, "pe", pe_b)
                        dma(sy, p, s_ldx, x_sb[:],
                            src_x.rearrange("(c p) t -> p c t", p=128)[:, :, c0:c0 + 512])
                    n_ldx = p.cnt(s_ldx)
                    @block.tensor
                    def _(t):
                        t.wait_ge(p.sem(s_ldx), n_ldx)
                        if ts == 0:
                            t.wait_ge(p.sem(s_gw), n_ldw0)
                        p.wait(t, "act")
                        p.wait(t, "dve")
                    @block.scalar
                    def _(s):
                        s.wait_ge(p.sem(s_ldx), n_ldx)
                    @block.vector
                    def _(v):
                        v.wait_ge(p.sem(s_ldx), n_ldx)
                    hv = lambda c: h_sb[:, c, :]
                    dve_h = emit_ln_tile(p, consts, x_sb, hv, pb_row0, pb_row1, pb_bc, scr)

                    for proj, wsb, nO, dst in (("q", wqd_sb, 8, qlat), ("kv", wkvd_sb, 4, ckv)):
                        for o in range(nO):
                            bank = pb_p[o % 2]
                            dve_free = p.cnt("prog_dve")
                            @block.tensor
                            def _(t):
                                if proj == "q" and o == 0:
                                    p.wait(t, "dve", dve_h)
                                else:
                                    p.wait(t, "dve", dve_free)
                                for c in range(ND):
                                    ins = t.matmul(bank[:], wsb[:, c, o, :], h_sb[:, c, :],
                                                   start=(c == 0), stop=(c == ND - 1))
                                p.mark(ins, "pe")
                            pe_g = p.cnt("prog_pe")
                            @block.vector
                            def _(v):
                                p.wait(v, "pe", pe_g)
                                ins = v.tensor_copy(dst[:, o, c0:c0 + 512], bank[:])
                                p.mark(ins, "dve")

        with contextlib.ExitStack() as stH:
            sbH = lambda n, sh, dt: stH.enter_context(nc.sbuf_tensor(n, sh, dt))
            mask_a = sbH("a_mask", [128, 4, 512], BF16)
            wq_sb = sbH("a_wq", [128, 2, 8, 128], BF16)
            wk_sb = sbH("a_wk", [128, 2, 4, 128], BF16)
            wv_sb = sbH("a_wv", [128, 2, 4, 128], BF16)
            qt_sb = sbH("a_qt", [128, T], BF16)
            kt_sb = sbH("a_kt", [128, T], BF16)
            v_sb = sbH("a_v", [128, 32, 128], BF16)
            p_sb = sbH("a_p", [128, 4, 512], BF16)
            sq_sb = sbH("a_sq", [128, 2, 512], BF16)
            bc_sb = sbH("a_bc", [128, 512], F32)
            ctx_sb = sbH("a_ctx", [128, 2, 512], BF16)
            rsum_sb = sbH("a_rsum", [1, 512], F32)
            nrm_sb = sbH("a_nrm", [1, 32], F32)
            negc_sb = sbH("a_negc", [128, 1], F32)
            with contextlib.ExitStack() as stP:
                psH = lambda n, sh: stP.enter_context(nc.psum_tensor(n, sh, F32))
                pb_S = [psH("a_ps0", [128, 512]), psH("a_ps1", [128, 512]),
                        psH("a_ps2", [128, 512])]
                pb_sum = [psH("a_psum0", [1, 512]), psH("a_psum1", [1, 512])]
                pb_ctx = [psH("a_pctx0", [128, 512]), psH("a_pctx1", [128, 512])]
                pb_bc2 = psH("a_pbc2", [128, 512])

                pe_a1 = p.cnt("prog_pe"); dve_a1 = p.cnt("prog_dve"); act_a1 = p.cnt("prog_act")
                pe_a1b = p.cnt("prog_pe")
                @block.sync
                def _(sy):
                    p.wait(sy, "pe", pe_a1b)
                    p.wait(sy, "dve", dve_a1)
                    p.wait(sy, "act", act_a1)
                    dma(sy, p, s_ldw, mask_a[:], w["c_mask_a"].rearrange("d p f -> p d f"))
                n_mask = p.cnt(s_ldw)
                wql = {0: 0, 1: 0}
                @block.vector
                def _(v):
                    v.wait_ge(p.sem(s_ldw), n_mask)
                for hd in range(16):
                    buf = hd % 2
                    s_gwb = s_gw + str(buf)
                    @block.gpsimd
                    def _(g):
                        if hd < 2:
                            p.wait(g, "pe", pe_a1)
                            p.wait(g, "dve", dve_a1)
                            p.wait(g, "act", act_a1)
                        p.wait(g, "pe", wql[buf])
                        dma(g, p, s_gwb, wq_sb[:, buf],
                            w["Wqup_blk"][:, hd].rearrange("r p m -> p r m"))
                        dma(g, p, s_gwb, wk_sb[:, buf],
                            w["Wkup_blk"][:, hd].rearrange("r p m -> p r m"))
                        dma(g, p, s_gwb, wv_sb[:, buf],
                            w["Wvup_blk"][:, hd].rearrange("r p m -> p r m"))
                    n_w = p.cnt(s_gwb)

                    for ts in range(NTT):
                        for which, wsb, nR, dst in (("q", wq_sb, 8, qt_sb), ("k", wk_sb, 4, kt_sb)):
                            bank = pb_S[ts % 2]
                            dve_free = p.cnt("prog_dve")
                            act_free = p.cnt("prog_act")
                            @block.tensor
                            def _(t):
                                t.wait_ge(p.sem(s_gwb), n_w)
                                p.wait(t, "dve", dve_free)
                                p.wait(t, "act", act_free)
                                srcq = qlat if which == "q" else ckv
                                for r in range(nR):
                                    ins = t.matmul(bank[:], wsb[:, buf, r, :],
                                                   srcq[:, r, ts * 512:(ts + 1) * 512],
                                                   start=(r == 0), stop=(r == nR - 1))
                                p.mark(ins, "pe")
                            pe_g = p.cnt("prog_pe")
                            @block.vector
                            def _(v):
                                p.wait(v, "pe", pe_g)
                                ins = v.tensor_copy(dst[:, ts * 512:(ts + 1) * 512], bank[:])
                                p.mark(ins, "dve")
                    for tc in range(32):
                        bank = pb_S[tc % 2]
                        dve_free = p.cnt("prog_dve")
                        @block.tensor
                        def _(t):
                            p.wait(t, "dve", dve_free)
                            for r in range(4):
                                ins = t.matmul(bank[:, 0:128], ckv[:, r, tc * 128:(tc + 1) * 128],
                                               wv_sb[:, buf, r, :],
                                               start=(r == 0), stop=(r == 3))
                            p.mark(ins, "pe")
                        pe_g = p.cnt("prog_pe")
                        @block.vector
                        def _(v):
                            p.wait(v, "pe", pe_g)
                            ins = v.tensor_copy(v_sb[:, tc, :], bank[:, 0:128])
                            p.mark(ins, "dve")
                    wql[buf] = p.cnt("prog_pe")

                    peb = p.cnt("prog_pe")
                    actb = p.cnt("prog_act")
                    dveb = p.cnt("prog_dve")
                    srcs = [(qt_sb, ts) for ts in range(NTT)] + [(kt_sb, ts) for ts in range(NTT)]
                    @block.scalar
                    def _(s):
                        p.wait(s, "pe", peb)
                        p.wait(s, "dve", dveb)
                        for idx, (src_t, ts) in enumerate(srcs):
                            if idx >= 2:
                                s.wait_ge(p.sem("prog_pe"), peb + idx - 1)
                            ins = s.activation(out=sq_sb[:, idx % 2, :],
                                               in_=src_t[:, ts * 512:(ts + 1) * 512],
                                               func=AF.Square)
                            p.mark(ins, "act")
                    @block.tensor
                    def _(t):
                        for idx in range(16):
                            t.wait_ge(p.sem("prog_act"), actb + idx + 1)
                            if idx >= 2:
                                t.wait_ge(p.sem("prog_dve"), dveb + idx - 1)
                            ins = t.matmul(pb_sum[idx % 2][:], consts["ones_col_b"][:],
                                           sq_sb[:, idx % 2, :], start=True, stop=True)
                            p.mark(ins, "pe")
                    @block.vector
                    def _(v):
                        for idx in range(16):
                            v.wait_ge(p.sem("prog_pe"), peb + idx + 1)
                            ins = v.reduce_max(nrm_sb[0:1, idx:idx + 1],
                                               pb_sum[idx % 2][:], axis=AX.X)
                            p.mark(ins, "dve")
                    dve_n = p.cnt("prog_dve")
                    @block.vector
                    def _(v):
                        v.wait_ge(p.sem("prog_dve"), dve_n)
                        v.reduce_max(nrm_sb[0:1, 16:17], nrm_sb[0:1, 0:8], axis=AX.X)
                        ins = v.reduce_max(nrm_sb[0:1, 17:18], nrm_sb[0:1, 8:16], axis=AX.X)
                        nn = p.mark(ins, "dve")
                        v.wait_ge(p.sem("prog_dve"), nn)
                        ins = v.tensor_tensor(nrm_sb[0:1, 18:19], nrm_sb[0:1, 16:17],
                                              nrm_sb[0:1, 17:18], OP.mult)
                        p.mark(ins, "dve")
                    dve_c = p.cnt("prog_dve")
                    @block.scalar
                    def _(s):
                        p.wait(s, "dve", dve_c)
                        ins = s.activation(out=nrm_sb[0:1, 19:20], in_=nrm_sb[0:1, 18:19],
                                           func=AF.Sqrt, scale=1.0 / 128.0)
                        nn = p.mark(ins, "act")
                        s.wait_ge(p.sem("prog_act"), nn)
                        ins = s.activation(out=nrm_sb[0:1, 20:21], in_=nrm_sb[0:1, 19:20],
                                           func=AF.Copy, scale=-1.0)
                        p.mark(ins, "act")
                    act_c = p.cnt("prog_act")
                    dve_negc_f = p.cnt("prog_dve")
                    @block.tensor
                    def _(t):
                        p.wait(t, "act", act_c)
                        p.wait(t, "dve", dve_negc_f)
                        ins = t.matmul(pb_bc2[:, 0:1], consts["ones_row_f"][:],
                                       nrm_sb[0:1, 20:21], start=True, stop=True)
                        p.mark(ins, "pe")
                    pe_c = p.cnt("prog_pe")
                    @block.vector
                    def _(v):
                        p.wait(v, "pe", pe_c)
                        ins = v.tensor_copy(negc_sb[:], pb_bc2[:, 0:1])
                        p.mark(ins, "dve")
                    dve_negc = p.cnt("prog_dve")

                    for qt in range(NTT):
                        nS = 4 * qt + 4
                        q_sl = qt_sb[:, qt * 512:(qt + 1) * 512]
                        sum_bank = pb_sum[qt % 2]
                        ctx_bank = pb_ctx[qt % 2]
                        dve_qt0 = p.cnt("prog_dve")
                        pe0 = p.cnt("prog_pe")
                        act0 = p.cnt("prog_act")
                        dve0 = p.cnt("prog_dve")
                        def pe_score(c):
                            return pe0 + (1 if c == 0 else 2 * c)
                        def pe_sumpv(c):
                            return pe0 + (2 * c + 3 if c <= nS - 2 else 2 * nS)
                        def act_exp(c):
                            return act0 + c + 1
                        def dve_mask(c):
                            return dve0 + (c - (nS - 4)) + 1
                        @block.tensor
                        def _(t):
                            def emit_score(c):
                                if c == 0:
                                    p.wait(t, "dve", dve_negc)
                                    p.wait(t, "dve", dve_qt0)
                                if c >= 3:
                                    t.wait_ge(p.sem("prog_act"), act_exp(c - 3))
                                ins = t.matmul(pb_S[c % 3][:], kt_sb[:, c * 128:(c + 1) * 128],
                                               q_sl, start=True, stop=True)
                                p.mark(ins, "pe")
                            def emit_sumpv(c):
                                t.wait_ge(p.sem("prog_act"), act_exp(c))
                                if c >= nS - 4:
                                    t.wait_ge(p.sem("prog_dve"), dve_mask(c))
                                t.matmul(sum_bank[:], consts["ones_col_b"][:], p_sb[:, c % 4, :],
                                         start=(c == 0), stop=(c == nS - 1),
                                         skip_group_check=True)
                                ins = t.matmul(ctx_bank[:], v_sb[:, c, :], p_sb[:, c % 4, :],
                                               start=(c == 0), stop=(c == nS - 1),
                                               skip_group_check=True)
                                p.mark(ins, "pe")
                            emit_score(0)
                            for c in range(1, nS):
                                emit_score(c)
                                emit_sumpv(c - 1)
                            emit_sumpv(nS - 1)
                        @block.scalar
                        def _(s):
                            for c in range(nS):
                                s.wait_ge(p.sem("prog_pe"),
                                          max(pe_score(c), pe_sumpv(c - 4) if c >= 4 else 0))
                                if c >= 4 and c - 4 >= nS - 4:
                                    s.wait_ge(p.sem("prog_dve"), dve_mask(c - 4))
                                ins = s.activation(out=p_sb[:, c % 4, :], in_=pb_S[c % 3][:],
                                                   func=AF.Exp, bias=negc_sb[:],
                                                   scale=INV_SQRT_DK)
                                p.mark(ins, "act")
                        @block.vector
                        def _(v):
                            for c in range(nS - 4, nS):
                                v.wait_ge(p.sem("prog_act"), act_exp(c))
                                ins = v.tensor_tensor(p_sb[:, c % 4, :], p_sb[:, c % 4, :],
                                                      mask_a[:, c - (nS - 4), :], OP.mult)
                                p.mark(ins, "dve")
                        pe_attn = p.cnt("prog_pe")
                        @block.vector
                        def _(v):
                            p.wait(v, "pe", pe_attn)
                            ins = v.reciprocal(rsum_sb[:], sum_bank[:])
                            p.mark(ins, "dve")
                        dve_r = p.cnt("prog_dve")
                        @block.tensor
                        def _(t):
                            p.wait(t, "dve", dve_r)
                            ins = t.matmul(pb_bc2[:], consts["ones_row_f"][:], rsum_sb[:],
                                           start=True, stop=True)
                            p.mark(ins, "pe")
                        pe_bc = p.cnt("prog_pe")
                        s_cw = s_ctxw + str(qt % 2)
                        n_ctxw_prev = p.cnt(s_cw)
                        @block.vector
                        def _(v):
                            p.wait(v, "pe", pe_bc)
                            ins = v.tensor_copy(bc_sb[:], pb_bc2[:])
                            nn = p.mark(ins, "dve")
                            v.wait_ge(p.sem("prog_dve"), nn)
                            if n_ctxw_prev > 0:
                                v.wait_ge(p.sem(s_cw), n_ctxw_prev)
                            ins = v.tensor_tensor(ctx_sb[:, qt % 2, :], ctx_bank[:],
                                                  bc_sb[:], OP.mult)
                            p.mark(ins, "dve")
                        dve_ctx = p.cnt("prog_dve")
                        @block.sync
                        def _(sy):
                            p.wait(sy, "dve", dve_ctx)
                            dma(sy, p, s_cw,
                                ctx_hbm[hd * 128:(hd + 1) * 128, qt * 512:(qt + 1) * 512],
                                ctx_sb[:, qt % 2, :])

    n_cw0 = p.cnt(s_ctxw + "0")
    n_cw1 = p.cnt(s_ctxw + "1")
    with contextlib.ExitStack() as stO:
        sbO = lambda n, sh, dt: stO.enter_context(nc.sbuf_tensor(n, sh, dt))
        wo_sb = sbO("a_wo", [128, ND, ND, 128], BF16)
        bo_sb = sbO("a_bo", [128, ND], F32)
        ctxt_sb = sbO("a_ctxt", [128, 2, ND, 512], BF16)
        xres_sb = sbO("a_xres", [128, 4, 512], F32)
        t2_sb = sbO("a_t2", [128, 2, 512], F32)
        with contextlib.ExitStack() as stP:
            psO = lambda n, sh: stP.enter_context(nc.psum_tensor(n, sh, F32))
            pb_o = [psO("a_po0", [128, 512]), psO("a_po1", [128, 512])]
            pe_a2 = p.cnt("prog_pe"); dve_a2 = p.cnt("prog_dve"); act_a2 = p.cnt("prog_act")
            @block.sync
            def _(sy):
                p.wait(sy, "pe", pe_a2); p.wait(sy, "dve", dve_a2); p.wait(sy, "act", act_a2)
                sy.wait_ge(p.sem(s_ctxw + "0"), n_cw0)
                sy.wait_ge(p.sem(s_ctxw + "1"), n_cw1)
                dma(sy, p, s_ldw, bo_sb[:], w["bo_T"][:])
            n_bo_h = p.cnt(s_ldw)
            @block.gpsimd
            def _(g):
                p.wait(g, "pe", pe_a2); p.wait(g, "dve", dve_a2); p.wait(g, "act", act_a2)
                g.wait_ge(p.sem(s_ctxw + "0"), n_cw0)
                g.wait_ge(p.sem(s_ctxw + "1"), n_cw1)
                for ih in range(4):
                    dma(g, p, s_gw + "0", wo_sb[:, ih * 4:(ih + 1) * 4],
                        w["Wo_blk"][ih * 4:(ih + 1) * 4].rearrange("i o p m -> p i o m"))
            n_bo = p.cnt(s_gw + "0")
            for ts in range(NTT):
                cbuf = ts % 2
                s_lc = s_ldc + str(cbuf)
                pe_b = p.cnt("prog_pe")
                @block.sync
                def _(sy):
                    sy.wait_ge(p.sem(s_ctxw + "0"), n_cw0)
                    sy.wait_ge(p.sem(s_ctxw + "1"), n_cw1)
                    p.wait(sy, "pe", pe_b)
                    dma(sy, p, s_lc, ctxt_sb[:, cbuf],
                        ctx_hbm.rearrange("(c p) t -> p c t", p=128)[:, :, ts * 512:(ts + 1) * 512])
                n_ldc = p.cnt(s_lc)
                if dbg is not None and ts == 0:
                    @block.sync
                    def _(sy):
                        sy.wait_ge(p.sem(s_lc), n_ldc)
                        sy.wait_ge(p.sem(s_gw + "0"), n_bo)
                        dma(sy, p, "dbgd", dbg["ctxt0"], ctxt_sb[:, 0])
                        dma(sy, p, "dbgd", dbg["wo"], wo_sb[:])
                for o in range(ND):
                    dve_res = p.cnt("prog_dve")
                    s_lr = s_ldr + str(o % 4)
                    @block.scalar
                    def _(s):
                        if ts == 0 and o == 0:
                            s.wait_ge(p.sem(s_ctxw + "0"), n_cw0)
                            s.wait_ge(p.sem(s_ctxw + "1"), n_cw1)
                        p.wait(s, "dve", dve_res)
                        dma(s, p, s_lr, xres_sb[:, o % 4, :],
                            src_x[o * 128:(o + 1) * 128, ts * 512:(ts + 1) * 512])
                    n_r = p.cnt(s_lr)
                    bank = pb_o[o % 2]
                    dve_free = p.cnt("prog_dve")
                    @block.tensor
                    def _(t):
                        t.wait_ge(p.sem(s_gw + "0"), n_bo)
                        t.wait_ge(p.sem(s_ldw), n_bo_h)
                        t.wait_ge(p.sem(s_lc), n_ldc)
                        p.wait(t, "dve", dve_free)
                        for c in range(ND):
                            ins = t.matmul(bank[:], wo_sb[:, c, o, :],
                                           ctxt_sb[:, cbuf, c, :],
                                           start=(c == 0), stop=(c == ND - 1))
                        p.mark(ins, "pe")
                    pe_g = p.cnt("prog_pe")
                    s_ou = s_out + str(o % 2)
                    n_out_prev = p.cnt(s_ou)
                    @block.vector
                    def _(v):
                        p.wait(v, "pe", pe_g)
                        v.wait_ge(p.sem(s_lr), n_r)
                        if n_out_prev > 0:
                            v.wait_ge(p.sem(s_ou), n_out_prev)
                        ins = v.scalar_tensor_tensor(t2_sb[:, o % 2, :], bank[:],
                                                     bo_sb[:, o:o + 1], xres_sb[:, o % 4, :],
                                                     OP.add, OP.add)
                        p.mark(ins, "dve")
                    dve_g = p.cnt("prog_dve")
                    @block.sync
                    def _(sy):
                        p.wait(sy, "dve", dve_g)
                        dma(sy, p, s_ou,
                            dst_x[o * 128:(o + 1) * 128, ts * 512:(ts + 1) * 512],
                            t2_sb[:, o % 2, :])
    n_o0 = p.cnt(s_out + "0"); n_o1 = p.cnt(s_out + "1")
    @block.sync
    def _(sy):
        sy.wait_ge(p.sem(s_out + "0"), n_o0)
        sy.wait_ge(p.sem(s_out + "1"), n_o1)
        ins = sy.engine_nop() if hasattr(sy, "engine_nop") else sy.nop()
        h, n = p.inc("a_done", 1)
        ins.then_inc(h, 1)
    return "a_done", p.cnt("a_done")



def emit_phase_c(p, consts, src_x, dst_x, ctx_hbm, w, s_prev=None, n_prev=0):
    nc, block = p.nc, p.block
    s_ldx, s_hw, s_out, s_ow, s_ldc, s_ldr = ("c_ldx", "c_hw", "c_out",
                                              "c_ow", "c_ldc", "c_ldr")
    s_gw = "c_gw"

    with contextlib.ExitStack() as stC:
        stQ = contextlib.ExitStack()
        sbC = lambda n, sh, dt: stQ.enter_context(nc.sbuf_tensor(n, sh, dt))
        h_sb = sbC("c_h", [128, ND, T], BF16)
        fw_sb = sbC("c_fw", [128, 4, 128], BF16)
        fb_sb = sbC("c_fb", [128, 8], F32)
        mask_c = sbC("c_maskc", [128, 128], BF16)

        with contextlib.ExitStack() as stA:
            sbA = lambda n, sh, dt: stA.enter_context(nc.sbuf_tensor(n, sh, dt))
            x2_sb = sbA("c_x", [128, 2, ND, 512], F32)
            scr = {
                "tmp": sbA("c_tmp", [128, 512], F32),
            "tmpb": sbA("c_tmpb", [128, 2, 512], BF16),
                "row_m": sbA("c_rm", [1, 512], F32),
                "row_v": sbA("c_rv", [1, 512], F32),
                "mb": sbA("c_mb", [128, 512], F32),
            }
            with contextlib.ExitStack() as stP:
                psA = lambda n, sh: stP.enter_context(nc.psum_tensor(n, sh, F32))
                pb_row0 = psA("c_pr0", [1, 512])
                pb_row1 = psA("c_pr1", [1, 512])
                pb_bc = psA("c_pbc", [128, 512])

                @block.sync
                def _(sy):
                    if s_prev is not None:
                        sy.wait_ge(p.sem(s_prev), n_prev)
                    dma(sy, p, s_hw, mask_c[:], w["c_mask_c"])
                    dma(sy, p, s_hw, fb_sb[:, 0:1], w["fq1_b"])
                    dma(sy, p, s_hw, fb_sb[:, 1:2], w["fq2_b"])
                    dma(sy, p, s_hw, fb_sb[:, 2:3], w["fk1_b"])
                    dma(sy, p, s_hw, fb_sb[:, 3:4], w["fk2_b"])
                @block.gpsimd
                def _(g):
                    if s_prev is not None:
                        g.wait_ge(p.sem(s_prev), n_prev)
                    dma(g, p, "c_gf", fw_sb[:, 0, :], w["fq1_wT"])
                    dma(g, p, "c_gf", fw_sb[:, 1, :], w["fq2_wT"])
                    dma(g, p, "c_gf", fw_sb[:, 2, :], w["fk1_wT"])
                    dma(g, p, "c_gf", fw_sb[:, 3, :], w["fk2_wT"])
                n_hw0 = p.cnt(s_hw)
                n_gf = p.cnt("c_gf")
                @block.vector
                def _(v):
                    v.wait_ge(p.sem(s_hw), n_hw0)
                    ins = v.tensor_scalar(fb_sb[:, 4:5], fb_sb[:, 0:1], INV_SQRT_DK,
                                          None, OP.mult)
                    p.mark(ins, "dve")

                slot_guard = {0: (0, 0), 1: (0, 0)}
                for ts in range(NTT):
                    c0 = ts * 512
                    sl = ts % 2
                    x_sb = x2_sb[:, sl]
                    gd, gp = slot_guard[sl]
                    s_lx = s_ldx + str(sl)
                    @block.sync
                    def _(sy):
                        if gd > 0:
                            sy.wait_ge(p.sem("prog_dve"), gd)
                        if gp > 0:
                            sy.wait_ge(p.sem("prog_pe"), gp)
                        dma(sy, p, s_lx, x_sb,
                            src_x.rearrange("(c p) t -> p c t", p=128)[:, :, c0:c0 + 512])
                    n_ldx = p.cnt(s_lx)
                    @block.tensor
                    def _(t):
                        t.wait_ge(p.sem(s_lx), n_ldx)
                        p.wait(t, "act")
                        p.wait(t, "dve")
                    @block.scalar
                    def _(s):
                        s.wait_ge(p.sem(s_lx), n_ldx)
                    @block.vector
                    def _(v):
                        v.wait_ge(p.sem(s_lx), n_ldx)
                    hv = lambda c: h_sb[:, c, c0:c0 + 512]
                    emit_ln_tile(p, consts, x_sb, hv, pb_row0, pb_row1, pb_bc, scr)
                    slot_guard[sl] = (p.cnt("prog_dve"), p.cnt("prog_pe"))

        with contextlib.ExitStack() as stH:
            sbH = lambda n, sh, dt: stH.enter_context(nc.sbuf_tensor(n, sh, dt))
            wq_sb = sbH("c_wq", [128, 2, ND, 128], BF16)
            wk_sb = sbH("c_wk", [128, 2, ND, 128], BF16)
            wv_sb = sbH("c_wv", [128, 2, ND, 128], BF16)
            qt_sb = sbH("c_qt", [128, T], BF16)
            kt_sb = sbH("c_kt", [128, T], BF16)
            vt_sb = sbH("c_vt", [128, T], BF16)
            qp_sb = sbH("c_qp", [128, T], BF16)
            kp_sb = kt_sb
            ktm_sb = sbH("c_ktm", [128, 32, 128], BF16)
            vtm_sb = sbH("c_vtm", [128, 32, 128], BF16)
            kv_sb = sbH("c_kv", [128, 128], BF16)
            o_sb = qt_sb
            t1_sb = sbH("c_t1", [128, 512], F32)
            sT_sb = sbH("c_sT", [128, 128], BF16)
            pe_c1 = p.cnt("prog_pe"); dve_c1 = p.cnt("prog_dve"); act_c1 = p.cnt("prog_act")
            with contextlib.ExitStack() as stP:
                psH = lambda n, sh, dt=F32: stP.enter_context(nc.psum_tensor(n, sh, dt))
                pb_p = [psH("c_pp0", [128, 512]), psH("c_pp1", [128, 512])]
                pb_st = psH("c_pst", [128, 128])
                pb_o = [psH("c_po0", [128, 512]), psH("c_po1", [128, 512])]
                pb_sT = [psH("c_pt0", [128, 128])]
                pb_tx = [psH("c_ptx0", [128, 128], BF16), psH("c_ptx1", [128, 128], BF16)]

                wql = {0: 0, 1: 0}
                for hd in range(16):
                    buf = hd % 2
                    s_gwb = s_gw + str(buf)
                    @block.gpsimd
                    def _(g):
                        if hd < 2:
                            p.wait(g, "pe", pe_c1); p.wait(g, "dve", dve_c1)
                            p.wait(g, "act", act_c1)
                        p.wait(g, "pe", wql[buf])
                        dma(g, p, s_gwb, wq_sb[:, buf],
                            w["wq_blk"][:, hd].rearrange("i p m -> p i m"))
                        dma(g, p, s_gwb, wk_sb[:, buf],
                            w["wk_blk"][:, hd].rearrange("i p m -> p i m"))
                        dma(g, p, s_gwb, wv_sb[:, buf],
                            w["wv_blk"][:, hd].rearrange("i p m -> p i m"))
                    n_w = p.cnt(s_gwb)

                    n_ow_head = p.cnt(s_ow)
                    for which, wsb, dst in (("q", wq_sb, qt_sb), ("k", wk_sb, kt_sb),
                                            ("v", wv_sb, vt_sb)):
                        for ts in range(NTT):
                            bank = pb_p[ts % 2]
                            dve_free = p.cnt("prog_dve")
                            act_free = p.cnt("prog_act")
                            @block.tensor
                            def _(t):
                                t.wait_ge(p.sem(s_gwb), n_w)
                                p.wait(t, "dve", dve_free)
                                p.wait(t, "act", act_free)
                                for c in range(ND):
                                    ins = t.matmul(bank[:], wsb[:, buf, c, :],
                                                   h_sb[:, c, ts * 512:(ts + 1) * 512],
                                                   start=(c == 0), stop=(c == ND - 1))
                                p.mark(ins, "pe")
                            pe_g = p.cnt("prog_pe")
                            @block.vector
                            def _(v):
                                p.wait(v, "pe", pe_g)
                                if which == "q" and ts == 0 and n_ow_head > 0:
                                    v.wait_ge(p.sem(s_ow), n_ow_head)
                                ins = v.tensor_copy(dst[:, ts * 512:(ts + 1) * 512], bank[:])
                                p.mark(ins, "dve")
                    wql[buf] = p.cnt("prog_pe")

                    peb3 = p.cnt("prog_pe")
                    actb3 = p.cnt("prog_act")
                    dveb3 = p.cnt("prog_dve")
                    titers = [("q", qt_sb, qp_sb, 0, 1, 4, ts) for ts in range(NTT)] + \
                             [("k", kt_sb, kp_sb, 2, 3, 2, ts) for ts in range(NTT)]
                    def tbanks(k):
                        return (pb_p[0], pb_p[1]) if k % 2 == 0 else (pb_o[0], pb_o[1])
                    @block.tensor
                    def _(t):
                        for k, (which, src_t, dst, i1, i2, bscl, ts) in enumerate(titers):
                            bA, bB = tbanks(k)
                            if k == 0:
                                p.wait(t, "dve", dveb3)
                                p.wait(t, "act", actb3)
                                if hd == 0:
                                    t.wait_ge(p.sem(s_hw), n_hw0)
                                    t.wait_ge(p.sem("c_gf"), n_gf)
                            if k >= 2:
                                t.wait_ge(p.sem("prog_act"), actb3 + k - 1)
                                t.wait_ge(p.sem("prog_dve"), dveb3 + k - 1)
                            t.matmul(bA[:], fw_sb[:, i1, :],
                                     src_t[:, ts * 512:(ts + 1) * 512],
                                     start=True, stop=True, skip_group_check=True)
                            ins = t.matmul(bB[:], fw_sb[:, i2, :],
                                           src_t[:, ts * 512:(ts + 1) * 512],
                                           start=True, stop=True, skip_group_check=True)
                            p.mark(ins, "pe")
                    @block.scalar
                    def _(s):
                        for k, (which, src_t, dst, i1, i2, bscl, ts) in enumerate(titers):
                            bA, bB = tbanks(k)
                            s.wait_ge(p.sem("prog_pe"), peb3 + k + 1)
                            if k >= 1:
                                s.wait_ge(p.sem("prog_dve"), dveb3 + k)
                            if which == "q":
                                ins = s.activation(out=t1_sb[:], in_=bA[:],
                                                   func=AF.Identity,
                                                   bias=fb_sb[:, bscl:bscl + 1],
                                                   scale=INV_SQRT_DK)
                            else:
                                ins = s.activation(out=t1_sb[:], in_=bA[:],
                                                   func=AF.Identity,
                                                   bias=fb_sb[:, bscl:bscl + 1])
                            p.mark(ins, "act")
                    @block.vector
                    def _(v):
                        for k, (which, src_t, dst, i1, i2, bscl, ts) in enumerate(titers):
                            bA, bB = tbanks(k)
                            v.wait_ge(p.sem("prog_act"), actb3 + k + 1)
                            ins = v.scalar_tensor_tensor(
                                dst[:, ts * 512:(ts + 1) * 512], bB[:],
                                fb_sb[:, i2:i2 + 1], t1_sb[:], OP.add, OP.mult)
                            p.mark(ins, "dve")
                    dve_tf = p.cnt("prog_dve")
                    peb4 = p.cnt("prog_pe")
                    dveb4 = p.cnt("prog_dve")
                    txit = [("k", kp_sb, ktm_sb, tc) for tc in range(32)] + \
                           [("v", vt_sb, vtm_sb, tc) for tc in range(32)]
                    @block.tensor
                    def _(t):
                        for k, (which, src_t, dst, tc) in enumerate(txit):
                            if k == 0:
                                p.wait(t, "dve", dveb4)
                            if k >= 2:
                                t.wait_ge(p.sem("prog_dve"), dveb4 + k - 1)
                            ins = t.transpose(pb_tx[k % 2][:],
                                              src_t[:, tc * 128:(tc + 1) * 128],
                                              consts["ident_b"][:])
                            p.mark(ins, "pe")
                    @block.vector
                    def _(v):
                        for k, (which, src_t, dst, tc) in enumerate(txit):
                            v.wait_ge(p.sem("prog_pe"), peb4 + k + 1)
                            ins = v.tensor_copy(dst[:, tc, :], pb_tx[k % 2][:])
                            p.mark(ins, "dve")

                    n_ow_prev = p.cnt(s_ow)
                    dve_o_prev = {0: None, 1: None}
                    pe_upd_prev = None
                    for j in range(32):
                        ob = j % 2
                        q_sl = qp_sb[:, j * 128:(j + 1) * 128]
                        ktm_sl = ktm_sb[:, j, :]
                        vtm_sl = vtm_sb[:, j, :]
                        if j > 0:
                            @block.vector
                            def _(v):
                                p.wait(v, "pe", pe_upd_prev)
                                ins = v.tensor_copy(kv_sb[:], pb_st[:])
                                p.mark(ins, "dve")
                        dve_kv = p.cnt("prog_dve")
                        dve_ofree = dve_o_prev[ob]
                        @block.tensor
                        def _(t):
                            if j == 0:
                                if n_ow_prev > 0:
                                    t.wait_ge(p.sem(s_ow), n_ow_prev)
                                p.wait(t, "dve", dve_tf)
                            if j > 0:
                                p.wait(t, "dve", dve_kv)
                                t.matmul(pb_o[ob][:, 0:128], kv_sb[:], q_sl,
                                         start=True, stop=False, skip_group_check=True)
                            ins = t.matmul(pb_sT[0][:], kp_sb[:, j * 128:(j + 1) * 128],
                                           q_sl, start=True, stop=True,
                                           skip_group_check=True)
                            p.mark(ins, "pe")
                        pe_sT = p.cnt("prog_pe")
                        @block.vector
                        def _(v):
                            p.wait(v, "pe", pe_sT)
                            ins = v.tensor_tensor(sT_sb[:], pb_sT[0][:],
                                                  mask_c[:], OP.mult)
                            p.mark(ins, "dve")
                        dve_sT = p.cnt("prog_dve")
                        @block.tensor
                        def _(t):
                            p.wait(t, "dve", dve_sT)
                            if dve_ofree is not None:
                                p.wait(t, "dve", dve_ofree)
                            t.matmul(pb_o[ob][:, 0:128], vtm_sl, sT_sb[:],
                                     start=(j == 0), stop=True, skip_group_check=True)
                            ins = t.matmul(pb_st[:], ktm_sl, vtm_sl,
                                           start=(j == 0), stop=(j == 31),
                                           skip_group_check=True)
                            p.mark(ins, "pe")
                        pe_upd_prev = p.cnt("prog_pe")
                        @block.vector
                        def _(v):
                            p.wait(v, "pe", pe_upd_prev)
                            ins = v.tensor_copy(o_sb[:, j * 128:(j + 1) * 128],
                                                pb_o[ob][:, 0:128])
                            p.mark(ins, "dve")
                        dve_o_prev[ob] = p.cnt("prog_dve")
                    dve_o_all = p.cnt("prog_dve")
                    @block.sync
                    def _(sy):
                        p.wait(sy, "dve", dve_o_all)
                        dma(sy, p, s_ow,
                            ctx_hbm[hd * 128:(hd + 1) * 128, :], o_sb[:])

        stQ.close()
        n_ow0 = p.cnt(s_ow)
        pe_c3 = p.cnt("prog_pe"); dve_c3 = p.cnt("prog_dve"); act_c3 = p.cnt("prog_act")
        with contextlib.ExitStack() as stO:
            sbO = lambda n, sh, dt: stO.enter_context(nc.sbuf_tensor(n, sh, dt))
            wo_sb = sbO("c_wo", [128, ND, ND, 128], BF16)
            nw_sb = sbO("c_nw", [128, 1], F32)
            ot_sb = sbO("c_ot", [128, 2, ND, 512], BF16)
            sq_sb = sbO("c_sq", [128, 512], BF16)
            bc_sb = sbO("c_bc", [128, 512], F32)
            row_sb = sbO("c_row", [1, 512], F32)
            xres_sb = sbO("c_xres", [128, 4, 512], F32)
            t2_sb = sbO("c_t2", [128, 2, 512], F32)
            with contextlib.ExitStack() as stP:
                psO = lambda n, sh: stP.enter_context(nc.psum_tensor(n, sh, F32))
                pb_oo = [psO("c_poo0", [128, 512]), psO("c_poo1", [128, 512])]
                pb_row = psO("c_prow", [1, 512])
                pb_bc2 = psO("c_pbc2", [128, 512])
                @block.sync
                def _(sy):
                    p.wait(sy, "pe", pe_c3); p.wait(sy, "dve", dve_c3); p.wait(sy, "act", act_c3)
                    sy.wait_ge(p.sem(s_ow), n_ow0)
                    dma(sy, p, s_hw, nw_sb[:], w["norm_w"])
                n_nw = p.cnt(s_hw)
                @block.gpsimd
                def _(g):
                    p.wait(g, "pe", pe_c3); p.wait(g, "dve", dve_c3); p.wait(g, "act", act_c3)
                    g.wait_ge(p.sem(s_ow), n_ow0)
                    for ih in range(4):
                        dma(g, p, s_gw + "0", wo_sb[:, ih * 4:(ih + 1) * 4],
                            w["wo_blk"][ih * 4:(ih + 1) * 4].rearrange("i o p m -> p i o m"))
                n_wo = p.cnt(s_gw + "0")
                for ts in range(NTT):
                    cbuf = ts % 2
                    s_lc = s_ldc + str(cbuf)
                    pe_b = p.cnt("prog_pe")
                    act_b = p.cnt("prog_act")
                    @block.sync
                    def _(sy):
                        p.wait(sy, "pe", pe_b)
                        p.wait(sy, "act", act_b)
                        dma(sy, p, s_lc, ot_sb[:, cbuf],
                            ctx_hbm.rearrange("(c p) t -> p c t", p=128)[:, :, ts * 512:(ts + 1) * 512])
                    n_ldc = p.cnt(s_lc)
                    peb = p.cnt("prog_pe")
                    actb = p.cnt("prog_act")
                    dveb = p.cnt("prog_dve")
                    def pe_row(hh): return peb + 2 * hh + 1
                    def pe_bcm(hh): return peb + 2 * hh + 2
                    def act_sq(hh): return actb + 3 * hh + 1
                    def act_ex(hh): return actb + 3 * hh + 3
                    def dve_cp(hh): return dveb + 2 * hh + 1
                    @block.scalar
                    def _(s):
                        s.wait_ge(p.sem(s_lc), n_ldc)
                        if ts == 0:
                            s.wait_ge(p.sem(s_hw), n_nw)
                        for hh in range(ND):
                            if hh >= 1:
                                s.wait_ge(p.sem("prog_pe"), pe_row(hh - 1))
                            ins = s.activation(out=sq_sb[:], in_=ot_sb[:, cbuf, hh, :],
                                               func=AF.Square)
                            nn = p.mark(ins, "act")
                            s.wait_ge(p.sem("prog_pe"), pe_row(hh) - 1 + 1)
                            ins = s.activation(out=row_sb[:], in_=pb_row[:], func=AF.Ln,
                                               scale=1.0 / 128.0, bias=consts["eps_row"][:])
                            nn = p.mark(ins, "act")
                            s.wait_ge(p.sem("prog_act"), nn)
                            ins = s.activation(out=row_sb[:], in_=row_sb[:], func=AF.Exp,
                                               scale=-0.5)
                            p.mark(ins, "act")
                    @block.tensor
                    def _(t):
                        if ts == 0:
                            p.wait(t, "act", actb)
                            p.wait(t, "dve", dveb)
                        for hh in range(ND):
                            t.wait_ge(p.sem("prog_act"), act_sq(hh))
                            ins = t.matmul(pb_row[:], consts["ones_col_b"][:], sq_sb[:],
                                           start=True, stop=True)
                            p.mark(ins, "pe")
                            t.wait_ge(p.sem("prog_act"), act_ex(hh))
                            if hh >= 1:
                                t.wait_ge(p.sem("prog_dve"), dve_cp(hh - 1))
                            ins = t.matmul(pb_bc2[:], consts["ones_row_f"][:], row_sb[:],
                                           start=True, stop=True)
                            p.mark(ins, "pe")
                    @block.vector
                    def _(v):
                        for hh in range(ND):
                            v.wait_ge(p.sem("prog_pe"), pe_bcm(hh))
                            ins = v.tensor_copy(bc_sb[:], pb_bc2[:])
                            nn = p.mark(ins, "dve")
                            v.wait_ge(p.sem("prog_dve"), nn)
                            ins = v.scalar_tensor_tensor(ot_sb[:, cbuf, hh, :],
                                                         ot_sb[:, cbuf, hh, :],
                                                         nw_sb[:], bc_sb[:],
                                                         OP.mult, OP.mult)
                            nn = p.mark(ins, "dve")
                            v.wait_ge(p.sem("prog_dve"), nn)
                    dve_rms = p.cnt("prog_dve")
                    for o in range(ND):
                        dve_res = p.cnt("prog_dve")
                        s_lr = s_ldr + str(o % 4)
                        @block.scalar
                        def _(s):
                            p.wait(s, "dve", dve_res)
                            dma(s, p, s_lr, xres_sb[:, o % 4, :],
                                src_x[o * 128:(o + 1) * 128, ts * 512:(ts + 1) * 512])
                        n_r = p.cnt(s_lr)
                        bank = pb_oo[o % 2]
                        dve_free = p.cnt("prog_dve")
                        @block.tensor
                        def _(t):
                            t.wait_ge(p.sem(s_gw + "0"), n_wo)
                            p.wait(t, "dve", dve_free)
                            for c in range(ND):
                                ins = t.matmul(bank[:], wo_sb[:, c, o, :],
                                               ot_sb[:, cbuf, c, :],
                                               start=(c == 0), stop=(c == ND - 1))
                            p.mark(ins, "pe")
                        pe_g = p.cnt("prog_pe")
                        s_ou = s_out + str(o % 2)
                        n_out_prev = p.cnt(s_ou)
                        @block.vector
                        def _(v):
                            p.wait(v, "pe", pe_g)
                            v.wait_ge(p.sem(s_lr), n_r)
                            if n_out_prev > 0:
                                v.wait_ge(p.sem(s_ou), n_out_prev)
                            ins = v.tensor_add(t2_sb[:, o % 2, :], bank[:],
                                               xres_sb[:, o % 4, :])
                            p.mark(ins, "dve")
                        dve_g = p.cnt("prog_dve")
                        @block.sync
                        def _(sy):
                            p.wait(sy, "dve", dve_g)
                            dma(sy, p, s_ou,
                                dst_x[o * 128:(o + 1) * 128, ts * 512:(ts + 1) * 512],
                                t2_sb[:, o % 2, :])
    n_o0 = p.cnt(s_out + "0"); n_o1 = p.cnt(s_out + "1")
    @block.sync
    def _(sy):
        sy.wait_ge(p.sem(s_out + "0"), n_o0)
        sy.wait_ge(p.sem(s_out + "1"), n_o1)
        ins = sy.engine_nop() if hasattr(sy, "engine_nop") else sy.nop()
        h, n = p.inc("c_done", 1)
        ins.then_inc(h, 1)
    return "c_done", p.cnt("c_done")


def host_prep_c(inp):
    return {
        "wq_blk": blk(np.asarray(inp["L1_attn_wq"])),
        "wk_blk": blk(np.asarray(inp["L1_attn_wk"])),
        "wv_blk": blk(np.asarray(inp["L1_attn_wv"])),
        "wo_blk": blk(np.asarray(inp["L1_attn_wo"])),
        "fq1_wT": np.ascontiguousarray(np.asarray(inp["L1_attn_fq1_w"]).T),
        "fq2_wT": np.ascontiguousarray(np.asarray(inp["L1_attn_fq2_w"]).T),
        "fk1_wT": np.ascontiguousarray(np.asarray(inp["L1_attn_fk1_w"]).T),
        "fk2_wT": np.ascontiguousarray(np.asarray(inp["L1_attn_fk2_w"]).T),
        "fq1_b": np.asarray(inp["L1_attn_fq1_b"]).reshape(128, 1).copy(),
        "fq2_b": np.asarray(inp["L1_attn_fq2_b"]).reshape(128, 1).copy(),
        "fk1_b": np.asarray(inp["L1_attn_fk1_b"]).reshape(128, 1).copy(),
        "fk2_b": np.asarray(inp["L1_attn_fk2_b"]).reshape(128, 1).copy(),
        "norm_w": np.asarray(inp["L1_attn_norm_w"]).reshape(128, 1).copy(),
    }

C_W_SHAPES = {
    "wq_blk": ([ND, ND, 128, 128], F32),
    "wk_blk": ([ND, ND, 128, 128], F32),
    "wv_blk": ([ND, ND, 128, 128], F32),
    "wo_blk": ([ND, ND, 128, 128], F32),
    "fq1_wT": ([128, 128], F32),
    "fq2_wT": ([128, 128], F32),
    "fk1_wT": ([128, 128], F32),
    "fk2_wT": ([128, 128], F32),
    "fq1_b": ([128, 1], F32),
    "fq2_b": ([128, 1], F32),
    "fk1_b": ([128, 1], F32),
    "fk2_b": ([128, 1], F32),
    "norm_w": ([128, 1], F32),
}


def build_c_test():
    nc = bass.Bass("TRN2", target_bir_lowering=False)
    stack = contextlib.ExitStack()
    xin = nc.dram_tensor("xin", [D, T], F32, kind="ExternalInput").ap()
    xout = nc.dram_tensor("xout", [D, T], F32, kind="ExternalOutput").ap()
    ctx_hbm = nc.dram_tensor("ctx_hbm", [D, T], BF16, kind="Internal").ap()
    cin = {k: nc.dram_tensor(k, sh, dt, kind="ExternalInput").ap()
           for k, (sh, dt) in CONST_SHAPES.items()}
    wc = {k: nc.dram_tensor(k, sh, dt, kind="ExternalInput").ap()
          for k, (sh, dt) in C_W_SHAPES.items()}
    wc["c_mask_c"] = cin["c_mask_c"]
    with stack:
        with nc.Block() as block:
            p = Prog(nc, block, stack)
            consts = emit_consts(p, cin)
            s_done, n_done = emit_phase_c(p, consts, xin, xout, ctx_hbm, wc)
            @block.sync
            def _(sy):
                sy.wait_ge(p.sem(s_done), n_done)
    return nc



def emit_phase_d(p, consts, src_x, dst_x, moe_acc, w, s_prev=None, n_prev=0):
    nc, block = p.nc, p.block
    s_ldx, s_hw, s_out, s_ldr, s_lda = "d_ldx", "d_hw", "d_out", "d_ldr", "d_lda"

    def vsync(v, ins):
        nn = p.mark(ins, "dve")
        v.wait_ge(p.sem("prog_dve"), nn)
        return nn

    with contextlib.ExitStack() as stD:
        sbD = lambda n, sh, dt: stD.enter_context(nc.sbuf_tensor(n, sh, dt))
        h_sb = sbD("d_h", [128, ND, T], BF16)
        w_sb = sbD("d_w", [128, 32, 12], F32)
        s1_sb = sbD("d_s1", [128, 32], F32)

        with contextlib.ExitStack() as stA:
            sbA = lambda n, sh, dt: stA.enter_context(nc.sbuf_tensor(n, sh, dt))
            x_sb = sbA("d_x", [128, ND, 512], F32)
            clf_sb = sbA("d_clf", [128, ND, 12], F32)
            se_sb = sbA("d_se", [12, 1], F32)
            eb_sb = sbA("d_eb", [128, 12], F32)
            row12 = sbA("d_r12", [1, 16], F32)
            mb12 = sbA("d_mb12", [12, 512], F32)
            lg_sb = sbA("d_lg", [12, 512], F32)
            rt_sb = sbA("d_rt", [128, 12], F32)
            pr_sb = sbA("d_pr", [128, 12], F32)
            sc1 = sbA("d_sc1", [128, 12], F32)
            sc2 = sbA("d_sc2", [128, 12], F32)
            rr_sb = sbA("d_rr2", [128, 4], F32)
            scr = {
                "tmp": sbA("d_tmp", [128, 512], F32),
            "tmpb": sbA("d_tmpb", [128, 2, 512], BF16),
                "row_m": sbA("d_rm", [1, 512], F32),
                "row_v": sbA("d_rv", [1, 512], F32),
                "mb": sbA("d_mb", [128, 512], F32),
            }
            with contextlib.ExitStack() as stP:
                psA = lambda n, sh: stP.enter_context(nc.psum_tensor(n, sh, F32))
                pb_row0 = psA("d_pr0", [1, 512])
                pb_row1 = psA("d_pr1", [1, 512])
                pb_bc = psA("d_pbc", [128, 512])
                pb_lg = psA("d_plg", [12, 512])
                pb_t = psA("d_pt", [128, 12])

                @block.sync
                def _(sy):
                    if s_prev is not None:
                        sy.wait_ge(p.sem(s_prev), n_prev)
                    dma(sy, p, s_hw, clf_sb[:], w["clfT"].rearrange("(c p) e -> p c e", p=128))
                    dma(sy, p, s_hw, eb_sb[0:1, :], w["ebias_row"])
                n_hw0 = p.cnt(s_hw)
                @block.tensor
                def _(t):
                    t.wait_ge(p.sem(s_hw), n_hw0)
                    p.wait(t, "act")
                    p.wait(t, "dve")
                    for c in range(ND):
                        t.matmul(pb_t[0:12, 0:1], clf_sb[:, c, :],
                                 consts["ones_col_f"][:], start=(c == 0),
                                 stop=(c == ND - 1), skip_group_check=True)
                    ins = t.matmul(pb_bc[:, 0:12], consts["ones_row_f"][:], eb_sb[0:1, :],
                                   start=True, stop=True, skip_group_check=True)
                    p.mark(ins, "pe")
                pe_eb = p.cnt("prog_pe")
                @block.vector
                def _(v):
                    p.wait(v, "pe", pe_eb)
                    v.tensor_copy(se_sb[:], pb_t[0:12, 0:1])
                    ins = v.tensor_copy(eb_sb[:], pb_bc[:, 0:12])
                    vsync(v, ins)
                dve_pre = p.cnt("prog_dve")

                for ts in range(NTT):
                    c0 = ts * 512
                    dve_b = p.cnt("prog_dve")
                    pe_b = p.cnt("prog_pe")
                    @block.sync
                    def _(sy):
                        p.wait(sy, "dve", dve_b)
                        p.wait(sy, "pe", pe_b)
                        dma(sy, p, s_ldx, x_sb[:],
                            src_x.rearrange("(c p) t -> p c t", p=128)[:, :, c0:c0 + 512])
                    n_ldx = p.cnt(s_ldx)
                    @block.tensor
                    def _(t):
                        t.wait_ge(p.sem(s_ldx), n_ldx)
                        p.wait(t, "act")
                        p.wait(t, "dve")
                    @block.scalar
                    def _(s):
                        s.wait_ge(p.sem(s_ldx), n_ldx)
                    @block.vector
                    def _(v):
                        v.wait_ge(p.sem(s_ldx), n_ldx)
                    hv = lambda c: h_sb[:, c, c0:c0 + 512]
                    dve_h = emit_ln_tile(p, consts, x_sb, hv, pb_row0, pb_row1, pb_bc, scr)

                    @block.tensor
                    def _(t):
                        p.wait(t, "dve", dve_h)
                        for c in range(ND):
                            ins = t.matmul(pb_lg[:], clf_sb[:, c, :], x_sb[:, c, :],
                                           start=(c == 0), stop=(c == ND - 1))
                        ins = t.matmul(pb_t[:, 0:12] if False else pb_bc[:, 0:512],
                                       consts["ones_row_f"][:], scr["row_m"][:],
                                       start=True, stop=True)
                        p.mark(ins, "pe")
                    pe_lg = p.cnt("prog_pe")
                    @block.vector
                    def _(v):
                        p.wait(v, "pe", pe_lg)
                        ins = v.tensor_copy(mb12[:], pb_bc[0:12, 0:512])
                        vsync(v, ins)
                        ins = v.tensor_scalar(mb12[:], mb12[:], se_sb[0:12, 0:1], None,
                                              OP.mult)
                        vsync(v, ins)
                        ins = v.tensor_sub(lg_sb[:], pb_lg[:], mb12[:])
                        vsync(v, ins)
                        p.mark(v.engine_nop(), "dve")
                    dve_lgc = p.cnt("prog_dve")
                    @block.tensor
                    def _(t):
                        p.wait(t, "dve", dve_lgc)
                        ins = t.matmul(pb_bc[:, 0:512], consts["ones_row_f"][:],
                                       scr["row_v"][:], start=True, stop=True)
                        p.mark(ins, "pe")
                    pe_rb = p.cnt("prog_pe")
                    @block.vector
                    def _(v):
                        p.wait(v, "pe", pe_rb)
                        ins = v.tensor_tensor(lg_sb[:], lg_sb[:], pb_bc[0:12, 0:512],
                                              OP.mult)
                        vsync(v, ins)
                    dve_lg = p.cnt("prog_dve")

                    for sub in range(4):
                        slot = ts * 4 + sub
                        dve_now = p.cnt("prog_dve")
                        @block.tensor
                        def _(t):
                            p.wait(t, "dve", max(dve_lg, dve_now))
                            ins = t.transpose(pb_t[:], lg_sb[:, sub * 128:(sub + 1) * 128],
                                              consts["ident_f"][0:12, 0:12])
                            p.mark(ins, "pe")
                        pe_tr = p.cnt("prog_pe")
                        @block.vector
                        def _(v):
                            p.wait(v, "pe", pe_tr)
                            ins = v.tensor_copy(rt_sb[:], pb_t[:])
                            vsync(v, ins)
                            ins = v.reduce_max(rr_sb[:, 0:1], rt_sb[:], axis=AX.X)
                            vsync(v, ins)
                            ins = v.tensor_scalar(rr_sb[:, 1:2], rr_sb[:, 0:1], -1.0,
                                                  None, OP.mult)
                            vsync(v, ins)
                        dve_mx = p.cnt("prog_dve")
                        @block.scalar
                        def _(s):
                            p.wait(s, "dve", dve_mx)
                            ins = s.activation(out=pr_sb[:], in_=rt_sb[:], func=AF.Exp,
                                               bias=rr_sb[:, 1:2])
                            p.mark(ins, "act")
                        act_ex = p.cnt("prog_act")
                        @block.vector
                        def _(v):
                            p.wait(v, "act", act_ex)
                            ins = v.reduce_sum(rr_sb[:, 2:3], pr_sb[:], axis=AX.X)
                            vsync(v, ins)
                            ins = v.reciprocal(rr_sb[:, 3:4], rr_sb[:, 2:3])
                            vsync(v, ins)
                            ins = v.tensor_scalar(pr_sb[:], pr_sb[:], rr_sb[:, 3:4],
                                                  None, OP.mult)
                            vsync(v, ins)
                            ins = v.tensor_add(sc1[:], pr_sb[:], eb_sb[:])
                            vsync(v, ins)
                            ins = v.reduce_max(rr_sb[:, 0:1], sc1[:], axis=AX.X)
                            vsync(v, ins)
                            ins = v.tensor_scalar(sc2[:], sc1[:], rr_sb[:, 0:1], None,
                                                  OP.is_ge)
                            vsync(v, ins)
                            ins = v.scalar_tensor_tensor(sc1[:], sc2[:], -1e30, sc1[:],
                                                         OP.mult, OP.add)
                            vsync(v, ins)
                            ins = v.reduce_max(rr_sb[:, 1:2], sc1[:], axis=AX.X)
                            vsync(v, ins)
                            ins = v.tensor_scalar(sc1[:], sc1[:], rr_sb[:, 1:2], None,
                                                  OP.is_ge)
                            vsync(v, ins)
                            ins = v.tensor_add(sc2[:], sc2[:], sc1[:])
                            vsync(v, ins)
                            ins = v.scalar_tensor_tensor(w_sb[:, slot, :], sc2[:], 1.5,
                                                         pr_sb[:], OP.mult, OP.mult)
                            vsync(v, ins)
                            ins = v.reduce_sum(s1_sb[:, slot:slot + 1],
                                               w_sb[:, slot, 8:12], axis=AX.X)
                            vsync(v, ins)
                            ins = v.tensor_scalar(s1_sb[:, slot:slot + 1],
                                                  s1_sb[:, slot:slot + 1], 1.0, None,
                                                  OP.add)
                            vsync(v, ins)

        pe_d0 = p.cnt("prog_pe"); dve_d0 = p.cnt("prog_dve"); act_d0 = p.cnt("prog_act")
        with contextlib.ExitStack() as stE:
            sbE = lambda n, sh, dt: stE.enter_context(nc.sbuf_tensor(n, sh, dt))
            wg_sb = sbE("d_wg", [128, 2, ND, 128], BF16)
            wu_sb = sbE("d_wu", [128, 2, ND, 128], BF16)
            wd_sb = sbE("d_wd", [128, 2, 8, 128], BF16)
            act_sb = sbE("d_act", [128, 8, T // 2], BF16)
            wbc_sb = sbE("d_wbc", [128, NTT, 512], BF16)
            wrow_sb = sbE("d_wrow", [1, 512], F32)
            silu_sb = sbE("d_silu", [128, 512], F32)
            t3_sb = silu_sb
            dout_sb = sbE("d_dout", [128, 6, 512], F32)
            with contextlib.ExitStack() as stP:
                psE = lambda n, sh: stP.enter_context(nc.psum_tensor(n, sh, F32))
                pb_g = [psE("d_pg0", [128, 512]), psE("d_pg1", [128, 512])]
                pb_u = [psE("d_pu0", [128, 512]), psE("d_pu1", [128, 512])]
                pb_d = [psE("d_pd0", [128, 512]), psE("d_pd1", [128, 512])]
                pb_row = psE("d_prow", [1, 512])
                pb_bc2 = psE("d_pbc2", [128, 512])

                n_acc_prev = {i: 0 for i in range(8)}
                acc_done = {}
                pref_n_wg = {}; pref_n_wu = {}
                acc_cnt = {i: 0 for i in range(8)}
                gul = {0: 0, 1: 0}
                dl = {0: 0, 1: 0}
                for e in range(8):
                    for ts in range(NTT):
                        dve_f = p.cnt("prog_dve")
                        @block.tensor
                        def _(t):
                            if e == 0 and ts == 0:
                                p.wait(t, "pe", pe_d0)
                                p.wait(t, "act", act_d0)
                            p.wait(t, "dve", dve_f)
                            for sub in range(4):
                                ins = t.matmul(pb_row[0:1, sub * 128:(sub + 1) * 128],
                                               w_sb[:, ts * 4 + sub, e:e + 1],
                                               consts["ident_f"][:],
                                               start=True, stop=True,
                                               skip_group_check=True)
                            p.mark(ins, "pe")
                        pe_r = p.cnt("prog_pe")
                        @block.vector
                        def _(v):
                            p.wait(v, "pe", pe_r)
                            ins = v.tensor_copy(wrow_sb[:], pb_row[:])
                            nn = p.mark(ins, "dve")
                            v.wait_ge(p.sem("prog_dve"), nn)
                        dve_r = p.cnt("prog_dve")
                        @block.tensor
                        def _(t):
                            p.wait(t, "dve", dve_r)
                            ins = t.matmul(pb_bc2[:], consts["ones_row_f"][:], wrow_sb[:],
                                           start=True, stop=True)
                            p.mark(ins, "pe")
                        pe_b2 = p.cnt("prog_pe")
                        @block.vector
                        def _(v):
                            p.wait(v, "pe", pe_b2)
                            ins = v.tensor_copy(wbc_sb[:, ts, :], pb_bc2[:])
                            p.mark(ins, "dve")
                    for th in range(2):
                        peb = p.cnt("prog_pe")
                        actb = p.cnt("prog_act")
                        dveb = p.cnt("prog_dve")
                        n_wg = dict(pref_n_wg); n_wu = dict(pref_n_wu)
                        pref_n_wg.clear(); pref_n_wu.clear()
                        for f in range(8):
                            if f in n_wg:
                                continue
                            buf = f % 2
                            s_gg = "d_gg" + str(buf); s_gu = "d_gu" + str(buf)
                            @block.gpsimd
                            def _(g):
                                if e == 0 and th == 0 and f < 2:
                                    p.wait(g, "pe", pe_d0); p.wait(g, "dve", dve_d0)
                                    p.wait(g, "act", act_d0)
                                if f >= 2:
                                    g.wait_ge(p.sem("prog_pe"), peb + 4 * (f - 1))
                                elif gul[buf] > 0:
                                    p.wait(g, "pe", gul[buf])
                                dma(g, p, s_gg, wg_sb[:, buf],
                                    w["gate_blk"][e][:, f].rearrange("i p m -> p i m"))
                                dma(g, p, s_gu, wu_sb[:, buf],
                                    w["up_blk"][e][:, f].rearrange("i p m -> p i m"))
                            n_wg[f] = p.cnt(s_gg); n_wu[f] = p.cnt(s_gu)
                        def pe_gu(f, tsr):
                            return peb + 4 * f + tsr + 1
                        def act_si(f, tsr):
                            return actb + 4 * f + tsr + 1
                        def dve_aw(f, tsr):
                            return dveb + 2 * (4 * f + tsr) + 2
                        @block.tensor
                        def _(t):
                            for f in range(8):
                                buf = f % 2
                                t.wait_ge(p.sem("d_gg" + str(buf)), n_wg[f])
                                t.wait_ge(p.sem("d_gu" + str(buf)), n_wu[f])
                                if f == 0 and th == 0:
                                    p.wait(t, "act")
                                    p.wait(t, "dve")
                                for tsr in range(4):
                                    ts = th * 4 + tsr
                                    pf, pt = (f, tsr - 2) if tsr >= 2 else (f - 1, tsr + 2)
                                    if pf >= 0:
                                        t.wait_ge(p.sem("prog_act"), act_si(pf, pt))
                                        t.wait_ge(p.sem("prog_dve"), dve_aw(pf, pt))
                                    elif th > 0 or e > 0:
                                        p.wait(t, "act")
                                        p.wait(t, "dve")
                                    pg = pb_g[tsr % 2]; pu = pb_u[tsr % 2]
                                    for c in range(ND):
                                        t.matmul(pg[:], wg_sb[:, buf, c, :],
                                                 h_sb[:, c, ts * 512:(ts + 1) * 512],
                                                 start=(c == 0), stop=(c == ND - 1))
                                    for c in range(ND):
                                        ins = t.matmul(pu[:], wu_sb[:, buf, c, :],
                                                       h_sb[:, c, ts * 512:(ts + 1) * 512],
                                                       start=(c == 0), stop=(c == ND - 1))
                                    p.mark(ins, "pe")
                        gul[0] = peb + 4 * 6 + 4
                        gul[1] = peb + 4 * 7 + 4
                        @block.scalar
                        def _(s):
                            for f in range(8):
                                for tsr in range(4):
                                    k = 4 * f + tsr
                                    s.wait_ge(p.sem("prog_pe"), pe_gu(f, tsr))
                                    if k >= 1:
                                        s.wait_ge(p.sem("prog_dve"), dveb + 2 * (k - 1) + 2)
                                    ins = s.activation(out=silu_sb[:], in_=pb_g[tsr % 2][:],
                                                       func=AF.Silu)
                                    p.mark(ins, "act")
                        @block.vector
                        def _(v):
                            for f in range(8):
                                for tsr in range(4):
                                    ts = th * 4 + tsr
                                    v.wait_ge(p.sem("prog_act"), act_si(f, tsr))
                                    ins = v.tensor_tensor(t3_sb[:], silu_sb[:],
                                                          pb_u[tsr % 2][:], OP.mult)
                                    nn = p.mark(ins, "dve")
                                    v.wait_ge(p.sem("prog_dve"), nn)
                                    ins = v.tensor_tensor(
                                        act_sb[:, f, tsr * 512:(tsr + 1) * 512],
                                        t3_sb[:], wbc_sb[:, ts, :], OP.mult)
                                    nn = p.mark(ins, "dve")
                                    v.wait_ge(p.sem("prog_dve"), nn)
                        peb2 = p.cnt("prog_pe")
                        dveb2 = p.cnt("prog_dve")
                        n_wd = {}
                        def emit_accums(g, o):
                            for tsr in range(4):
                                ts = th * 4 + tsr
                                ring = (4 * o + tsr) % 6
                                g.wait_ge(p.sem("prog_dve"), dveb2 + 4 * o + tsr + 1)
                                dma(g, p, s_lda + str(ring),
                                    moe_acc[o * 128:(o + 1) * 128,
                                            ts * 512:(ts + 1) * 512],
                                    dout_sb[:, ring, :],
                                    **({} if e == 0 else {"accum_op": OP.add}))
                                if e == 7:
                                    acc_done[(o, ts)] = (s_lda + str(ring),
                                                        p.cnt(s_lda + str(ring)))
                        ne, nth = (e, 1) if th == 0 else (e + 1, 0)
                        for o in range(ND):
                            buf = o % 2
                            s_gd = "d_gd" + str(buf)
                            @block.gpsimd
                            def _(g):
                                if e > 0 and th == 0 and o == 1:
                                    for i in range(8):
                                        g.wait_ge(p.sem(s_lda + str(i)), n_acc_prev[i])
                                if o >= 1:
                                    emit_accums(g, o - 1)
                                if o >= 2:
                                    g.wait_ge(p.sem("prog_pe"), peb2 + 4 * (o - 1))
                                elif dl[buf] > 0:
                                    p.wait(g, "pe", dl[buf])
                                dma(g, p, s_gd, wd_sb[:, buf],
                                    w["down_blk"][e][:, o].rearrange("i p m -> p i m"))
                                if o in (2, 3) and ne < 8:
                                    fb = o - 2
                                    g.wait_ge(p.sem("prog_pe"), gul[fb])
                                    dma(g, p, "d_gg" + str(fb), wg_sb[:, fb],
                                        w["gate_blk"][ne][:, fb].rearrange("i p m -> p i m"))
                                    dma(g, p, "d_gu" + str(fb), wu_sb[:, fb],
                                        w["up_blk"][ne][:, fb].rearrange("i p m -> p i m"))
                                    pref_n_wg[fb] = p.cnt("d_gg" + str(fb))
                                    pref_n_wu[fb] = p.cnt("d_gu" + str(fb))
                            n_wd[o] = p.cnt(s_gd)
                        @block.gpsimd
                        def _(g):
                            emit_accums(g, ND - 1)
                        def pe_dn(o, tsr):
                            return peb2 + 4 * o + tsr + 1
                        @block.tensor
                        def _(t):
                            for o in range(ND):
                                buf = o % 2
                                t.wait_ge(p.sem("d_gd" + str(buf)), n_wd[o])
                                if o == 0:
                                    p.wait(t, "dve")
                                for tsr in range(4):
                                    pf, pt = (o, tsr - 2) if tsr >= 2 else (o - 1, tsr + 2)
                                    if pf >= 0:
                                        t.wait_ge(p.sem("prog_dve"),
                                                  dveb2 + 4 * pf + pt + 1)
                                    pd = pb_d[tsr % 2]
                                    for f in range(8):
                                        ins = t.matmul(pd[:], wd_sb[:, buf, f, :],
                                                       act_sb[:, f, tsr * 512:(tsr + 1) * 512],
                                                       start=(f == 0), stop=(f == 7))
                                    p.mark(ins, "pe")
                        dl[0] = peb2 + 4 * 14 + 4
                        dl[1] = peb2 + 4 * 15 + 4
                        @block.vector
                        def _(v):
                            for o in range(ND):
                                for tsr in range(4):
                                    ring = (4 * o + tsr) % 6
                                    s_ac = s_lda + str(ring)
                                    v.wait_ge(p.sem("prog_pe"), pe_dn(o, tsr))
                                    na = acc_cnt[ring]
                                    if na > 0:
                                        v.wait_ge(p.sem(s_ac), na)
                                    ins = v.tensor_copy(dout_sb[:, ring, :], pb_d[tsr % 2][:])
                                    p.mark(ins, "dve")
                                    acc_cnt[ring] += 16
                    for i in range(8):
                        n_acc_prev[i] = p.cnt(s_lda + str(i))
                    for i in range(8):
                        n_acc_prev[i] = p.cnt(s_lda + str(i))

        pe_d1 = p.cnt("prog_pe"); dve_d1 = p.cnt("prog_dve"); act_d1 = p.cnt("prog_act")
        n_acc_fin = {i: p.cnt(s_lda + str(i)) for i in range(6)}
        with contextlib.ExitStack() as stF:
            sbF = lambda n, sh, dt: stF.enter_context(nc.sbuf_tensor(n, sh, dt))
            ma_sb = sbF("d_ma", [128, 6, 512], F32)
            xres_sb = sbF("d_xres", [128, 6, 512], F32)
            s1b_sb = sbF("d_s1b", [128, 8, 512], F32)
            srow_sb = sbF("d_srow", [1, 512], F32)
            t2_sb = sbF("d_t2", [128, 4, 512], F32)
            with contextlib.ExitStack() as stP:
                psF = lambda n, sh: stP.enter_context(nc.psum_tensor(n, sh, F32))
                pb_row2 = psF("d_prow2", [1, 512])
                pb_bc3 = psF("d_pbc3", [128, 512])
                ma_guard = {i: 0 for i in range(6)}
                peb5 = p.cnt("prog_pe")
                dveb5 = p.cnt("prog_dve")
                @block.tensor
                def _(t):
                    p.wait(t, "pe", pe_d1)
                    p.wait(t, "act", act_d1)
                    p.wait(t, "dve", dve_d1)
                    for ts in range(NTT):
                        if ts >= 1:
                            t.wait_ge(p.sem("prog_dve"), dveb5 + 2 * (ts - 1) + 1)
                        for sub in range(4):
                            ins = t.matmul(pb_row2[0:1, sub * 128:(sub + 1) * 128],
                                           s1_sb[:, ts * 4 + sub:ts * 4 + sub + 1],
                                           consts["ident_f"][:], start=True, stop=True,
                                           skip_group_check=True)
                        p.mark(ins, "pe")
                        t.wait_ge(p.sem("prog_dve"), dveb5 + 2 * ts + 1)
                        ins = t.matmul(pb_bc3[:], consts["ones_row_f"][:], srow_sb[:],
                                       start=True, stop=True)
                        p.mark(ins, "pe")
                @block.vector
                def _(v):
                    for ts in range(NTT):
                        v.wait_ge(p.sem("prog_pe"), peb5 + 2 * ts + 1)
                        ins = v.tensor_copy(srow_sb[:], pb_row2[:])
                        nn = p.mark(ins, "dve")
                        v.wait_ge(p.sem("prog_dve"), nn)
                        v.wait_ge(p.sem("prog_pe"), peb5 + 2 * ts + 2)
                        ins = v.tensor_copy(s1b_sb[:, ts, :], pb_bc3[:])
                        nn = p.mark(ins, "dve")
                        v.wait_ge(p.sem("prog_dve"), nn)
                dve_s1b = p.cnt("prog_dve")
                for ts in range(NTT):
                    for o in range(ND):
                        ring = (ts * ND + o) % 6
                        s_ma = "d_ma" + str(ring)
                        s_lr = s_ldr + str(ring)
                        g = ma_guard[ring]
                        @block.sync
                        def _(sy):
                            if ts == 0 and o == 0:
                                for i in range(6):
                                    sy.wait_ge(p.sem(s_lda + str(i)), n_acc_fin[i])
                                p.wait(sy, "pe", pe_d1)
                                p.wait(sy, "dve", dve_d1)
                                p.wait(sy, "act", act_d1)
                            if g > 0:
                                sy.wait_ge(p.sem("prog_dve"), g)
                            dma(sy, p, s_ma, ma_sb[:, ring, :],
                                moe_acc[o * 128:(o + 1) * 128, ts * 512:(ts + 1) * 512])
                        n_ma = p.cnt(s_ma)
                        @block.gpsimd
                        def _(gg):
                            if g > 0:
                                gg.wait_ge(p.sem("prog_dve"), g)
                            dma(gg, p, s_lr, xres_sb[:, ring, :],
                                src_x[o * 128:(o + 1) * 128, ts * 512:(ts + 1) * 512])
                        n_r = p.cnt(s_lr)
                        s_ou = s_out + str(o % 4)
                        n_out_prev = p.cnt(s_ou)
                        @block.vector
                        def _(v):
                            v.wait_ge(p.sem(s_ma), n_ma)
                            v.wait_ge(p.sem(s_lr), n_r)
                            p.wait(v, "dve", dve_s1b)
                            if n_out_prev > 0:
                                v.wait_ge(p.sem(s_ou), n_out_prev)
                            ins = v.tensor_tensor(t2_sb[:, o % 4, :], xres_sb[:, ring, :],
                                                  s1b_sb[:, ts, :], OP.mult)
                            nn = p.mark(ins, "dve")
                            v.wait_ge(p.sem("prog_dve"), nn)
                            ins = v.tensor_add(t2_sb[:, o % 4, :], t2_sb[:, o % 4, :],
                                               ma_sb[:, ring, :])
                            p.mark(ins, "dve")
                        dve_g = p.cnt("prog_dve")
                        ma_guard[ring] = dve_g
                        @block.scalar
                        def _(sc):
                            p.wait(sc, "dve", dve_g)
                            dma(sc, p, s_ou,
                                dst_x[o * 128:(o + 1) * 128, ts * 512:(ts + 1) * 512],
                                t2_sb[:, o % 4, :])
    n_os = {i: p.cnt(s_out + str(i)) for i in range(4)}
    @block.sync
    def _(sy):
        for i in range(4):
            sy.wait_ge(p.sem(s_out + str(i)), n_os[i])
        ins = sy.engine_nop() if hasattr(sy, "engine_nop") else sy.nop()
        h, n = p.inc("d_done", 1)
        ins.then_inc(h, 1)
    return "d_done", p.cnt("d_done")


def host_prep_d(inp):
    gate = np.asarray(inp["L1_ffn_gate_w"])
    up = np.asarray(inp["L1_ffn_up_w"])
    down = np.asarray(inp["L1_ffn_down_w"])
    return {
        "clfT": np.ascontiguousarray(np.asarray(inp["L1_ffn_classifier_w"]).T),
        "ebias_row": np.asarray(inp["L1_ffn_e_score_correction_bias"]).reshape(1, 12).copy(),
        "gate_blk": np.stack([blk(gate[e]) for e in range(8)]),
        "up_blk": np.stack([blk(up[e]) for e in range(8)]),
        "down_blk": np.stack([blk(down[e]) for e in range(8)]),
    }

D_W_SHAPES = {
    "clfT": ([D, 12], F32),
    "ebias_row": ([1, 12], F32),
    "gate_blk": ([8, ND, 8, 128, 128], F32),
    "up_blk": ([8, ND, 8, 128, 128], F32),
    "down_blk": ([8, 8, ND, 128, 128], F32),
}


def build_d_test():
    nc = bass.Bass("TRN2", target_bir_lowering=False)
    stack = contextlib.ExitStack()
    xin = nc.dram_tensor("xin", [D, T], F32, kind="ExternalInput").ap()
    xout = nc.dram_tensor("xout", [D, T], F32, kind="ExternalOutput").ap()
    moe_acc = nc.dram_tensor("moe_acc", [D, T], F32, kind="Internal").ap()
    cin = {k: nc.dram_tensor(k, sh, dt, kind="ExternalInput").ap()
           for k, (sh, dt) in CONST_SHAPES.items()}
    wd = {k: nc.dram_tensor(k, sh, dt, kind="ExternalInput").ap()
          for k, (sh, dt) in D_W_SHAPES.items()}
    with stack:
        with nc.Block() as block:
            p = Prog(nc, block, stack)
            consts = emit_consts(p, cin)
            s_done, n_done = emit_phase_d(p, consts, xin, xout, moe_acc, wd)
            @block.sync
            def _(sy):
                sy.wait_ge(p.sem(s_done), n_done)
    return nc



def emit_consts(p, cin):
    block = p.block
    consts = {
        "ones_col_f": p.sb("k_ones_cf", [128, 1], F32),
        "ones_col_b": p.sb("k_ones_cb", [128, 1], BF16),
        "ones_row_f": p.sb("k_ones_rf", [1, 128], F32),
        "ident_f": p.sb("k_id_f", [128, 128], F32),
        "ident_b": p.sb("k_id_b", [128, 128], BF16),
        "eps_row": p.sb("k_eps", [1, 1], F32),
    }
    @block.sync
    def _(sy):
        dma(sy, p, "k_const", consts["ident_f"][:], cin["c_ident_f"])
        dma(sy, p, "k_const", consts["ident_b"][:], cin["c_ident_b"])
    @block.gpsimd
    def _(g):
        g.memset(consts["eps_row"][:], 1e-5)
        g.memset(consts["ones_col_f"][:], 1.0)
        g.memset(consts["ones_col_b"][:], 1.0)
        ins = g.memset(consts["ones_row_f"][:], 1.0)
        p.mark(ins, "pool")
    n_c = p.cnt("k_const")
    n_p = p.cnt("prog_pool")
    for name in ("tensor", "scalar", "vector", "sync", "gpsimd"):
        sec = getattr(block, name)
        @sec
        def _(e):
            e.wait_ge(p.sem("k_const"), n_c)
            e.wait_ge(p.sem("prog_pool"), n_p)
    return consts



def host_prep_b(inp):
    import ml_dtypes
    return {
        "WgT": np.ascontiguousarray(np.asarray(inp["L0_ffn_Wg"]).T),
        "W0a_blk": blk(np.asarray(inp["L0_ffn_W0a"])),
        "W0b_blk": blk(np.asarray(inp["L0_ffn_W0b"])),
        "W1a_blk": blk(np.asarray(inp["L0_ffn_W1a"])),
        "W1b_blk": blk(np.asarray(inp["L0_ffn_W1b"])),
        "b0a_T": vecT(np.asarray(inp["L0_ffn_b0a"])),
        "b1a_T": vecT(np.asarray(inp["L0_ffn_b1a"])),
        "b0b_T": vecT(np.asarray(inp["L0_ffn_b0b"])),
        "b1b_T": vecT(np.asarray(inp["L0_ffn_b1b"])),
    }

B_W_SHAPES = {
    "WgT": ([D, 2], F32),
    "W0a_blk": ([ND, 32, 128, 128], F32),
    "W0b_blk": ([32, ND, 128, 128], F32),
    "W1a_blk": ([ND, 32, 128, 128], F32),
    "W1b_blk": ([32, ND, 128, 128], F32),
    "b0a_T": ([128, 32], F32),
    "b1a_T": ([128, 32], F32),
    "b0b_T": ([128, 16], F32),
    "b1b_T": ([128, 16], F32),
}

CONST_SHAPES = {
    "c_ident_f": ([128, 128], F32),
    "c_ident_b": ([128, 128], BF16),
    "c_mask_a": ([4, 128, 512], BF16),
    "c_mask_c": ([128, 128], BF16),
}


def build_b_test():
    nc = bass.Bass("TRN2", target_bir_lowering=False)
    stack = contextlib.ExitStack()
    xin = nc.dram_tensor("xin", [D, T], F32, kind="ExternalInput").ap()
    xout = nc.dram_tensor("xout", [D, T], F32, kind="ExternalOutput").ap()
    cin = {k: nc.dram_tensor(k, sh, dt, kind="ExternalInput").ap()
           for k, (sh, dt) in CONST_SHAPES.items()}
    wb = {k: nc.dram_tensor(k, sh, dt, kind="ExternalInput").ap()
          for k, (sh, dt) in B_W_SHAPES.items()}
    with stack:
        with nc.Block() as block:
            p = Prog(nc, block, stack)
            consts = emit_consts(p, cin)
            s_out, n_out = emit_phase_b(p, consts, xin, xout, wb)
            @block.sync
            def _(sy):
                sy.wait_ge(p.sem(s_out), n_out)
    return nc


def host_prep_a(inp):
    return {
        "Wqd_blk": blk(np.asarray(inp["L0_attn_Wq_down"])),
        "Wkvd_blk": blk(np.asarray(inp["L0_attn_Wkv_down"])),
        "Wqup_blk": blk(np.asarray(inp["L0_attn_Wq_up"])),
        "Wkup_blk": blk(np.asarray(inp["L0_attn_Wk_up"])),
        "Wvup_blk": blk(np.asarray(inp["L0_attn_Wv_up"])),
        "Wo_blk": blk(np.asarray(inp["L0_attn_Wo"])),
        "bo_T": vecT(np.asarray(inp["L0_attn_bo"])),
    }

A_W_SHAPES = {
    "Wqd_blk": ([16, 8, 128, 128], F32),
    "Wkvd_blk": ([16, 4, 128, 128], F32),
    "Wqup_blk": ([8, 16, 128, 128], F32),
    "Wkup_blk": ([4, 16, 128, 128], F32),
    "Wvup_blk": ([4, 16, 128, 128], F32),
    "Wo_blk": ([16, 16, 128, 128], F32),
    "bo_T": ([128, 16], F32),
}


def build_a_test():
    nc = bass.Bass("TRN2", target_bir_lowering=False)
    stack = contextlib.ExitStack()
    xin = nc.dram_tensor("xin", [D, T], F32, kind="ExternalInput").ap()
    xout = nc.dram_tensor("xout", [D, T], F32, kind="ExternalOutput").ap()
    ctx_hbm = nc.dram_tensor("ctx_hbm", [D, T], BF16, kind="Internal").ap()
    cin = {k: nc.dram_tensor(k, sh, dt, kind="ExternalInput").ap()
           for k, (sh, dt) in CONST_SHAPES.items()}
    wa = {k: nc.dram_tensor(k, sh, dt, kind="ExternalInput").ap()
          for k, (sh, dt) in A_W_SHAPES.items()}
    wa["c_mask_a"] = cin["c_mask_a"]
    with stack:
        with nc.Block() as block:
            p = Prog(nc, block, stack)
            consts = emit_consts(p, cin)
            s_out, n_out = emit_phase_a(p, consts, xin, xout, ctx_hbm, wa)
            @block.sync
            def _(sy):
                sy.wait_ge(p.sem(s_out), n_out)
    return nc



ALL_W_SHAPES = {}
ALL_W_SHAPES.update(A_W_SHAPES)
ALL_W_SHAPES.update(B_W_SHAPES)
ALL_W_SHAPES.update(C_W_SHAPES)
ALL_W_SHAPES.update(D_W_SHAPES)


def build_full():
    nc = bass.Bass("TRN2", target_bir_lowering=False)
    stack = contextlib.ExitStack()
    x0 = nc.dram_tensor("x0", [D, T], F32, kind="ExternalInput").ap()
    yT = nc.dram_tensor("yT", [D, T], F32, kind="ExternalOutput").ap()
    xc1 = nc.dram_tensor("xc1", [D, T], F32, kind="Internal").ap()
    xc2 = nc.dram_tensor("xc2", [D, T], F32, kind="Internal").ap()
    ctx_hbm = nc.dram_tensor("ctx_hbm", [D, T], BF16, kind="Internal").ap()
    moe_acc = nc.dram_tensor("moe_acc", [D, T], F32, kind="Internal").ap()
    cin = {k: nc.dram_tensor(k, sh, dt, kind="ExternalInput").ap()
           for k, (sh, dt) in CONST_SHAPES.items()}
    wall = {k: nc.dram_tensor(k, sh, dt, kind="ExternalInput").ap()
            for k, (sh, dt) in ALL_W_SHAPES.items()}
    wall["c_mask_a"] = cin["c_mask_a"]
    wall["c_mask_c"] = cin["c_mask_c"]
    with stack:
        with nc.Block() as block:
            p = Prog(nc, block, stack)
            consts = emit_consts(p, cin)
            sA, nA = emit_phase_a(p, consts, x0, xc1, ctx_hbm, wall)
            sB, nB = emit_phase_b(p, consts, xc1, xc2, wall, s_prev=sA, n_prev=nA)
            sC, nC = emit_phase_c(p, consts, xc2, xc1, ctx_hbm, wall, s_prev=sB, n_prev=nB)
            sD, nD = emit_phase_d(p, consts, xc1, yT, moe_acc, wall, s_prev=sC, n_prev=nC)
            @block.sync
            def _(sy):
                sy.wait_ge(p.sem(sD), nD)
    return nc


_NC_CACHE = [None]

def kernel(**inputs):
    if _NC_CACHE[0] is None:
        _NC_CACHE[0] = build_full()
    nc = _NC_CACHE[0]
    shared = {}
    shared.update(host_consts())
    shared.update(host_prep_a(inputs))
    shared.update(host_prep_b(inputs))
    shared.update(host_prep_c(inputs))
    shared.update(host_prep_d(inputs))
    x = np.asarray(inputs["L0_attn_x"])
    in_maps = []
    for b in range(8):
        m = {"x0": np.ascontiguousarray(x[b].T)}
        m.update(shared)
        in_maps.append(m)
    res = run_bass_kernel_spmd(nc, in_maps, core_ids=list(range(8)))
    out = np.stack([np.asarray(res.results[b]["yT"]).T for b in range(8)])
    return np.ascontiguousarray(out.astype(np.float32))


# version 10
# speedup vs baseline: 1.1596x; 1.0014x over previous
import contextlib
import math
import numpy as np

import concourse.bass as bass
import concourse.mybir as mybir
from concourse.bass_utils import run_bass_kernel_spmd

F32 = mybir.dt.float32
BF16 = mybir.dt.bfloat16
U8 = mybir.dt.uint8
AF = mybir.ActivationFunctionType
OP = mybir.AluOpType
AX = mybir.AxisListType

T = 4096
D = 2048
ND = D // 128
NTT = T // 512
INV_SQRT_DK = 1.0 / math.sqrt(128.0)


def blk(w):
    O, I = w.shape
    return np.ascontiguousarray(
        w.reshape(O // 128, 128, I // 128, 128).transpose(2, 0, 3, 1))

def vecT(b):
    return np.ascontiguousarray(b.reshape(-1, 128).T)

def host_consts():
    import ml_dtypes
    ident = np.eye(128, dtype=np.float32)
    ma = np.zeros((4, 128, 512), dtype=np.float32)
    pp = np.arange(128)[:, None]
    ff = np.arange(512)[None, :]
    for d in range(4):
        ma[d] = (pp <= ff - d * 128)
    mc = (np.arange(64)[:, None] <= np.arange(64)[None, :]).astype(np.float32)
    tk = np.arange(128)[:, None]; tq = np.arange(128)[None, :]
    mc2 = ((tk // 64 < tq // 64) | ((tk // 64 == tq // 64) & (tk <= tq))).astype(np.float32)
    return {
        "c_ident_f": ident,
        "c_ident_b": ident.astype(ml_dtypes.bfloat16),
        "c_mask_a": ma.astype(ml_dtypes.bfloat16),
        "c_mask_c": mc2.astype(ml_dtypes.bfloat16),
    }


class Prog:
    def __init__(self, nc, block, stack):
        self.nc = nc
        self.block = block
        self.stack = stack
        self._sems = {}

    def sem(self, name):
        if name not in self._sems:
            h = self.stack.enter_context(self.nc.semaphore(name))
            self._sems[name] = [h, 0]
        return self._sems[name][0]

    def inc(self, name, k=1):
        self.sem(name)
        s = self._sems[name]
        s[1] += k
        return s[0], s[1]

    def cnt(self, name):
        self.sem(name)
        return self._sems[name][1]

    def sb(self, name, shape, dtype):
        return self.stack.enter_context(self.nc.sbuf_tensor(name, shape, dtype))

    def ps(self, name, shape, dtype=F32):
        return self.stack.enter_context(self.nc.psum_tensor(name, shape, dtype))

    def mark(self, ins, eng):
        h, n = self.inc("prog_" + eng)
        ins.then_inc(h, 1)
        return n

    def wait(self, e, eng, n=None):
        if n is None:
            n = self.cnt("prog_" + eng)
        if n > 0:
            e.wait_ge(self.sem("prog_" + eng), n)


def dma(eng, p, sname, out, in_, **kw):
    ins = eng.dma_start(out=out, in_=in_, **kw)
    h, n = p.inc(sname, 16)
    ins.then_inc(h, 16)
    return n


def emit_ln_tile(p, consts, x_sb, h_view, pb_row0, pb_row1, pb_bc, scr):
    nc, block = p.nc, p.block
    tmp, row_m, row_v, mb = (scr["tmp"], scr["row_m"], scr["row_v"], scr["mb"])

    pe_before = p.cnt("prog_pe")
    dve_before = p.cnt("prog_dve")
    act0 = p.cnt("prog_act")
    pe0 = p.cnt("prog_pe")
    dve0 = p.cnt("prog_dve")
    tmpb = scr["tmpb"]
    @block.scalar
    def _(s):
        p.wait(s, "pe", pe_before)
        p.wait(s, "dve", dve_before)
        for c in range(ND):
            if c >= 2:
                s.wait_ge(p.sem("prog_pe"), pe0 + (c - 1))
            ins = s.activation(out=tmpb[:, c % 2, :], in_=x_sb[:, c, :], func=AF.Square)
            p.mark(ins, "act")
    @block.vector
    def _(v):
        for c in range(ND):
            ins = v.tensor_copy(h_view(c), x_sb[:, c, :])
            p.mark(ins, "dve")
    @block.tensor
    def _(t):
        for c in range(ND):
            t.wait_ge(p.sem("prog_dve"), dve0 + c + 1)
            t.matmul(pb_row0[:], consts["ones_col_b"][:], h_view(c),
                     start=(c == 0), stop=(c == ND - 1))
            t.wait_ge(p.sem("prog_act"), act0 + c + 1)
            ins = t.matmul(pb_row1[:], consts["ones_col_b"][:], tmpb[:, c % 2, :],
                           start=(c == 0), stop=(c == ND - 1))
            p.mark(ins, "pe")
    pe_stats = p.cnt("prog_pe")

    @block.vector
    def _(v):
        p.wait(v, "pe", pe_stats)
        ins = v.tensor_scalar(row_m[:], pb_row0[:], 1.0 / D, None, OP.mult)
        nn = p.mark(ins, "dve")
        v.wait_ge(p.sem("prog_dve"), nn)
        ins = v.tensor_scalar(row_v[:], pb_row1[:], 1.0 / D, None, OP.mult)
        nn = p.mark(ins, "dve")
        v.wait_ge(p.sem("prog_dve"), nn)
        ins = v.tensor_tensor(tmp[0:1, :], row_m[:], row_m[:], OP.mult)
        nn = p.mark(ins, "dve")
        v.wait_ge(p.sem("prog_dve"), nn)
        ins = v.tensor_sub(row_v[:], row_v[:], tmp[0:1, :])
        p.mark(ins, "dve")
    dve_var = p.cnt("prog_dve")

    @block.scalar
    def _(s):
        p.wait(s, "dve", dve_var)
        ins = s.activation(out=row_v[:], in_=row_v[:], func=AF.Ln, bias=consts["eps_row"][:])
        nn = p.mark(ins, "act")
        s.wait_ge(p.sem("prog_act"), nn)
        ins = s.activation(out=row_v[:], in_=row_v[:], func=AF.Exp, scale=-0.5)
        p.mark(ins, "act")
    act_rstd = p.cnt("prog_act")

    @block.tensor
    def _(t):
        p.wait(t, "act", act_rstd)
        ins = t.matmul(pb_bc[:], consts["ones_row_f"][:], row_m[:], start=True, stop=True)
        p.mark(ins, "pe")
    pe_bc1 = p.cnt("prog_pe")

    @block.vector
    def _(v):
        p.wait(v, "pe", pe_bc1)
        ins = v.tensor_copy(mb[:], pb_bc[:])
        p.mark(ins, "dve")
    dve_mb = p.cnt("prog_dve")

    @block.tensor
    def _(t):
        p.wait(t, "dve", dve_mb)
        ins = t.matmul(pb_bc[:], consts["ones_row_f"][:], row_v[:], start=True, stop=True)
        p.mark(ins, "pe")
    pe_bc2 = p.cnt("prog_pe")

    @block.vector
    def _(v):
        p.wait(v, "pe", pe_bc2)
        for c in range(ND):
            ins = v.tensor_sub(tmp[:], x_sb[:, c, :], mb[:])
            nn = p.mark(ins, "dve")
            v.wait_ge(p.sem("prog_dve"), nn)
            ins = v.tensor_tensor(h_view(c), tmp[:], pb_bc[:], OP.mult)
            nn = p.mark(ins, "dve")
            v.wait_ge(p.sem("prog_dve"), nn)
    return p.cnt("prog_dve")


def emit_phase_b(p, consts, src_x, dst_x, w, s_prev=None, n_prev=0):
    nc, block = p.nc, p.block
    TTILE = 1024
    NTILE = T // TTILE

    with contextlib.ExitStack() as st:
        sb = lambda n, sh, dt: st.enter_context(nc.sbuf_tensor(n, sh, dt))
        ps = lambda n, sh: st.enter_context(nc.psum_tensor(n, sh, F32))

        x_sb = sb("b_x", [128, ND, 512], F32)
        h_sb = sb("b_h", [128, ND, TTILE], BF16)
        hf_sb = sb("b_hf", [128, 32, TTILE], BF16)
        o0_sb = sb("b_o0", [128, ND, TTILE], BF16)
        wa_sb = sb("b_wa", [128, 2, ND, 128], BF16)
        wb_sb = sb("b_wb", [128, 2, 32, 128], BF16)
        wg_sb = sb("b_wg", [128, ND, 2], F32)
        bias_sb = sb("b_bias", [128, 96], F32)
        mask_sb = sb("b_mask", [128, 2, 512], U8)
        xres_sb = sb("b_xres", [128, 2, 512], F32)
        t1_sb = sb("b_t1", [128, 512], F32)
        t2_sb = sb("b_t2", [128, 2, 512], F32)
        wgd_sb = sb("b_wgd", [128, ND], F32)
        s01_sb = sb("b_s01", [1, 32], F32)
        scr = {
            "tmp": sb("b_tmp", [128, 512], F32),
            "tmpb": sb("b_tmpb", [128, 2, 512], BF16),
            "row_m": sb("b_rm", [1, 512], F32),
            "row_v": sb("b_rv", [1, 512], F32),
            "mb": sb("b_mb", [128, 512], F32),
        }
        pb_row0 = ps("b_pr0", [1, 512])
        pb_row1 = ps("b_pr1", [1, 512])
        pb_bc = ps("b_pbc", [128, 512])
        pb_h = [ps("b_ph0", [128, 512]), ps("b_ph1", [128, 512])]
        pb_o = [ps("b_po0", [128, 512]), ps("b_po1", [128, 512])]

        s_ldx, s_ldw, s_ldr, s_out = "b_ldx", "b_hw", "b_ldr", "b_out"
        s_gw = "b_gw"

        @block.sync
        def _(sy):
            if s_prev is not None:
                sy.wait_ge(p.sem(s_prev), n_prev)
            dma(sy, p, s_ldw, wg_sb[:], w["WgT"].rearrange("(c p) e -> p c e", p=128))
            dma(sy, p, s_ldw, bias_sb[:, 0:32], w["b0a_T"][:])
            dma(sy, p, s_ldw, bias_sb[:, 32:64], w["b1a_T"][:])
            dma(sy, p, s_ldw, bias_sb[:, 64:80], w["b0b_T"][:])
            dma(sy, p, s_ldw, bias_sb[:, 80:96], w["b1b_T"][:])
        n_ldw0 = p.cnt(s_ldw)
        @block.vector
        def _(v):
            v.wait_ge(p.sem(s_ldw), n_ldw0)
            ins = v.tensor_sub(wgd_sb[:], wg_sb[:, :, 0], wg_sb[:, :, 1])
            p.mark(ins, "dve")
        dve_wgd = p.cnt("prog_dve")
        @block.tensor
        def _(t):
            p.wait(t, "dve", dve_wgd)
            ins = t.matmul(pb_row0[:, 0:16], consts["ones_col_f"][:], wgd_sb[:],
                           start=True, stop=True)
            p.mark(ins, "pe")
        pe_s01 = p.cnt("prog_pe")
        @block.vector
        def _(v):
            p.wait(v, "pe", pe_s01)
            ins = v.reduce_sum(s01_sb[:, 16:17], pb_row0[:, 0:16], axis=AX.X)
            p.mark(ins, "dve")

        def b0a(f): return bias_sb[:, f:f + 1]
        def b1a(f): return bias_sb[:, 32 + f:33 + f]
        def b0b(o): return bias_sb[:, 64 + o:65 + o]
        def b1b(o): return bias_sb[:, 80 + o:81 + o]

        wa_last = {0: 0, 1: 0}
        wb_last = {0: 0, 1: 0}
        out_cnt = {0: 0, 1: 0}
        for tile in range(NTILE):
            t0 = tile * TTILE
            for sub in range(2):
                c0 = t0 + sub * 512
                dve_before = p.cnt("prog_dve")
                pe_before = p.cnt("prog_pe")
                @block.sync
                def _(sy):
                    p.wait(sy, "dve", dve_before)
                    p.wait(sy, "pe", pe_before)
                    if s_prev is not None and tile == 0 and sub == 0:
                        sy.wait_ge(p.sem(s_prev), n_prev)
                    dma(sy, p, s_ldx, x_sb[:],
                        src_x.rearrange("(c p) t -> p c t", p=128)[:, :, c0:c0 + 512])
                n_ldx = p.cnt(s_ldx)
                @block.tensor
                def _(t):
                    t.wait_ge(p.sem(s_ldx), n_ldx)
                    p.wait(t, "act")
                    p.wait(t, "dve")
                @block.scalar
                def _(s):
                    s.wait_ge(p.sem(s_ldx), n_ldx)
                @block.vector
                def _(v):
                    v.wait_ge(p.sem(s_ldx), n_ldx)
                hv = lambda c: h_sb[:, c, sub * 512:(sub + 1) * 512]
                emit_ln_tile(p, consts, x_sb, hv, pb_row0, pb_row1, pb_bc, scr)
                dve_h = p.cnt("prog_dve")

                @block.tensor
                def _(t):
                    p.wait(t, "dve", dve_h)
                    for c in range(ND):
                        ins = t.matmul(pb_row0[:], wgd_sb[:, c:c + 1],
                                       x_sb[:, c, :],
                                       start=(c == 0), stop=(c == ND - 1))
                    p.mark(ins, "pe")
                pe_logit = p.cnt("prog_pe")
                @block.vector
                def _(v):
                    p.wait(v, "pe", pe_logit)
                    ins = v.scalar_tensor_tensor(scr["row_v"][:], scr["row_m"][:],
                                                 s01_sb[0:1, 16:17], pb_row0[:],
                                                 OP.mult, OP.subtract)
                    p.mark(ins, "dve")
                dve_diff = p.cnt("prog_dve")
                @block.tensor
                def _(t):
                    p.wait(t, "dve", dve_diff)
                    ins = t.matmul(pb_bc[:], consts["ones_row_f"][:], scr["row_v"][:],
                                   start=True, stop=True)
                    p.mark(ins, "pe")
                pe_bcm = p.cnt("prog_pe")
                @block.vector
                def _(v):
                    p.wait(v, "pe", pe_bcm)
                    ins = v.tensor_single_scalar(mask_sb[:, sub, :], pb_bc[:],
                                                 0.0, OP.is_le)
                    p.mark(ins, "dve")

            for e in range(2):
                Wa = w["W0a_blk"] if e == 0 else w["W1a_blk"]
                Wb = w["W0b_blk"] if e == 0 else w["W1b_blk"]
                peb = p.cnt("prog_pe")
                actb = p.cnt("prog_act")
                dveb = p.cnt("prog_dve")
                n_wa = {}
                for f in range(32):
                    buf = f % 2
                    s_gwb = s_gw + str(buf)
                    @block.gpsimd
                    def _(g):
                        if s_prev is not None and tile == 0 and e == 0 and f < 2:
                            g.wait_ge(p.sem(s_prev), n_prev)
                        if f >= 2:
                            g.wait_ge(p.sem("prog_pe"), peb + 2 * (f - 2) + 2)
                        elif wa_last[buf] > 0:
                            p.wait(g, "pe", wa_last[buf])
                        dma(g, p, s_gwb, wa_sb[:, buf],
                            Wa[:, f].rearrange("i p m -> p i m"))
                    n_wa[f] = p.cnt(s_gwb)
                def act_h(f, sub):
                    return (actb + 2 * f + sub + 1) if e == 0 else None
                def dve_h(f, sub):
                    return dveb + 2 * f + sub + 1
                @block.tensor
                def _(t):
                    for f in range(32):
                        buf = f % 2
                        t.wait_ge(p.sem(s_gw + str(buf)), n_wa[f])
                        if f == 0:
                            p.wait(t, "act")
                            p.wait(t, "dve")
                        for sub in range(2):
                            if f >= 1:
                                if e == 0:
                                    t.wait_ge(p.sem("prog_act"), act_h(f - 1, sub))
                                else:
                                    t.wait_ge(p.sem("prog_dve"), dve_h(f - 1, sub))
                            for c in range(ND):
                                ins = t.matmul(pb_h[sub][:], wa_sb[:, buf, c, :],
                                               h_sb[:, c, sub * 512:(sub + 1) * 512],
                                               start=(c == 0), stop=(c == ND - 1))
                            p.mark(ins, "pe")
                wa_last[0] = peb + 63
                wa_last[1] = peb + 64
                if e == 0:
                    @block.scalar
                    def _(s):
                        for f in range(32):
                            for sub in range(2):
                                s.wait_ge(p.sem("prog_pe"), peb + 2 * f + sub + 1)
                                ins = s.activation(out=hf_sb[:, f, sub * 512:(sub + 1) * 512],
                                                   in_=pb_h[sub][:], func=AF.Gelu,
                                                   bias=b0a(f))
                                p.mark(ins, "act")
                else:
                    @block.vector
                    def _(v):
                        for f in range(32):
                            for sub in range(2):
                                v.wait_ge(p.sem("prog_pe"), peb + 2 * f + sub + 1)
                                ins = v.tensor_scalar(hf_sb[:, f, sub * 512:(sub + 1) * 512],
                                                      pb_h[sub][:], b1a(f), 0.0,
                                                      OP.add, OP.max)
                                p.mark(ins, "dve")
                peb2 = p.cnt("prog_pe")
                actb2 = p.cnt("prog_act")
                dveb2 = p.cnt("prog_dve")
                n_wb = {}
                for o in range(ND):
                    buf = o % 2
                    s_gwb = s_gw + "b" + str(buf)
                    @block.gpsimd
                    def _(g):
                        if o >= 2:
                            g.wait_ge(p.sem("prog_pe"), peb2 + 2 * (o - 2) + 2)
                        elif wb_last[buf] > 0:
                            p.wait(g, "pe", wb_last[buf])
                        dma(g, p, s_gwb, wb_sb[:, buf],
                            Wb[:, o].rearrange("i p m -> p i m"))
                    n_wb[o] = p.cnt(s_gwb)
                if e == 1:
                    n_xr = {}
                    for o in range(ND):
                        @block.scalar
                        def _(s):
                            if o >= 1:
                                s.wait_ge(p.sem("prog_dve"), dveb2 + 3 * (2 * o))
                            else:
                                s.wait_ge(p.sem("prog_dve"), dveb2)
                            dma(s, p, s_ldr, xres_sb[:, 0, :],
                                src_x[o * 128:(o + 1) * 128, t0:t0 + 512])
                            dma(s, p, s_ldr, xres_sb[:, 1, :],
                                src_x[o * 128:(o + 1) * 128, t0 + 512:t0 + 1024])
                        n_xr[o] = p.cnt(s_ldr)
                @block.tensor
                def _(t):
                    for o in range(ND):
                        buf = o % 2
                        t.wait_ge(p.sem(s_gw + "b" + str(buf)), n_wb[o])
                        for sub in range(2):
                            if o >= 1:
                                if e == 0:
                                    t.wait_ge(p.sem("prog_act"), actb2 + 2 * (o - 1) + sub + 1)
                                else:
                                    t.wait_ge(p.sem("prog_dve"), dveb2 + 3 * (2 * (o - 1) + sub) + 3)
                            if o == 0 and sub == 0:
                                p.wait(t, "act")
                                p.wait(t, "dve")
                            for f in range(32):
                                ins = t.matmul(pb_o[sub][:], wb_sb[:, buf, f, :],
                                               hf_sb[:, f, sub * 512:(sub + 1) * 512],
                                               start=(f == 0), stop=(f == 31))
                            p.mark(ins, "pe")
                if e == 0:
                    @block.scalar
                    def _(s):
                        for o in range(ND):
                            for sub in range(2):
                                s.wait_ge(p.sem("prog_pe"), peb2 + 2 * o + sub + 1)
                                ins = s.activation(out=o0_sb[:, o, sub * 512:(sub + 1) * 512],
                                                   in_=pb_o[sub][:], func=AF.Identity,
                                                   bias=b0b(o))
                                p.mark(ins, "act")
                else:
                    @block.vector
                    def _(v):
                        for o in range(ND):
                            v.wait_ge(p.sem(s_ldr), n_xr[o])
                            if o == 0:
                                p.wait(v, "act")
                            for sub in range(2):
                                s_ou = s_out + str(sub)
                                n_op = out_cnt[sub]
                                v.wait_ge(p.sem("prog_pe"), peb2 + 2 * o + sub + 1)
                                if n_op > 0:
                                    v.wait_ge(p.sem(s_ou), n_op)
                                ins = v.tensor_scalar(t1_sb[:], pb_o[sub][:], b1b(o),
                                                      None, OP.add)
                                nn = p.mark(ins, "dve")
                                v.wait_ge(p.sem("prog_dve"), nn)
                                ins = v.select(t2_sb[:, sub, :], mask_sb[:, sub, :],
                                               o0_sb[:, o, sub * 512:(sub + 1) * 512],
                                               t1_sb[:], add_drain=True)
                                nn = p.mark(ins, "dve")
                                v.wait_ge(p.sem("prog_dve"), nn)
                                ins = v.tensor_add(t2_sb[:, sub, :], t2_sb[:, sub, :],
                                                   xres_sb[:, sub, :])
                                p.mark(ins, "dve")
                                out_cnt[sub] += 16
                    @block.sync
                    def _(sy):
                        for o in range(ND):
                            for sub in range(2):
                                sy.wait_ge(p.sem("prog_dve"), dveb2 + 3 * (2 * o + sub) + 3)
                                dma(sy, p, s_out + str(sub),
                                    dst_x[o * 128:(o + 1) * 128,
                                          t0 + sub * 512:t0 + (sub + 1) * 512],
                                    t2_sb[:, sub, :])
                wb_last[0] = peb2 + 31
                wb_last[1] = peb2 + 32
        n_o0 = p.cnt(s_out + "0"); n_o1 = p.cnt(s_out + "1")
        @block.sync
        def _(sy):
            sy.wait_ge(p.sem(s_out + "0"), n_o0)
            sy.wait_ge(p.sem(s_out + "1"), n_o1)
            ins = sy.engine_nop() if hasattr(sy, "engine_nop") else sy.nop()
            h, n = p.inc("b_done", 1)
            ins.then_inc(h, 1)
    return "b_done", p.cnt("b_done")



def emit_phase_a(p, consts, src_x, dst_x, ctx_hbm, w, s_prev=None, n_prev=0, dbg=None):
    nc, block = p.nc, p.block
    s_ldx, s_ldw, s_ldr, s_ldc, s_out, s_ctxw = ("a_ldx", "a_hw", "a_ldr",
                                                 "a_ldc", "a_out", "a_ctxw")
    s_gw = "a_gw"

    with contextlib.ExitStack() as stQ:
        sbQ = lambda n, sh, dt: stQ.enter_context(nc.sbuf_tensor(n, sh, dt))
        qlat = sbQ("a_qlat", [128, 8, T], BF16)
        ckv = sbQ("a_ckv", [128, 4, T], BF16)
        @block.sync
        def _(sy):
            if s_prev is not None:
                sy.wait_ge(p.sem(s_prev), n_prev)

        with contextlib.ExitStack() as stA:
            sbA = lambda n, sh, dt: stA.enter_context(nc.sbuf_tensor(n, sh, dt))
            x_sb = sbA("a_x", [128, ND, 512], F32)
            h_sb = sbA("a_h", [128, ND, 512], BF16)
            wqd_sb = sbA("a_wqd", [128, ND, 8, 128], BF16)
            wkvd_sb = sbA("a_wkvd", [128, ND, 4, 128], BF16)
            scr = {
                "tmp": sbA("a_tmp", [128, 512], F32),
            "tmpb": sbA("a_tmpb", [128, 2, 512], BF16),
                "row_m": sbA("a_rm", [1, 512], F32),
                "row_v": sbA("a_rv", [1, 512], F32),
                "mb": sbA("a_mb", [128, 512], F32),
            }
            with contextlib.ExitStack() as stP:
                psA = lambda n, sh: stP.enter_context(nc.psum_tensor(n, sh, F32))
                pb_row0 = psA("a_pr0", [1, 512])
                pb_row1 = psA("a_pr1", [1, 512])
                pb_bc = psA("a_pbc", [128, 512])
                pb_p = [psA("a_pp0", [128, 512]), psA("a_pp1", [128, 512])]

                @block.gpsimd
                def _(g):
                    for ih in range(2):
                        dma(g, p, s_gw, wqd_sb[:, ih * 8:(ih + 1) * 8],
                            w["Wqd_blk"][ih * 8:(ih + 1) * 8].rearrange("i o p m -> p i o m"))
                    dma(g, p, s_gw, wkvd_sb[:], w["Wkvd_blk"].rearrange("i o p m -> p i o m"))
                n_ldw0 = p.cnt(s_gw)

                for ts in range(NTT):
                    c0 = ts * 512
                    dve_b = p.cnt("prog_dve")
                    pe_b = p.cnt("prog_pe")
                    @block.sync
                    def _(sy):
                        p.wait(sy, "dve", dve_b)
                        p.wait(sy, "pe", pe_b)
                        dma(sy, p, s_ldx, x_sb[:],
                            src_x.rearrange("(c p) t -> p c t", p=128)[:, :, c0:c0 + 512])
                    n_ldx = p.cnt(s_ldx)
                    @block.tensor
                    def _(t):
                        t.wait_ge(p.sem(s_ldx), n_ldx)
                        if ts == 0:
                            t.wait_ge(p.sem(s_gw), n_ldw0)
                        p.wait(t, "act")
                        p.wait(t, "dve")
                    @block.scalar
                    def _(s):
                        s.wait_ge(p.sem(s_ldx), n_ldx)
                    @block.vector
                    def _(v):
                        v.wait_ge(p.sem(s_ldx), n_ldx)
                    hv = lambda c: h_sb[:, c, :]
                    dve_h = emit_ln_tile(p, consts, x_sb, hv, pb_row0, pb_row1, pb_bc, scr)

                    for proj, wsb, nO, dst in (("q", wqd_sb, 8, qlat), ("kv", wkvd_sb, 4, ckv)):
                        for o in range(nO):
                            bank = pb_p[o % 2]
                            dve_free = p.cnt("prog_dve")
                            @block.tensor
                            def _(t):
                                if proj == "q" and o == 0:
                                    p.wait(t, "dve", dve_h)
                                else:
                                    p.wait(t, "dve", dve_free)
                                for c in range(ND):
                                    ins = t.matmul(bank[:], wsb[:, c, o, :], h_sb[:, c, :],
                                                   start=(c == 0), stop=(c == ND - 1))
                                p.mark(ins, "pe")
                            pe_g = p.cnt("prog_pe")
                            @block.vector
                            def _(v):
                                p.wait(v, "pe", pe_g)
                                ins = v.tensor_copy(dst[:, o, c0:c0 + 512], bank[:])
                                p.mark(ins, "dve")

        with contextlib.ExitStack() as stH:
            sbH = lambda n, sh, dt: stH.enter_context(nc.sbuf_tensor(n, sh, dt))
            mask_a = sbH("a_mask", [128, 4, 512], BF16)
            wq_sb = sbH("a_wq", [128, 2, 8, 128], BF16)
            wk_sb = sbH("a_wk", [128, 2, 4, 128], BF16)
            wv_sb = sbH("a_wv", [128, 2, 4, 128], BF16)
            qt_sb = sbH("a_qt", [128, T], BF16)
            kt_sb = sbH("a_kt", [128, T], BF16)
            v_sb = sbH("a_v", [128, 32, 128], BF16)
            p_sb = sbH("a_p", [128, 6, 512], BF16)
            sq_sb = sbH("a_sq", [128, 2, 512], BF16)
            bc_sb = sbH("a_bc", [128, 512], F32)
            ctx_sb = sbH("a_ctx", [128, 2, 512], BF16)
            rsum_sb = sbH("a_rsum", [1, 512], F32)
            nrm_sb = sbH("a_nrm", [1, 32], F32)
            negc_sb = sbH("a_negc", [128, 1], F32)
            with contextlib.ExitStack() as stP:
                psH = lambda n, sh: stP.enter_context(nc.psum_tensor(n, sh, F32))
                pb_S = [psH("a_ps0", [128, 512]), psH("a_ps1", [128, 512]),
                        psH("a_ps2", [128, 512])]
                pb_sum = [psH("a_psum0", [1, 512]), psH("a_psum1", [1, 512])]
                pb_ctx = [psH("a_pctx0", [128, 512]), psH("a_pctx1", [128, 512])]
                pb_bc2 = psH("a_pbc2", [128, 512])

                pe_a1 = p.cnt("prog_pe"); dve_a1 = p.cnt("prog_dve"); act_a1 = p.cnt("prog_act")
                pe_a1b = p.cnt("prog_pe")
                @block.sync
                def _(sy):
                    p.wait(sy, "pe", pe_a1b)
                    p.wait(sy, "dve", dve_a1)
                    p.wait(sy, "act", act_a1)
                    dma(sy, p, s_ldw, mask_a[:], w["c_mask_a"].rearrange("d p f -> p d f"))
                n_mask = p.cnt(s_ldw)
                wql = {0: 0, 1: 0}
                @block.vector
                def _(v):
                    v.wait_ge(p.sem(s_ldw), n_mask)
                for hd in range(16):
                    buf = hd % 2
                    s_gwb = s_gw + str(buf)
                    @block.gpsimd
                    def _(g):
                        if hd < 2:
                            p.wait(g, "pe", pe_a1)
                            p.wait(g, "dve", dve_a1)
                            p.wait(g, "act", act_a1)
                        p.wait(g, "pe", wql[buf])
                        dma(g, p, s_gwb, wq_sb[:, buf],
                            w["Wqup_blk"][:, hd].rearrange("r p m -> p r m"))
                        dma(g, p, s_gwb, wk_sb[:, buf],
                            w["Wkup_blk"][:, hd].rearrange("r p m -> p r m"))
                        dma(g, p, s_gwb, wv_sb[:, buf],
                            w["Wvup_blk"][:, hd].rearrange("r p m -> p r m"))
                    n_w = p.cnt(s_gwb)

                    for ts in range(NTT):
                        for which, wsb, nR, dst in (("q", wq_sb, 8, qt_sb), ("k", wk_sb, 4, kt_sb)):
                            bank = pb_S[ts % 2]
                            dve_free = p.cnt("prog_dve")
                            act_free = p.cnt("prog_act")
                            @block.tensor
                            def _(t):
                                t.wait_ge(p.sem(s_gwb), n_w)
                                p.wait(t, "dve", dve_free)
                                p.wait(t, "act", act_free)
                                srcq = qlat if which == "q" else ckv
                                for r in range(nR):
                                    ins = t.matmul(bank[:], wsb[:, buf, r, :],
                                                   srcq[:, r, ts * 512:(ts + 1) * 512],
                                                   start=(r == 0), stop=(r == nR - 1))
                                p.mark(ins, "pe")
                            pe_g = p.cnt("prog_pe")
                            @block.vector
                            def _(v):
                                p.wait(v, "pe", pe_g)
                                ins = v.tensor_copy(dst[:, ts * 512:(ts + 1) * 512], bank[:])
                                p.mark(ins, "dve")
                    for tc in range(32):
                        bank = pb_S[tc % 2]
                        dve_free = p.cnt("prog_dve")
                        @block.tensor
                        def _(t):
                            p.wait(t, "dve", dve_free)
                            for r in range(4):
                                ins = t.matmul(bank[:, 0:128], ckv[:, r, tc * 128:(tc + 1) * 128],
                                               wv_sb[:, buf, r, :],
                                               start=(r == 0), stop=(r == 3))
                            p.mark(ins, "pe")
                        pe_g = p.cnt("prog_pe")
                        @block.vector
                        def _(v):
                            p.wait(v, "pe", pe_g)
                            ins = v.tensor_copy(v_sb[:, tc, :], bank[:, 0:128])
                            p.mark(ins, "dve")
                    wql[buf] = p.cnt("prog_pe")

                    peb = p.cnt("prog_pe")
                    actb = p.cnt("prog_act")
                    dveb = p.cnt("prog_dve")
                    srcs = [(qt_sb, ts) for ts in range(NTT)] + [(kt_sb, ts) for ts in range(NTT)]
                    @block.scalar
                    def _(s):
                        p.wait(s, "pe", peb)
                        p.wait(s, "dve", dveb)
                        for idx, (src_t, ts) in enumerate(srcs):
                            if idx >= 2:
                                s.wait_ge(p.sem("prog_pe"), peb + idx - 1)
                            ins = s.activation(out=sq_sb[:, idx % 2, :],
                                               in_=src_t[:, ts * 512:(ts + 1) * 512],
                                               func=AF.Square)
                            p.mark(ins, "act")
                    @block.tensor
                    def _(t):
                        for idx in range(16):
                            t.wait_ge(p.sem("prog_act"), actb + idx + 1)
                            if idx >= 2:
                                t.wait_ge(p.sem("prog_dve"), dveb + idx - 1)
                            ins = t.matmul(pb_sum[idx % 2][:], consts["ones_col_b"][:],
                                           sq_sb[:, idx % 2, :], start=True, stop=True)
                            p.mark(ins, "pe")
                    @block.vector
                    def _(v):
                        for idx in range(16):
                            v.wait_ge(p.sem("prog_pe"), peb + idx + 1)
                            ins = v.reduce_max(nrm_sb[0:1, idx:idx + 1],
                                               pb_sum[idx % 2][:], axis=AX.X)
                            p.mark(ins, "dve")
                    dve_n = p.cnt("prog_dve")
                    @block.vector
                    def _(v):
                        v.wait_ge(p.sem("prog_dve"), dve_n)
                        v.reduce_max(nrm_sb[0:1, 16:17], nrm_sb[0:1, 0:8], axis=AX.X)
                        ins = v.reduce_max(nrm_sb[0:1, 17:18], nrm_sb[0:1, 8:16], axis=AX.X)
                        nn = p.mark(ins, "dve")
                        v.wait_ge(p.sem("prog_dve"), nn)
                        ins = v.tensor_tensor(nrm_sb[0:1, 18:19], nrm_sb[0:1, 16:17],
                                              nrm_sb[0:1, 17:18], OP.mult)
                        p.mark(ins, "dve")
                    dve_c = p.cnt("prog_dve")
                    @block.scalar
                    def _(s):
                        p.wait(s, "dve", dve_c)
                        ins = s.activation(out=nrm_sb[0:1, 19:20], in_=nrm_sb[0:1, 18:19],
                                           func=AF.Sqrt, scale=1.0 / 128.0)
                        nn = p.mark(ins, "act")
                        s.wait_ge(p.sem("prog_act"), nn)
                        ins = s.activation(out=nrm_sb[0:1, 20:21], in_=nrm_sb[0:1, 19:20],
                                           func=AF.Copy, scale=-1.0)
                        p.mark(ins, "act")
                    act_c = p.cnt("prog_act")
                    dve_negc_f = p.cnt("prog_dve")
                    @block.tensor
                    def _(t):
                        p.wait(t, "act", act_c)
                        p.wait(t, "dve", dve_negc_f)
                        ins = t.matmul(pb_bc2[:, 0:1], consts["ones_row_f"][:],
                                       nrm_sb[0:1, 20:21], start=True, stop=True)
                        p.mark(ins, "pe")
                    pe_c = p.cnt("prog_pe")
                    @block.vector
                    def _(v):
                        p.wait(v, "pe", pe_c)
                        ins = v.tensor_copy(negc_sb[:], pb_bc2[:, 0:1])
                        p.mark(ins, "dve")
                    dve_negc = p.cnt("prog_dve")

                    for qt in range(NTT):
                        nS = 4 * qt + 4
                        q_sl = qt_sb[:, qt * 512:(qt + 1) * 512]
                        sum_bank = pb_sum[qt % 2]
                        ctx_bank = pb_ctx[qt % 2]
                        dve_qt0 = p.cnt("prog_dve")
                        pe0 = p.cnt("prog_pe")
                        act0 = p.cnt("prog_act")
                        dve0 = p.cnt("prog_dve")
                        def pe_score(c):
                            return pe0 + (1 if c == 0 else 2 * c)
                        def pe_sumpv(c):
                            return pe0 + (2 * c + 3 if c <= nS - 2 else 2 * nS)
                        def act_exp(c):
                            return act0 + c + 1
                        def dve_mask(c):
                            return dve0 + (c - (nS - 4)) + 1
                        @block.tensor
                        def _(t):
                            def emit_score(c):
                                if c == 0:
                                    p.wait(t, "dve", dve_negc)
                                    p.wait(t, "dve", dve_qt0)
                                if c >= 3:
                                    t.wait_ge(p.sem("prog_act"), act_exp(c - 3))
                                ins = t.matmul(pb_S[c % 3][:], kt_sb[:, c * 128:(c + 1) * 128],
                                               q_sl, start=True, stop=True)
                                p.mark(ins, "pe")
                            def emit_sumpv(c):
                                t.wait_ge(p.sem("prog_act"), act_exp(c))
                                if c >= nS - 4:
                                    t.wait_ge(p.sem("prog_dve"), dve_mask(c))
                                t.matmul(sum_bank[:], consts["ones_col_b"][:], p_sb[:, c % 6, :],
                                         start=(c == 0), stop=(c == nS - 1),
                                         skip_group_check=True)
                                ins = t.matmul(ctx_bank[:], v_sb[:, c, :], p_sb[:, c % 6, :],
                                               start=(c == 0), stop=(c == nS - 1),
                                               skip_group_check=True)
                                p.mark(ins, "pe")
                            emit_score(0)
                            for c in range(1, nS):
                                emit_score(c)
                                emit_sumpv(c - 1)
                            emit_sumpv(nS - 1)
                        @block.scalar
                        def _(s):
                            for c in range(nS):
                                s.wait_ge(p.sem("prog_pe"),
                                          max(pe_score(c), pe_sumpv(c - 6) if c >= 6 else 0))
                                if c >= 6 and c - 6 >= nS - 4:
                                    s.wait_ge(p.sem("prog_dve"), dve_mask(c - 6))
                                ins = s.activation(out=p_sb[:, c % 6, :], in_=pb_S[c % 3][:],
                                                   func=AF.Exp, bias=negc_sb[:],
                                                   scale=INV_SQRT_DK)
                                p.mark(ins, "act")
                        @block.vector
                        def _(v):
                            for c in range(nS - 4, nS):
                                v.wait_ge(p.sem("prog_act"), act_exp(c))
                                ins = v.tensor_tensor(p_sb[:, c % 6, :], p_sb[:, c % 6, :],
                                                      mask_a[:, c - (nS - 4), :], OP.mult)
                                p.mark(ins, "dve")
                        pe_attn = p.cnt("prog_pe")
                        @block.vector
                        def _(v):
                            p.wait(v, "pe", pe_attn)
                            ins = v.reciprocal(rsum_sb[:], sum_bank[:])
                            p.mark(ins, "dve")
                        dve_r = p.cnt("prog_dve")
                        @block.tensor
                        def _(t):
                            p.wait(t, "dve", dve_r)
                            ins = t.matmul(pb_bc2[:], consts["ones_row_f"][:], rsum_sb[:],
                                           start=True, stop=True)
                            p.mark(ins, "pe")
                        pe_bc = p.cnt("prog_pe")
                        s_cw = s_ctxw + str(qt % 2)
                        n_ctxw_prev = p.cnt(s_cw)
                        @block.vector
                        def _(v):
                            p.wait(v, "pe", pe_bc)
                            ins = v.tensor_copy(bc_sb[:], pb_bc2[:])
                            nn = p.mark(ins, "dve")
                            v.wait_ge(p.sem("prog_dve"), nn)
                            if n_ctxw_prev > 0:
                                v.wait_ge(p.sem(s_cw), n_ctxw_prev)
                            ins = v.tensor_tensor(ctx_sb[:, qt % 2, :], ctx_bank[:],
                                                  bc_sb[:], OP.mult)
                            p.mark(ins, "dve")
                        dve_ctx = p.cnt("prog_dve")
                        @block.sync
                        def _(sy):
                            p.wait(sy, "dve", dve_ctx)
                            dma(sy, p, s_cw,
                                ctx_hbm[hd * 128:(hd + 1) * 128, qt * 512:(qt + 1) * 512],
                                ctx_sb[:, qt % 2, :])

    n_cw0 = p.cnt(s_ctxw + "0")
    n_cw1 = p.cnt(s_ctxw + "1")
    with contextlib.ExitStack() as stO:
        sbO = lambda n, sh, dt: stO.enter_context(nc.sbuf_tensor(n, sh, dt))
        wo_sb = sbO("a_wo", [128, ND, ND, 128], BF16)
        bo_sb = sbO("a_bo", [128, ND], F32)
        ctxt_sb = sbO("a_ctxt", [128, 2, ND, 512], BF16)
        xres_sb = sbO("a_xres", [128, 4, 512], F32)
        t2_sb = sbO("a_t2", [128, 2, 512], F32)
        with contextlib.ExitStack() as stP:
            psO = lambda n, sh: stP.enter_context(nc.psum_tensor(n, sh, F32))
            pb_o = [psO("a_po0", [128, 512]), psO("a_po1", [128, 512])]
            pe_a2 = p.cnt("prog_pe"); dve_a2 = p.cnt("prog_dve"); act_a2 = p.cnt("prog_act")
            @block.sync
            def _(sy):
                p.wait(sy, "pe", pe_a2); p.wait(sy, "dve", dve_a2); p.wait(sy, "act", act_a2)
                sy.wait_ge(p.sem(s_ctxw + "0"), n_cw0)
                sy.wait_ge(p.sem(s_ctxw + "1"), n_cw1)
                dma(sy, p, s_ldw, bo_sb[:], w["bo_T"][:])
            n_bo_h = p.cnt(s_ldw)
            @block.gpsimd
            def _(g):
                p.wait(g, "pe", pe_a2); p.wait(g, "dve", dve_a2); p.wait(g, "act", act_a2)
                g.wait_ge(p.sem(s_ctxw + "0"), n_cw0)
                g.wait_ge(p.sem(s_ctxw + "1"), n_cw1)
                for ih in range(4):
                    dma(g, p, s_gw + "0", wo_sb[:, ih * 4:(ih + 1) * 4],
                        w["Wo_blk"][ih * 4:(ih + 1) * 4].rearrange("i o p m -> p i o m"))
            n_bo = p.cnt(s_gw + "0")
            for ts in range(NTT):
                cbuf = ts % 2
                s_lc = s_ldc + str(cbuf)
                pe_b = p.cnt("prog_pe")
                @block.sync
                def _(sy):
                    sy.wait_ge(p.sem(s_ctxw + "0"), n_cw0)
                    sy.wait_ge(p.sem(s_ctxw + "1"), n_cw1)
                    p.wait(sy, "pe", pe_b)
                    dma(sy, p, s_lc, ctxt_sb[:, cbuf],
                        ctx_hbm.rearrange("(c p) t -> p c t", p=128)[:, :, ts * 512:(ts + 1) * 512])
                n_ldc = p.cnt(s_lc)
                if dbg is not None and ts == 0:
                    @block.sync
                    def _(sy):
                        sy.wait_ge(p.sem(s_lc), n_ldc)
                        sy.wait_ge(p.sem(s_gw + "0"), n_bo)
                        dma(sy, p, "dbgd", dbg["ctxt0"], ctxt_sb[:, 0])
                        dma(sy, p, "dbgd", dbg["wo"], wo_sb[:])
                for o in range(ND):
                    dve_res = p.cnt("prog_dve")
                    s_lr = s_ldr + str(o % 4)
                    @block.scalar
                    def _(s):
                        if ts == 0 and o == 0:
                            s.wait_ge(p.sem(s_ctxw + "0"), n_cw0)
                            s.wait_ge(p.sem(s_ctxw + "1"), n_cw1)
                        p.wait(s, "dve", dve_res)
                        dma(s, p, s_lr, xres_sb[:, o % 4, :],
                            src_x[o * 128:(o + 1) * 128, ts * 512:(ts + 1) * 512])
                    n_r = p.cnt(s_lr)
                    bank = pb_o[o % 2]
                    dve_free = p.cnt("prog_dve")
                    @block.tensor
                    def _(t):
                        t.wait_ge(p.sem(s_gw + "0"), n_bo)
                        t.wait_ge(p.sem(s_ldw), n_bo_h)
                        t.wait_ge(p.sem(s_lc), n_ldc)
                        p.wait(t, "dve", dve_free)
                        for c in range(ND):
                            ins = t.matmul(bank[:], wo_sb[:, c, o, :],
                                           ctxt_sb[:, cbuf, c, :],
                                           start=(c == 0), stop=(c == ND - 1))
                        p.mark(ins, "pe")
                    pe_g = p.cnt("prog_pe")
                    s_ou = s_out + str(o % 2)
                    n_out_prev = p.cnt(s_ou)
                    @block.vector
                    def _(v):
                        p.wait(v, "pe", pe_g)
                        v.wait_ge(p.sem(s_lr), n_r)
                        if n_out_prev > 0:
                            v.wait_ge(p.sem(s_ou), n_out_prev)
                        ins = v.scalar_tensor_tensor(t2_sb[:, o % 2, :], bank[:],
                                                     bo_sb[:, o:o + 1], xres_sb[:, o % 4, :],
                                                     OP.add, OP.add)
                        p.mark(ins, "dve")
                    dve_g = p.cnt("prog_dve")
                    @block.sync
                    def _(sy):
                        p.wait(sy, "dve", dve_g)
                        dma(sy, p, s_ou,
                            dst_x[o * 128:(o + 1) * 128, ts * 512:(ts + 1) * 512],
                            t2_sb[:, o % 2, :])
    n_o0 = p.cnt(s_out + "0"); n_o1 = p.cnt(s_out + "1")
    @block.sync
    def _(sy):
        sy.wait_ge(p.sem(s_out + "0"), n_o0)
        sy.wait_ge(p.sem(s_out + "1"), n_o1)
        ins = sy.engine_nop() if hasattr(sy, "engine_nop") else sy.nop()
        h, n = p.inc("a_done", 1)
        ins.then_inc(h, 1)
    return "a_done", p.cnt("a_done")



def emit_phase_c(p, consts, src_x, dst_x, ctx_hbm, w, s_prev=None, n_prev=0):
    nc, block = p.nc, p.block
    s_ldx, s_hw, s_out, s_ow, s_ldc, s_ldr = ("c_ldx", "c_hw", "c_out",
                                              "c_ow", "c_ldc", "c_ldr")
    s_gw = "c_gw"

    with contextlib.ExitStack() as stC:
        stQ = contextlib.ExitStack()
        sbC = lambda n, sh, dt: stQ.enter_context(nc.sbuf_tensor(n, sh, dt))
        h_sb = sbC("c_h", [128, ND, T], BF16)
        fw_sb = sbC("c_fw", [128, 4, 128], BF16)
        fb_sb = sbC("c_fb", [128, 8], F32)
        mask_c = sbC("c_maskc", [128, 128], BF16)

        with contextlib.ExitStack() as stA:
            sbA = lambda n, sh, dt: stA.enter_context(nc.sbuf_tensor(n, sh, dt))
            x_sb = sbA("c_x", [128, ND, 512], F32)
            scr = {
                "tmp": sbA("c_tmp", [128, 512], F32),
            "tmpb": sbA("c_tmpb", [128, 2, 512], BF16),
                "row_m": sbA("c_rm", [1, 512], F32),
                "row_v": sbA("c_rv", [1, 512], F32),
                "mb": sbA("c_mb", [128, 512], F32),
            }
            with contextlib.ExitStack() as stP:
                psA = lambda n, sh: stP.enter_context(nc.psum_tensor(n, sh, F32))
                pb_row0 = psA("c_pr0", [1, 512])
                pb_row1 = psA("c_pr1", [1, 512])
                pb_bc = psA("c_pbc", [128, 512])

                @block.sync
                def _(sy):
                    if s_prev is not None:
                        sy.wait_ge(p.sem(s_prev), n_prev)
                    dma(sy, p, s_hw, mask_c[:], w["c_mask_c"])
                    dma(sy, p, s_hw, fb_sb[:, 0:1], w["fq1_b"])
                    dma(sy, p, s_hw, fb_sb[:, 1:2], w["fq2_b"])
                    dma(sy, p, s_hw, fb_sb[:, 2:3], w["fk1_b"])
                    dma(sy, p, s_hw, fb_sb[:, 3:4], w["fk2_b"])
                @block.gpsimd
                def _(g):
                    if s_prev is not None:
                        g.wait_ge(p.sem(s_prev), n_prev)
                    dma(g, p, "c_gf", fw_sb[:, 0, :], w["fq1_wT"])
                    dma(g, p, "c_gf", fw_sb[:, 1, :], w["fq2_wT"])
                    dma(g, p, "c_gf", fw_sb[:, 2, :], w["fk1_wT"])
                    dma(g, p, "c_gf", fw_sb[:, 3, :], w["fk2_wT"])
                n_hw0 = p.cnt(s_hw)
                n_gf = p.cnt("c_gf")
                @block.vector
                def _(v):
                    v.wait_ge(p.sem(s_hw), n_hw0)
                    ins = v.tensor_scalar(fb_sb[:, 4:5], fb_sb[:, 0:1], INV_SQRT_DK,
                                          None, OP.mult)
                    p.mark(ins, "dve")

                for ts in range(NTT):
                    c0 = ts * 512
                    dve_b = p.cnt("prog_dve")
                    pe_b = p.cnt("prog_pe")
                    @block.sync
                    def _(sy):
                        p.wait(sy, "dve", dve_b)
                        p.wait(sy, "pe", pe_b)
                        dma(sy, p, s_ldx, x_sb[:],
                            src_x.rearrange("(c p) t -> p c t", p=128)[:, :, c0:c0 + 512])
                    n_ldx = p.cnt(s_ldx)
                    @block.tensor
                    def _(t):
                        t.wait_ge(p.sem(s_ldx), n_ldx)
                        p.wait(t, "act")
                        p.wait(t, "dve")
                    @block.scalar
                    def _(s):
                        s.wait_ge(p.sem(s_ldx), n_ldx)
                    @block.vector
                    def _(v):
                        v.wait_ge(p.sem(s_ldx), n_ldx)
                    hv = lambda c: h_sb[:, c, c0:c0 + 512]
                    emit_ln_tile(p, consts, x_sb, hv, pb_row0, pb_row1, pb_bc, scr)

        with contextlib.ExitStack() as stH:
            sbH = lambda n, sh, dt: stH.enter_context(nc.sbuf_tensor(n, sh, dt))
            wq_sb = sbH("c_wq", [128, 2, ND, 128], BF16)
            wk_sb = sbH("c_wk", [128, 2, ND, 128], BF16)
            wv_sb = sbH("c_wv", [128, 2, ND, 128], BF16)
            qt_sb = sbH("c_qt", [128, T], BF16)
            kt_sb = sbH("c_kt", [128, T], BF16)
            vt_sb = sbH("c_vt", [128, T], BF16)
            qp_sb = sbH("c_qp", [128, T], BF16)
            kp_sb = kt_sb
            ktm_sb = sbH("c_ktm", [128, 32, 128], BF16)
            vtm_sb = sbH("c_vtm", [128, 32, 128], BF16)
            kv_sb = sbH("c_kv", [128, 128], BF16)
            o_sb = qt_sb
            t1_sb = sbH("c_t1", [128, 512], F32)
            sT_sb = sbH("c_sT", [128, 128], BF16)
            pe_c1 = p.cnt("prog_pe"); dve_c1 = p.cnt("prog_dve"); act_c1 = p.cnt("prog_act")
            with contextlib.ExitStack() as stP:
                psH = lambda n, sh, dt=F32: stP.enter_context(nc.psum_tensor(n, sh, dt))
                pb_p = [psH("c_pp0", [128, 512]), psH("c_pp1", [128, 512])]
                pb_st = psH("c_pst", [128, 128])
                pb_o = [psH("c_po0", [128, 512]), psH("c_po1", [128, 512])]
                pb_sT = [psH("c_pt0", [128, 128])]
                pb_tx = [psH("c_ptx0", [128, 128], BF16), psH("c_ptx1", [128, 128], BF16)]

                wql = {0: 0, 1: 0}
                for hd in range(16):
                    buf = hd % 2
                    s_gwb = s_gw + str(buf)
                    @block.gpsimd
                    def _(g):
                        if hd < 2:
                            p.wait(g, "pe", pe_c1); p.wait(g, "dve", dve_c1)
                            p.wait(g, "act", act_c1)
                        p.wait(g, "pe", wql[buf])
                        dma(g, p, s_gwb, wq_sb[:, buf],
                            w["wq_blk"][:, hd].rearrange("i p m -> p i m"))
                        dma(g, p, s_gwb, wk_sb[:, buf],
                            w["wk_blk"][:, hd].rearrange("i p m -> p i m"))
                        dma(g, p, s_gwb, wv_sb[:, buf],
                            w["wv_blk"][:, hd].rearrange("i p m -> p i m"))
                    n_w = p.cnt(s_gwb)

                    n_ow_head = p.cnt(s_ow)
                    for which, wsb, dst in (("q", wq_sb, qt_sb), ("k", wk_sb, kt_sb),
                                            ("v", wv_sb, vt_sb)):
                        for ts in range(NTT):
                            bank = pb_p[ts % 2]
                            dve_free = p.cnt("prog_dve")
                            act_free = p.cnt("prog_act")
                            @block.tensor
                            def _(t):
                                t.wait_ge(p.sem(s_gwb), n_w)
                                p.wait(t, "dve", dve_free)
                                p.wait(t, "act", act_free)
                                for c in range(ND):
                                    ins = t.matmul(bank[:], wsb[:, buf, c, :],
                                                   h_sb[:, c, ts * 512:(ts + 1) * 512],
                                                   start=(c == 0), stop=(c == ND - 1))
                                p.mark(ins, "pe")
                            pe_g = p.cnt("prog_pe")
                            @block.vector
                            def _(v):
                                p.wait(v, "pe", pe_g)
                                if which == "q" and ts == 0 and n_ow_head > 0:
                                    v.wait_ge(p.sem(s_ow), n_ow_head)
                                ins = v.tensor_copy(dst[:, ts * 512:(ts + 1) * 512], bank[:])
                                p.mark(ins, "dve")
                    wql[buf] = p.cnt("prog_pe")

                    peb3 = p.cnt("prog_pe")
                    actb3 = p.cnt("prog_act")
                    dveb3 = p.cnt("prog_dve")
                    titers = [("q", qt_sb, qp_sb, 0, 1, 4, ts) for ts in range(NTT)] + \
                             [("k", kt_sb, kp_sb, 2, 3, 2, ts) for ts in range(NTT)]
                    def tbanks(k):
                        return (pb_p[0], pb_p[1]) if k % 2 == 0 else (pb_o[0], pb_o[1])
                    @block.tensor
                    def _(t):
                        for k, (which, src_t, dst, i1, i2, bscl, ts) in enumerate(titers):
                            bA, bB = tbanks(k)
                            if k == 0:
                                p.wait(t, "dve", dveb3)
                                p.wait(t, "act", actb3)
                                if hd == 0:
                                    t.wait_ge(p.sem(s_hw), n_hw0)
                                    t.wait_ge(p.sem("c_gf"), n_gf)
                            if k >= 2:
                                t.wait_ge(p.sem("prog_act"), actb3 + k - 1)
                                t.wait_ge(p.sem("prog_dve"), dveb3 + k - 1)
                            t.matmul(bA[:], fw_sb[:, i1, :],
                                     src_t[:, ts * 512:(ts + 1) * 512],
                                     start=True, stop=True, skip_group_check=True)
                            ins = t.matmul(bB[:], fw_sb[:, i2, :],
                                           src_t[:, ts * 512:(ts + 1) * 512],
                                           start=True, stop=True, skip_group_check=True)
                            p.mark(ins, "pe")
                    @block.scalar
                    def _(s):
                        for k, (which, src_t, dst, i1, i2, bscl, ts) in enumerate(titers):
                            bA, bB = tbanks(k)
                            s.wait_ge(p.sem("prog_pe"), peb3 + k + 1)
                            if k >= 1:
                                s.wait_ge(p.sem("prog_dve"), dveb3 + k)
                            if which == "q":
                                ins = s.activation(out=t1_sb[:], in_=bA[:],
                                                   func=AF.Identity,
                                                   bias=fb_sb[:, bscl:bscl + 1],
                                                   scale=INV_SQRT_DK)
                            else:
                                ins = s.activation(out=t1_sb[:], in_=bA[:],
                                                   func=AF.Identity,
                                                   bias=fb_sb[:, bscl:bscl + 1])
                            p.mark(ins, "act")
                    @block.vector
                    def _(v):
                        for k, (which, src_t, dst, i1, i2, bscl, ts) in enumerate(titers):
                            bA, bB = tbanks(k)
                            v.wait_ge(p.sem("prog_act"), actb3 + k + 1)
                            ins = v.scalar_tensor_tensor(
                                dst[:, ts * 512:(ts + 1) * 512], bB[:],
                                fb_sb[:, i2:i2 + 1], t1_sb[:], OP.add, OP.mult)
                            p.mark(ins, "dve")
                    dve_tf = p.cnt("prog_dve")
                    peb4 = p.cnt("prog_pe")
                    dveb4 = p.cnt("prog_dve")
                    txit = [("k", kp_sb, ktm_sb, tc) for tc in range(32)] + \
                           [("v", vt_sb, vtm_sb, tc) for tc in range(32)]
                    @block.tensor
                    def _(t):
                        for k, (which, src_t, dst, tc) in enumerate(txit):
                            if k == 0:
                                p.wait(t, "dve", dveb4)
                            if k >= 2:
                                t.wait_ge(p.sem("prog_dve"), dveb4 + k - 1)
                            ins = t.transpose(pb_tx[k % 2][:],
                                              src_t[:, tc * 128:(tc + 1) * 128],
                                              consts["ident_b"][:])
                            p.mark(ins, "pe")
                    @block.vector
                    def _(v):
                        for k, (which, src_t, dst, tc) in enumerate(txit):
                            v.wait_ge(p.sem("prog_pe"), peb4 + k + 1)
                            ins = v.tensor_copy(dst[:, tc, :], pb_tx[k % 2][:])
                            p.mark(ins, "dve")

                    n_ow_prev = p.cnt(s_ow)
                    dve_o_prev = {0: None, 1: None}
                    pe_upd_prev = None
                    for j in range(32):
                        ob = j % 2
                        q_sl = qp_sb[:, j * 128:(j + 1) * 128]
                        ktm_sl = ktm_sb[:, j, :]
                        vtm_sl = vtm_sb[:, j, :]
                        if j > 0:
                            @block.vector
                            def _(v):
                                p.wait(v, "pe", pe_upd_prev)
                                ins = v.tensor_copy(kv_sb[:], pb_st[:])
                                p.mark(ins, "dve")
                        dve_kv = p.cnt("prog_dve")
                        dve_ofree = dve_o_prev[ob]
                        @block.tensor
                        def _(t):
                            if j == 0:
                                if n_ow_prev > 0:
                                    t.wait_ge(p.sem(s_ow), n_ow_prev)
                                p.wait(t, "dve", dve_tf)
                            if j > 0:
                                p.wait(t, "dve", dve_kv)
                                t.matmul(pb_o[ob][:, 0:128], kv_sb[:], q_sl,
                                         start=True, stop=False, skip_group_check=True)
                            ins = t.matmul(pb_sT[0][:], kp_sb[:, j * 128:(j + 1) * 128],
                                           q_sl, start=True, stop=True,
                                           skip_group_check=True)
                            p.mark(ins, "pe")
                        pe_sT = p.cnt("prog_pe")
                        @block.vector
                        def _(v):
                            p.wait(v, "pe", pe_sT)
                            ins = v.tensor_tensor(sT_sb[:], pb_sT[0][:],
                                                  mask_c[:], OP.mult)
                            p.mark(ins, "dve")
                        dve_sT = p.cnt("prog_dve")
                        @block.tensor
                        def _(t):
                            p.wait(t, "dve", dve_sT)
                            if dve_ofree is not None:
                                p.wait(t, "dve", dve_ofree)
                            t.matmul(pb_o[ob][:, 0:128], vtm_sl, sT_sb[:],
                                     start=(j == 0), stop=True, skip_group_check=True)
                            ins = t.matmul(pb_st[:], ktm_sl, vtm_sl,
                                           start=(j == 0), stop=(j == 31),
                                           skip_group_check=True)
                            p.mark(ins, "pe")
                        pe_upd_prev = p.cnt("prog_pe")
                        @block.vector
                        def _(v):
                            p.wait(v, "pe", pe_upd_prev)
                            ins = v.tensor_copy(o_sb[:, j * 128:(j + 1) * 128],
                                                pb_o[ob][:, 0:128])
                            p.mark(ins, "dve")
                        dve_o_prev[ob] = p.cnt("prog_dve")
                    dve_o_all = p.cnt("prog_dve")
                    @block.sync
                    def _(sy):
                        p.wait(sy, "dve", dve_o_all)
                        dma(sy, p, s_ow,
                            ctx_hbm[hd * 128:(hd + 1) * 128, :], o_sb[:])

        stQ.close()
        n_ow0 = p.cnt(s_ow)
        pe_c3 = p.cnt("prog_pe"); dve_c3 = p.cnt("prog_dve"); act_c3 = p.cnt("prog_act")
        with contextlib.ExitStack() as stO:
            sbO = lambda n, sh, dt: stO.enter_context(nc.sbuf_tensor(n, sh, dt))
            wo_sb = sbO("c_wo", [128, ND, ND, 128], BF16)
            nw_sb = sbO("c_nw", [128, 1], F32)
            ot_sb = sbO("c_ot", [128, 2, ND, 512], BF16)
            sq_sb = sbO("c_sq", [128, 512], BF16)
            bc_sb = sbO("c_bc", [128, 512], F32)
            row_sb = sbO("c_row", [1, 512], F32)
            xres_sb = sbO("c_xres", [128, 4, 512], F32)
            t2_sb = sbO("c_t2", [128, 2, 512], F32)
            with contextlib.ExitStack() as stP:
                psO = lambda n, sh: stP.enter_context(nc.psum_tensor(n, sh, F32))
                pb_oo = [psO("c_poo0", [128, 512]), psO("c_poo1", [128, 512])]
                pb_row = psO("c_prow", [1, 512])
                pb_bc2 = psO("c_pbc2", [128, 512])
                @block.sync
                def _(sy):
                    p.wait(sy, "pe", pe_c3); p.wait(sy, "dve", dve_c3); p.wait(sy, "act", act_c3)
                    sy.wait_ge(p.sem(s_ow), n_ow0)
                    dma(sy, p, s_hw, nw_sb[:], w["norm_w"])
                n_nw = p.cnt(s_hw)
                @block.gpsimd
                def _(g):
                    p.wait(g, "pe", pe_c3); p.wait(g, "dve", dve_c3); p.wait(g, "act", act_c3)
                    g.wait_ge(p.sem(s_ow), n_ow0)
                    for ih in range(4):
                        dma(g, p, s_gw + "0", wo_sb[:, ih * 4:(ih + 1) * 4],
                            w["wo_blk"][ih * 4:(ih + 1) * 4].rearrange("i o p m -> p i o m"))
                n_wo = p.cnt(s_gw + "0")
                for ts in range(NTT):
                    cbuf = ts % 2
                    s_lc = s_ldc + str(cbuf)
                    pe_b = p.cnt("prog_pe")
                    act_b = p.cnt("prog_act")
                    @block.sync
                    def _(sy):
                        p.wait(sy, "pe", pe_b)
                        p.wait(sy, "act", act_b)
                        dma(sy, p, s_lc, ot_sb[:, cbuf],
                            ctx_hbm.rearrange("(c p) t -> p c t", p=128)[:, :, ts * 512:(ts + 1) * 512])
                    n_ldc = p.cnt(s_lc)
                    peb = p.cnt("prog_pe")
                    actb = p.cnt("prog_act")
                    dveb = p.cnt("prog_dve")
                    def pe_row(hh): return peb + 2 * hh + 1
                    def pe_bcm(hh): return peb + 2 * hh + 2
                    def act_sq(hh): return actb + 3 * hh + 1
                    def act_ex(hh): return actb + 3 * hh + 3
                    def dve_cp(hh): return dveb + 2 * hh + 1
                    @block.scalar
                    def _(s):
                        s.wait_ge(p.sem(s_lc), n_ldc)
                        if ts == 0:
                            s.wait_ge(p.sem(s_hw), n_nw)
                        for hh in range(ND):
                            if hh >= 1:
                                s.wait_ge(p.sem("prog_pe"), pe_row(hh - 1))
                            ins = s.activation(out=sq_sb[:], in_=ot_sb[:, cbuf, hh, :],
                                               func=AF.Square)
                            nn = p.mark(ins, "act")
                            s.wait_ge(p.sem("prog_pe"), pe_row(hh) - 1 + 1)
                            ins = s.activation(out=row_sb[:], in_=pb_row[:], func=AF.Ln,
                                               scale=1.0 / 128.0, bias=consts["eps_row"][:])
                            nn = p.mark(ins, "act")
                            s.wait_ge(p.sem("prog_act"), nn)
                            ins = s.activation(out=row_sb[:], in_=row_sb[:], func=AF.Exp,
                                               scale=-0.5)
                            p.mark(ins, "act")
                    @block.tensor
                    def _(t):
                        if ts == 0:
                            p.wait(t, "act", actb)
                            p.wait(t, "dve", dveb)
                        for hh in range(ND):
                            t.wait_ge(p.sem("prog_act"), act_sq(hh))
                            ins = t.matmul(pb_row[:], consts["ones_col_b"][:], sq_sb[:],
                                           start=True, stop=True)
                            p.mark(ins, "pe")
                            t.wait_ge(p.sem("prog_act"), act_ex(hh))
                            if hh >= 1:
                                t.wait_ge(p.sem("prog_dve"), dve_cp(hh - 1))
                            ins = t.matmul(pb_bc2[:], consts["ones_row_f"][:], row_sb[:],
                                           start=True, stop=True)
                            p.mark(ins, "pe")
                    @block.vector
                    def _(v):
                        for hh in range(ND):
                            v.wait_ge(p.sem("prog_pe"), pe_bcm(hh))
                            ins = v.tensor_copy(bc_sb[:], pb_bc2[:])
                            nn = p.mark(ins, "dve")
                            v.wait_ge(p.sem("prog_dve"), nn)
                            ins = v.scalar_tensor_tensor(ot_sb[:, cbuf, hh, :],
                                                         ot_sb[:, cbuf, hh, :],
                                                         nw_sb[:], bc_sb[:],
                                                         OP.mult, OP.mult)
                            nn = p.mark(ins, "dve")
                            v.wait_ge(p.sem("prog_dve"), nn)
                    dve_rms = p.cnt("prog_dve")
                    for o in range(ND):
                        dve_res = p.cnt("prog_dve")
                        s_lr = s_ldr + str(o % 4)
                        @block.scalar
                        def _(s):
                            p.wait(s, "dve", dve_res)
                            dma(s, p, s_lr, xres_sb[:, o % 4, :],
                                src_x[o * 128:(o + 1) * 128, ts * 512:(ts + 1) * 512])
                        n_r = p.cnt(s_lr)
                        bank = pb_oo[o % 2]
                        dve_free = p.cnt("prog_dve")
                        @block.tensor
                        def _(t):
                            t.wait_ge(p.sem(s_gw + "0"), n_wo)
                            p.wait(t, "dve", dve_free)
                            for c in range(ND):
                                ins = t.matmul(bank[:], wo_sb[:, c, o, :],
                                               ot_sb[:, cbuf, c, :],
                                               start=(c == 0), stop=(c == ND - 1))
                            p.mark(ins, "pe")
                        pe_g = p.cnt("prog_pe")
                        s_ou = s_out + str(o % 2)
                        n_out_prev = p.cnt(s_ou)
                        @block.vector
                        def _(v):
                            p.wait(v, "pe", pe_g)
                            v.wait_ge(p.sem(s_lr), n_r)
                            if n_out_prev > 0:
                                v.wait_ge(p.sem(s_ou), n_out_prev)
                            ins = v.tensor_add(t2_sb[:, o % 2, :], bank[:],
                                               xres_sb[:, o % 4, :])
                            p.mark(ins, "dve")
                        dve_g = p.cnt("prog_dve")
                        @block.sync
                        def _(sy):
                            p.wait(sy, "dve", dve_g)
                            dma(sy, p, s_ou,
                                dst_x[o * 128:(o + 1) * 128, ts * 512:(ts + 1) * 512],
                                t2_sb[:, o % 2, :])
    n_o0 = p.cnt(s_out + "0"); n_o1 = p.cnt(s_out + "1")
    @block.sync
    def _(sy):
        sy.wait_ge(p.sem(s_out + "0"), n_o0)
        sy.wait_ge(p.sem(s_out + "1"), n_o1)
        ins = sy.engine_nop() if hasattr(sy, "engine_nop") else sy.nop()
        h, n = p.inc("c_done", 1)
        ins.then_inc(h, 1)
    return "c_done", p.cnt("c_done")


def host_prep_c(inp):
    return {
        "wq_blk": blk(np.asarray(inp["L1_attn_wq"])),
        "wk_blk": blk(np.asarray(inp["L1_attn_wk"])),
        "wv_blk": blk(np.asarray(inp["L1_attn_wv"])),
        "wo_blk": blk(np.asarray(inp["L1_attn_wo"])),
        "fq1_wT": np.ascontiguousarray(np.asarray(inp["L1_attn_fq1_w"]).T),
        "fq2_wT": np.ascontiguousarray(np.asarray(inp["L1_attn_fq2_w"]).T),
        "fk1_wT": np.ascontiguousarray(np.asarray(inp["L1_attn_fk1_w"]).T),
        "fk2_wT": np.ascontiguousarray(np.asarray(inp["L1_attn_fk2_w"]).T),
        "fq1_b": np.asarray(inp["L1_attn_fq1_b"]).reshape(128, 1).copy(),
        "fq2_b": np.asarray(inp["L1_attn_fq2_b"]).reshape(128, 1).copy(),
        "fk1_b": np.asarray(inp["L1_attn_fk1_b"]).reshape(128, 1).copy(),
        "fk2_b": np.asarray(inp["L1_attn_fk2_b"]).reshape(128, 1).copy(),
        "norm_w": np.asarray(inp["L1_attn_norm_w"]).reshape(128, 1).copy(),
    }

C_W_SHAPES = {
    "wq_blk": ([ND, ND, 128, 128], F32),
    "wk_blk": ([ND, ND, 128, 128], F32),
    "wv_blk": ([ND, ND, 128, 128], F32),
    "wo_blk": ([ND, ND, 128, 128], F32),
    "fq1_wT": ([128, 128], F32),
    "fq2_wT": ([128, 128], F32),
    "fk1_wT": ([128, 128], F32),
    "fk2_wT": ([128, 128], F32),
    "fq1_b": ([128, 1], F32),
    "fq2_b": ([128, 1], F32),
    "fk1_b": ([128, 1], F32),
    "fk2_b": ([128, 1], F32),
    "norm_w": ([128, 1], F32),
}


def build_c_test():
    nc = bass.Bass("TRN2", target_bir_lowering=False)
    stack = contextlib.ExitStack()
    xin = nc.dram_tensor("xin", [D, T], F32, kind="ExternalInput").ap()
    xout = nc.dram_tensor("xout", [D, T], F32, kind="ExternalOutput").ap()
    ctx_hbm = nc.dram_tensor("ctx_hbm", [D, T], BF16, kind="Internal").ap()
    cin = {k: nc.dram_tensor(k, sh, dt, kind="ExternalInput").ap()
           for k, (sh, dt) in CONST_SHAPES.items()}
    wc = {k: nc.dram_tensor(k, sh, dt, kind="ExternalInput").ap()
          for k, (sh, dt) in C_W_SHAPES.items()}
    wc["c_mask_c"] = cin["c_mask_c"]
    with stack:
        with nc.Block() as block:
            p = Prog(nc, block, stack)
            consts = emit_consts(p, cin)
            s_done, n_done = emit_phase_c(p, consts, xin, xout, ctx_hbm, wc)
            @block.sync
            def _(sy):
                sy.wait_ge(p.sem(s_done), n_done)
    return nc



def emit_phase_d(p, consts, src_x, dst_x, moe_acc, w, s_prev=None, n_prev=0):
    nc, block = p.nc, p.block
    s_ldx, s_hw, s_out, s_ldr, s_lda = "d_ldx", "d_hw", "d_out", "d_ldr", "d_lda"

    def vsync(v, ins):
        nn = p.mark(ins, "dve")
        v.wait_ge(p.sem("prog_dve"), nn)
        return nn

    with contextlib.ExitStack() as stD:
        sbD = lambda n, sh, dt: stD.enter_context(nc.sbuf_tensor(n, sh, dt))
        h_sb = sbD("d_h", [128, ND, T], BF16)
        w_sb = sbD("d_w", [128, 32, 12], F32)
        s1_sb = sbD("d_s1", [128, 32], F32)

        with contextlib.ExitStack() as stA:
            sbA = lambda n, sh, dt: stA.enter_context(nc.sbuf_tensor(n, sh, dt))
            x_sb = sbA("d_x", [128, ND, 512], F32)
            clf_sb = sbA("d_clf", [128, ND, 12], F32)
            se_sb = sbA("d_se", [12, 1], F32)
            eb_sb = sbA("d_eb", [128, 12], F32)
            row12 = sbA("d_r12", [1, 16], F32)
            mb12 = sbA("d_mb12", [12, 512], F32)
            lg_sb = sbA("d_lg", [12, 512], F32)
            rt_sb = sbA("d_rt", [128, 12], F32)
            pr_sb = sbA("d_pr", [128, 12], F32)
            sc1 = sbA("d_sc1", [128, 12], F32)
            sc2 = sbA("d_sc2", [128, 12], F32)
            rr_sb = sbA("d_rr2", [128, 4], F32)
            scr = {
                "tmp": sbA("d_tmp", [128, 512], F32),
            "tmpb": sbA("d_tmpb", [128, 2, 512], BF16),
                "row_m": sbA("d_rm", [1, 512], F32),
                "row_v": sbA("d_rv", [1, 512], F32),
                "mb": sbA("d_mb", [128, 512], F32),
            }
            with contextlib.ExitStack() as stP:
                psA = lambda n, sh: stP.enter_context(nc.psum_tensor(n, sh, F32))
                pb_row0 = psA("d_pr0", [1, 512])
                pb_row1 = psA("d_pr1", [1, 512])
                pb_bc = psA("d_pbc", [128, 512])
                pb_lg = psA("d_plg", [12, 512])
                pb_t = psA("d_pt", [128, 12])

                @block.sync
                def _(sy):
                    if s_prev is not None:
                        sy.wait_ge(p.sem(s_prev), n_prev)
                    dma(sy, p, s_hw, clf_sb[:], w["clfT"].rearrange("(c p) e -> p c e", p=128))
                    dma(sy, p, s_hw, eb_sb[0:1, :], w["ebias_row"])
                n_hw0 = p.cnt(s_hw)
                @block.tensor
                def _(t):
                    t.wait_ge(p.sem(s_hw), n_hw0)
                    p.wait(t, "act")
                    p.wait(t, "dve")
                    for c in range(ND):
                        t.matmul(pb_t[0:12, 0:1], clf_sb[:, c, :],
                                 consts["ones_col_f"][:], start=(c == 0),
                                 stop=(c == ND - 1), skip_group_check=True)
                    ins = t.matmul(pb_bc[:, 0:12], consts["ones_row_f"][:], eb_sb[0:1, :],
                                   start=True, stop=True, skip_group_check=True)
                    p.mark(ins, "pe")
                pe_eb = p.cnt("prog_pe")
                @block.vector
                def _(v):
                    p.wait(v, "pe", pe_eb)
                    v.tensor_copy(se_sb[:], pb_t[0:12, 0:1])
                    ins = v.tensor_copy(eb_sb[:], pb_bc[:, 0:12])
                    vsync(v, ins)
                dve_pre = p.cnt("prog_dve")

                for ts in range(NTT):
                    c0 = ts * 512
                    dve_b = p.cnt("prog_dve")
                    pe_b = p.cnt("prog_pe")
                    @block.sync
                    def _(sy):
                        p.wait(sy, "dve", dve_b)
                        p.wait(sy, "pe", pe_b)
                        dma(sy, p, s_ldx, x_sb[:],
                            src_x.rearrange("(c p) t -> p c t", p=128)[:, :, c0:c0 + 512])
                    n_ldx = p.cnt(s_ldx)
                    @block.tensor
                    def _(t):
                        t.wait_ge(p.sem(s_ldx), n_ldx)
                        p.wait(t, "act")
                        p.wait(t, "dve")
                    @block.scalar
                    def _(s):
                        s.wait_ge(p.sem(s_ldx), n_ldx)
                    @block.vector
                    def _(v):
                        v.wait_ge(p.sem(s_ldx), n_ldx)
                    hv = lambda c: h_sb[:, c, c0:c0 + 512]
                    dve_h = emit_ln_tile(p, consts, x_sb, hv, pb_row0, pb_row1, pb_bc, scr)

                    @block.tensor
                    def _(t):
                        p.wait(t, "dve", dve_h)
                        for c in range(ND):
                            ins = t.matmul(pb_lg[:], clf_sb[:, c, :], x_sb[:, c, :],
                                           start=(c == 0), stop=(c == ND - 1))
                        ins = t.matmul(pb_t[:, 0:12] if False else pb_bc[:, 0:512],
                                       consts["ones_row_f"][:], scr["row_m"][:],
                                       start=True, stop=True)
                        p.mark(ins, "pe")
                    pe_lg = p.cnt("prog_pe")
                    @block.vector
                    def _(v):
                        p.wait(v, "pe", pe_lg)
                        ins = v.tensor_copy(mb12[:], pb_bc[0:12, 0:512])
                        vsync(v, ins)
                        ins = v.tensor_scalar(mb12[:], mb12[:], se_sb[0:12, 0:1], None,
                                              OP.mult)
                        vsync(v, ins)
                        ins = v.tensor_sub(lg_sb[:], pb_lg[:], mb12[:])
                        vsync(v, ins)
                        p.mark(v.engine_nop(), "dve")
                    dve_lgc = p.cnt("prog_dve")
                    @block.tensor
                    def _(t):
                        p.wait(t, "dve", dve_lgc)
                        ins = t.matmul(pb_bc[:, 0:512], consts["ones_row_f"][:],
                                       scr["row_v"][:], start=True, stop=True)
                        p.mark(ins, "pe")
                    pe_rb = p.cnt("prog_pe")
                    @block.vector
                    def _(v):
                        p.wait(v, "pe", pe_rb)
                        ins = v.tensor_tensor(lg_sb[:], lg_sb[:], pb_bc[0:12, 0:512],
                                              OP.mult)
                        vsync(v, ins)
                    dve_lg = p.cnt("prog_dve")

                    for sub in range(4):
                        slot = ts * 4 + sub
                        dve_now = p.cnt("prog_dve")
                        @block.tensor
                        def _(t):
                            p.wait(t, "dve", max(dve_lg, dve_now))
                            ins = t.transpose(pb_t[:], lg_sb[:, sub * 128:(sub + 1) * 128],
                                              consts["ident_f"][0:12, 0:12])
                            p.mark(ins, "pe")
                        pe_tr = p.cnt("prog_pe")
                        @block.vector
                        def _(v):
                            p.wait(v, "pe", pe_tr)
                            ins = v.tensor_copy(rt_sb[:], pb_t[:])
                            vsync(v, ins)
                            ins = v.reduce_max(rr_sb[:, 0:1], rt_sb[:], axis=AX.X)
                            vsync(v, ins)
                            ins = v.tensor_scalar(rr_sb[:, 1:2], rr_sb[:, 0:1], -1.0,
                                                  None, OP.mult)
                            vsync(v, ins)
                        dve_mx = p.cnt("prog_dve")
                        @block.scalar
                        def _(s):
                            p.wait(s, "dve", dve_mx)
                            ins = s.activation(out=pr_sb[:], in_=rt_sb[:], func=AF.Exp,
                                               bias=rr_sb[:, 1:2])
                            p.mark(ins, "act")
                        act_ex = p.cnt("prog_act")
                        @block.vector
                        def _(v):
                            p.wait(v, "act", act_ex)
                            ins = v.reduce_sum(rr_sb[:, 2:3], pr_sb[:], axis=AX.X)
                            vsync(v, ins)
                            ins = v.reciprocal(rr_sb[:, 3:4], rr_sb[:, 2:3])
                            vsync(v, ins)
                            ins = v.tensor_scalar(pr_sb[:], pr_sb[:], rr_sb[:, 3:4],
                                                  None, OP.mult)
                            vsync(v, ins)
                            ins = v.tensor_add(sc1[:], pr_sb[:], eb_sb[:])
                            vsync(v, ins)
                            ins = v.reduce_max(rr_sb[:, 0:1], sc1[:], axis=AX.X)
                            vsync(v, ins)
                            ins = v.tensor_scalar(sc2[:], sc1[:], rr_sb[:, 0:1], None,
                                                  OP.is_ge)
                            vsync(v, ins)
                            ins = v.scalar_tensor_tensor(sc1[:], sc2[:], -1e30, sc1[:],
                                                         OP.mult, OP.add)
                            vsync(v, ins)
                            ins = v.reduce_max(rr_sb[:, 1:2], sc1[:], axis=AX.X)
                            vsync(v, ins)
                            ins = v.tensor_scalar(sc1[:], sc1[:], rr_sb[:, 1:2], None,
                                                  OP.is_ge)
                            vsync(v, ins)
                            ins = v.tensor_add(sc2[:], sc2[:], sc1[:])
                            vsync(v, ins)
                            ins = v.scalar_tensor_tensor(w_sb[:, slot, :], sc2[:], 1.5,
                                                         pr_sb[:], OP.mult, OP.mult)
                            vsync(v, ins)
                            ins = v.reduce_sum(s1_sb[:, slot:slot + 1],
                                               w_sb[:, slot, 8:12], axis=AX.X)
                            vsync(v, ins)
                            ins = v.tensor_scalar(s1_sb[:, slot:slot + 1],
                                                  s1_sb[:, slot:slot + 1], 1.0, None,
                                                  OP.add)
                            vsync(v, ins)

        pe_d0 = p.cnt("prog_pe"); dve_d0 = p.cnt("prog_dve"); act_d0 = p.cnt("prog_act")
        with contextlib.ExitStack() as stE:
            sbE = lambda n, sh, dt: stE.enter_context(nc.sbuf_tensor(n, sh, dt))
            wg_sb = sbE("d_wg", [128, 2, ND, 128], BF16)
            wu_sb = sbE("d_wu", [128, 2, ND, 128], BF16)
            wd_sb = sbE("d_wd", [128, 2, 8, 128], BF16)
            act_sb = sbE("d_act", [128, 8, T // 2], BF16)
            wbc_sb = sbE("d_wbc", [128, NTT, 512], BF16)
            wrow_sb = sbE("d_wrow", [1, 512], F32)
            silu_sb = sbE("d_silu", [128, 512], F32)
            t3_sb = silu_sb
            dout_sb = sbE("d_dout", [128, 6, 512], F32)
            with contextlib.ExitStack() as stP:
                psE = lambda n, sh: stP.enter_context(nc.psum_tensor(n, sh, F32))
                pb_g = [psE("d_pg0", [128, 512]), psE("d_pg1", [128, 512])]
                pb_u = [psE("d_pu0", [128, 512]), psE("d_pu1", [128, 512])]
                pb_d = [psE("d_pd0", [128, 512]), psE("d_pd1", [128, 512])]
                pb_row = psE("d_prow", [1, 512])
                pb_bc2 = psE("d_pbc2", [128, 512])

                n_acc_prev = {i: 0 for i in range(8)}
                acc_done = {}
                pref_n_wg = {}; pref_n_wu = {}
                acc_cnt = {i: 0 for i in range(8)}
                gul = {0: 0, 1: 0}
                dl = {0: 0, 1: 0}
                for e in range(8):
                    for ts in range(NTT):
                        dve_f = p.cnt("prog_dve")
                        @block.tensor
                        def _(t):
                            if e == 0 and ts == 0:
                                p.wait(t, "pe", pe_d0)
                                p.wait(t, "act", act_d0)
                            p.wait(t, "dve", dve_f)
                            for sub in range(4):
                                ins = t.matmul(pb_row[0:1, sub * 128:(sub + 1) * 128],
                                               w_sb[:, ts * 4 + sub, e:e + 1],
                                               consts["ident_f"][:],
                                               start=True, stop=True,
                                               skip_group_check=True)
                            p.mark(ins, "pe")
                        pe_r = p.cnt("prog_pe")
                        @block.vector
                        def _(v):
                            p.wait(v, "pe", pe_r)
                            ins = v.tensor_copy(wrow_sb[:], pb_row[:])
                            nn = p.mark(ins, "dve")
                            v.wait_ge(p.sem("prog_dve"), nn)
                        dve_r = p.cnt("prog_dve")
                        @block.tensor
                        def _(t):
                            p.wait(t, "dve", dve_r)
                            ins = t.matmul(pb_bc2[:], consts["ones_row_f"][:], wrow_sb[:],
                                           start=True, stop=True)
                            p.mark(ins, "pe")
                        pe_b2 = p.cnt("prog_pe")
                        @block.vector
                        def _(v):
                            p.wait(v, "pe", pe_b2)
                            ins = v.tensor_copy(wbc_sb[:, ts, :], pb_bc2[:])
                            p.mark(ins, "dve")
                    for th in range(2):
                        peb = p.cnt("prog_pe")
                        actb = p.cnt("prog_act")
                        dveb = p.cnt("prog_dve")
                        n_wg = dict(pref_n_wg); n_wu = dict(pref_n_wu)
                        pref_n_wg.clear(); pref_n_wu.clear()
                        for f in range(8):
                            if f in n_wg:
                                continue
                            buf = f % 2
                            s_gg = "d_gg" + str(buf); s_gu = "d_gu" + str(buf)
                            @block.gpsimd
                            def _(g):
                                if e == 0 and th == 0 and f < 2:
                                    p.wait(g, "pe", pe_d0); p.wait(g, "dve", dve_d0)
                                    p.wait(g, "act", act_d0)
                                if f >= 2:
                                    g.wait_ge(p.sem("prog_pe"), peb + 4 * (f - 1))
                                elif gul[buf] > 0:
                                    p.wait(g, "pe", gul[buf])
                                dma(g, p, s_gg, wg_sb[:, buf],
                                    w["gate_blk"][e][:, f].rearrange("i p m -> p i m"))
                                dma(g, p, s_gu, wu_sb[:, buf],
                                    w["up_blk"][e][:, f].rearrange("i p m -> p i m"))
                            n_wg[f] = p.cnt(s_gg); n_wu[f] = p.cnt(s_gu)
                        def pe_gu(f, tsr):
                            return peb + 4 * f + tsr + 1
                        def act_si(f, tsr):
                            return actb + 4 * f + tsr + 1
                        def dve_aw(f, tsr):
                            return dveb + 2 * (4 * f + tsr) + 2
                        @block.tensor
                        def _(t):
                            for f in range(8):
                                buf = f % 2
                                t.wait_ge(p.sem("d_gg" + str(buf)), n_wg[f])
                                t.wait_ge(p.sem("d_gu" + str(buf)), n_wu[f])
                                if f == 0 and th == 0:
                                    p.wait(t, "act")
                                    p.wait(t, "dve")
                                for tsr in range(4):
                                    ts = th * 4 + tsr
                                    pf, pt = (f, tsr - 2) if tsr >= 2 else (f - 1, tsr + 2)
                                    if pf >= 0:
                                        t.wait_ge(p.sem("prog_act"), act_si(pf, pt))
                                        t.wait_ge(p.sem("prog_dve"), dve_aw(pf, pt))
                                    elif th > 0 or e > 0:
                                        p.wait(t, "act")
                                        p.wait(t, "dve")
                                    pg = pb_g[tsr % 2]; pu = pb_u[tsr % 2]
                                    for c in range(ND):
                                        t.matmul(pg[:], wg_sb[:, buf, c, :],
                                                 h_sb[:, c, ts * 512:(ts + 1) * 512],
                                                 start=(c == 0), stop=(c == ND - 1))
                                    for c in range(ND):
                                        ins = t.matmul(pu[:], wu_sb[:, buf, c, :],
                                                       h_sb[:, c, ts * 512:(ts + 1) * 512],
                                                       start=(c == 0), stop=(c == ND - 1))
                                    p.mark(ins, "pe")
                        gul[0] = peb + 4 * 6 + 4
                        gul[1] = peb + 4 * 7 + 4
                        @block.scalar
                        def _(s):
                            for f in range(8):
                                for tsr in range(4):
                                    k = 4 * f + tsr
                                    s.wait_ge(p.sem("prog_pe"), pe_gu(f, tsr))
                                    if k >= 1:
                                        s.wait_ge(p.sem("prog_dve"), dveb + 2 * (k - 1) + 2)
                                    ins = s.activation(out=silu_sb[:], in_=pb_g[tsr % 2][:],
                                                       func=AF.Silu)
                                    p.mark(ins, "act")
                        @block.vector
                        def _(v):
                            for f in range(8):
                                for tsr in range(4):
                                    ts = th * 4 + tsr
                                    v.wait_ge(p.sem("prog_act"), act_si(f, tsr))
                                    ins = v.tensor_tensor(t3_sb[:], silu_sb[:],
                                                          pb_u[tsr % 2][:], OP.mult)
                                    nn = p.mark(ins, "dve")
                                    v.wait_ge(p.sem("prog_dve"), nn)
                                    ins = v.tensor_tensor(
                                        act_sb[:, f, tsr * 512:(tsr + 1) * 512],
                                        t3_sb[:], wbc_sb[:, ts, :], OP.mult)
                                    nn = p.mark(ins, "dve")
                                    v.wait_ge(p.sem("prog_dve"), nn)
                        peb2 = p.cnt("prog_pe")
                        dveb2 = p.cnt("prog_dve")
                        n_wd = {}
                        def emit_accums(g, o):
                            for tsr in range(4):
                                ts = th * 4 + tsr
                                ring = (4 * o + tsr) % 6
                                g.wait_ge(p.sem("prog_dve"), dveb2 + 4 * o + tsr + 1)
                                dma(g, p, s_lda + str(ring),
                                    moe_acc[o * 128:(o + 1) * 128,
                                            ts * 512:(ts + 1) * 512],
                                    dout_sb[:, ring, :],
                                    **({} if e == 0 else {"accum_op": OP.add}))
                                if e == 7:
                                    acc_done[(o, ts)] = (s_lda + str(ring),
                                                        p.cnt(s_lda + str(ring)))
                        ne, nth = (e, 1) if th == 0 else (e + 1, 0)
                        for o in range(ND):
                            buf = o % 2
                            s_gd = "d_gd" + str(buf)
                            @block.gpsimd
                            def _(g):
                                if e > 0 and th == 0 and o == 1:
                                    for i in range(8):
                                        g.wait_ge(p.sem(s_lda + str(i)), n_acc_prev[i])
                                if o >= 1:
                                    emit_accums(g, o - 1)
                                if o >= 2:
                                    g.wait_ge(p.sem("prog_pe"), peb2 + 4 * (o - 1))
                                elif dl[buf] > 0:
                                    p.wait(g, "pe", dl[buf])
                                dma(g, p, s_gd, wd_sb[:, buf],
                                    w["down_blk"][e][:, o].rearrange("i p m -> p i m"))
                                if o in (2, 3) and ne < 8:
                                    fb = o - 2
                                    g.wait_ge(p.sem("prog_pe"), gul[fb])
                                    dma(g, p, "d_gg" + str(fb), wg_sb[:, fb],
                                        w["gate_blk"][ne][:, fb].rearrange("i p m -> p i m"))
                                    dma(g, p, "d_gu" + str(fb), wu_sb[:, fb],
                                        w["up_blk"][ne][:, fb].rearrange("i p m -> p i m"))
                                    pref_n_wg[fb] = p.cnt("d_gg" + str(fb))
                                    pref_n_wu[fb] = p.cnt("d_gu" + str(fb))
                            n_wd[o] = p.cnt(s_gd)
                        @block.gpsimd
                        def _(g):
                            emit_accums(g, ND - 1)
                        def pe_dn(o, tsr):
                            return peb2 + 4 * o + tsr + 1
                        @block.tensor
                        def _(t):
                            for o in range(ND):
                                buf = o % 2
                                t.wait_ge(p.sem("d_gd" + str(buf)), n_wd[o])
                                if o == 0:
                                    p.wait(t, "dve")
                                for tsr in range(4):
                                    pf, pt = (o, tsr - 2) if tsr >= 2 else (o - 1, tsr + 2)
                                    if pf >= 0:
                                        t.wait_ge(p.sem("prog_dve"),
                                                  dveb2 + 4 * pf + pt + 1)
                                    pd = pb_d[tsr % 2]
                                    for f in range(8):
                                        ins = t.matmul(pd[:], wd_sb[:, buf, f, :],
                                                       act_sb[:, f, tsr * 512:(tsr + 1) * 512],
                                                       start=(f == 0), stop=(f == 7))
                                    p.mark(ins, "pe")
                        dl[0] = peb2 + 4 * 14 + 4
                        dl[1] = peb2 + 4 * 15 + 4
                        @block.vector
                        def _(v):
                            for o in range(ND):
                                for tsr in range(4):
                                    ring = (4 * o + tsr) % 6
                                    s_ac = s_lda + str(ring)
                                    v.wait_ge(p.sem("prog_pe"), pe_dn(o, tsr))
                                    na = acc_cnt[ring]
                                    if na > 0:
                                        v.wait_ge(p.sem(s_ac), na)
                                    ins = v.tensor_copy(dout_sb[:, ring, :], pb_d[tsr % 2][:])
                                    p.mark(ins, "dve")
                                    acc_cnt[ring] += 16
                    for i in range(8):
                        n_acc_prev[i] = p.cnt(s_lda + str(i))
                    for i in range(8):
                        n_acc_prev[i] = p.cnt(s_lda + str(i))

        pe_d1 = p.cnt("prog_pe"); dve_d1 = p.cnt("prog_dve"); act_d1 = p.cnt("prog_act")
        n_acc_fin = {i: p.cnt(s_lda + str(i)) for i in range(6)}
        with contextlib.ExitStack() as stF:
            sbF = lambda n, sh, dt: stF.enter_context(nc.sbuf_tensor(n, sh, dt))
            ma_sb = sbF("d_ma", [128, 6, 512], F32)
            xres_sb = sbF("d_xres", [128, 6, 512], F32)
            s1b_sb = sbF("d_s1b", [128, 8, 512], F32)
            srow_sb = sbF("d_srow", [1, 512], F32)
            t2_sb = sbF("d_t2", [128, 4, 512], F32)
            with contextlib.ExitStack() as stP:
                psF = lambda n, sh: stP.enter_context(nc.psum_tensor(n, sh, F32))
                pb_row2 = psF("d_prow2", [1, 512])
                pb_bc3 = psF("d_pbc3", [128, 512])
                ma_guard = {i: 0 for i in range(6)}
                peb5 = p.cnt("prog_pe")
                dveb5 = p.cnt("prog_dve")
                @block.tensor
                def _(t):
                    p.wait(t, "pe", pe_d1)
                    p.wait(t, "act", act_d1)
                    p.wait(t, "dve", dve_d1)
                    for ts in range(NTT):
                        if ts >= 1:
                            t.wait_ge(p.sem("prog_dve"), dveb5 + 2 * (ts - 1) + 1)
                        for sub in range(4):
                            ins = t.matmul(pb_row2[0:1, sub * 128:(sub + 1) * 128],
                                           s1_sb[:, ts * 4 + sub:ts * 4 + sub + 1],
                                           consts["ident_f"][:], start=True, stop=True,
                                           skip_group_check=True)
                        p.mark(ins, "pe")
                        t.wait_ge(p.sem("prog_dve"), dveb5 + 2 * ts + 1)
                        ins = t.matmul(pb_bc3[:], consts["ones_row_f"][:], srow_sb[:],
                                       start=True, stop=True)
                        p.mark(ins, "pe")
                @block.vector
                def _(v):
                    for ts in range(NTT):
                        v.wait_ge(p.sem("prog_pe"), peb5 + 2 * ts + 1)
                        ins = v.tensor_copy(srow_sb[:], pb_row2[:])
                        nn = p.mark(ins, "dve")
                        v.wait_ge(p.sem("prog_dve"), nn)
                        v.wait_ge(p.sem("prog_pe"), peb5 + 2 * ts + 2)
                        ins = v.tensor_copy(s1b_sb[:, ts, :], pb_bc3[:])
                        nn = p.mark(ins, "dve")
                        v.wait_ge(p.sem("prog_dve"), nn)
                dve_s1b = p.cnt("prog_dve")
                for ts in range(NTT):
                    for o in range(ND):
                        ring = (ts * ND + o) % 6
                        s_ma = "d_ma" + str(ring)
                        s_lr = s_ldr + str(ring)
                        g = ma_guard[ring]
                        @block.sync
                        def _(sy):
                            if ts == 0 and o == 0:
                                for i in range(6):
                                    sy.wait_ge(p.sem(s_lda + str(i)), n_acc_fin[i])
                                p.wait(sy, "pe", pe_d1)
                                p.wait(sy, "dve", dve_d1)
                                p.wait(sy, "act", act_d1)
                            if g > 0:
                                sy.wait_ge(p.sem("prog_dve"), g)
                            dma(sy, p, s_ma, ma_sb[:, ring, :],
                                moe_acc[o * 128:(o + 1) * 128, ts * 512:(ts + 1) * 512])
                        n_ma = p.cnt(s_ma)
                        @block.gpsimd
                        def _(gg):
                            if g > 0:
                                gg.wait_ge(p.sem("prog_dve"), g)
                            dma(gg, p, s_lr, xres_sb[:, ring, :],
                                src_x[o * 128:(o + 1) * 128, ts * 512:(ts + 1) * 512])
                        n_r = p.cnt(s_lr)
                        s_ou = s_out + str(o % 4)
                        n_out_prev = p.cnt(s_ou)
                        @block.vector
                        def _(v):
                            v.wait_ge(p.sem(s_ma), n_ma)
                            v.wait_ge(p.sem(s_lr), n_r)
                            p.wait(v, "dve", dve_s1b)
                            if n_out_prev > 0:
                                v.wait_ge(p.sem(s_ou), n_out_prev)
                            ins = v.tensor_tensor(t2_sb[:, o % 4, :], xres_sb[:, ring, :],
                                                  s1b_sb[:, ts, :], OP.mult)
                            nn = p.mark(ins, "dve")
                            v.wait_ge(p.sem("prog_dve"), nn)
                            ins = v.tensor_add(t2_sb[:, o % 4, :], t2_sb[:, o % 4, :],
                                               ma_sb[:, ring, :])
                            p.mark(ins, "dve")
                        dve_g = p.cnt("prog_dve")
                        ma_guard[ring] = dve_g
                        @block.scalar
                        def _(sc):
                            p.wait(sc, "dve", dve_g)
                            dma(sc, p, s_ou,
                                dst_x[o * 128:(o + 1) * 128, ts * 512:(ts + 1) * 512],
                                t2_sb[:, o % 4, :])
    n_os = {i: p.cnt(s_out + str(i)) for i in range(4)}
    @block.sync
    def _(sy):
        for i in range(4):
            sy.wait_ge(p.sem(s_out + str(i)), n_os[i])
        ins = sy.engine_nop() if hasattr(sy, "engine_nop") else sy.nop()
        h, n = p.inc("d_done", 1)
        ins.then_inc(h, 1)
    return "d_done", p.cnt("d_done")


def host_prep_d(inp):
    gate = np.asarray(inp["L1_ffn_gate_w"])
    up = np.asarray(inp["L1_ffn_up_w"])
    down = np.asarray(inp["L1_ffn_down_w"])
    return {
        "clfT": np.ascontiguousarray(np.asarray(inp["L1_ffn_classifier_w"]).T),
        "ebias_row": np.asarray(inp["L1_ffn_e_score_correction_bias"]).reshape(1, 12).copy(),
        "gate_blk": np.stack([blk(gate[e]) for e in range(8)]),
        "up_blk": np.stack([blk(up[e]) for e in range(8)]),
        "down_blk": np.stack([blk(down[e]) for e in range(8)]),
    }

D_W_SHAPES = {
    "clfT": ([D, 12], F32),
    "ebias_row": ([1, 12], F32),
    "gate_blk": ([8, ND, 8, 128, 128], F32),
    "up_blk": ([8, ND, 8, 128, 128], F32),
    "down_blk": ([8, 8, ND, 128, 128], F32),
}


def build_d_test():
    nc = bass.Bass("TRN2", target_bir_lowering=False)
    stack = contextlib.ExitStack()
    xin = nc.dram_tensor("xin", [D, T], F32, kind="ExternalInput").ap()
    xout = nc.dram_tensor("xout", [D, T], F32, kind="ExternalOutput").ap()
    moe_acc = nc.dram_tensor("moe_acc", [D, T], F32, kind="Internal").ap()
    cin = {k: nc.dram_tensor(k, sh, dt, kind="ExternalInput").ap()
           for k, (sh, dt) in CONST_SHAPES.items()}
    wd = {k: nc.dram_tensor(k, sh, dt, kind="ExternalInput").ap()
          for k, (sh, dt) in D_W_SHAPES.items()}
    with stack:
        with nc.Block() as block:
            p = Prog(nc, block, stack)
            consts = emit_consts(p, cin)
            s_done, n_done = emit_phase_d(p, consts, xin, xout, moe_acc, wd)
            @block.sync
            def _(sy):
                sy.wait_ge(p.sem(s_done), n_done)
    return nc



def emit_consts(p, cin):
    block = p.block
    consts = {
        "ones_col_f": p.sb("k_ones_cf", [128, 1], F32),
        "ones_col_b": p.sb("k_ones_cb", [128, 1], BF16),
        "ones_row_f": p.sb("k_ones_rf", [1, 128], F32),
        "ident_f": p.sb("k_id_f", [128, 128], F32),
        "ident_b": p.sb("k_id_b", [128, 128], BF16),
        "eps_row": p.sb("k_eps", [1, 1], F32),
    }
    @block.sync
    def _(sy):
        dma(sy, p, "k_const", consts["ident_f"][:], cin["c_ident_f"])
        dma(sy, p, "k_const", consts["ident_b"][:], cin["c_ident_b"])
    @block.gpsimd
    def _(g):
        g.memset(consts["eps_row"][:], 1e-5)
        g.memset(consts["ones_col_f"][:], 1.0)
        g.memset(consts["ones_col_b"][:], 1.0)
        ins = g.memset(consts["ones_row_f"][:], 1.0)
        p.mark(ins, "pool")
    n_c = p.cnt("k_const")
    n_p = p.cnt("prog_pool")
    for name in ("tensor", "scalar", "vector", "sync", "gpsimd"):
        sec = getattr(block, name)
        @sec
        def _(e):
            e.wait_ge(p.sem("k_const"), n_c)
            e.wait_ge(p.sem("prog_pool"), n_p)
    return consts



def host_prep_b(inp):
    import ml_dtypes
    return {
        "WgT": np.ascontiguousarray(np.asarray(inp["L0_ffn_Wg"]).T),
        "W0a_blk": blk(np.asarray(inp["L0_ffn_W0a"])),
        "W0b_blk": blk(np.asarray(inp["L0_ffn_W0b"])),
        "W1a_blk": blk(np.asarray(inp["L0_ffn_W1a"])),
        "W1b_blk": blk(np.asarray(inp["L0_ffn_W1b"])),
        "b0a_T": vecT(np.asarray(inp["L0_ffn_b0a"])),
        "b1a_T": vecT(np.asarray(inp["L0_ffn_b1a"])),
        "b0b_T": vecT(np.asarray(inp["L0_ffn_b0b"])),
        "b1b_T": vecT(np.asarray(inp["L0_ffn_b1b"])),
    }

B_W_SHAPES = {
    "WgT": ([D, 2], F32),
    "W0a_blk": ([ND, 32, 128, 128], F32),
    "W0b_blk": ([32, ND, 128, 128], F32),
    "W1a_blk": ([ND, 32, 128, 128], F32),
    "W1b_blk": ([32, ND, 128, 128], F32),
    "b0a_T": ([128, 32], F32),
    "b1a_T": ([128, 32], F32),
    "b0b_T": ([128, 16], F32),
    "b1b_T": ([128, 16], F32),
}

CONST_SHAPES = {
    "c_ident_f": ([128, 128], F32),
    "c_ident_b": ([128, 128], BF16),
    "c_mask_a": ([4, 128, 512], BF16),
    "c_mask_c": ([128, 128], BF16),
}


def build_b_test():
    nc = bass.Bass("TRN2", target_bir_lowering=False)
    stack = contextlib.ExitStack()
    xin = nc.dram_tensor("xin", [D, T], F32, kind="ExternalInput").ap()
    xout = nc.dram_tensor("xout", [D, T], F32, kind="ExternalOutput").ap()
    cin = {k: nc.dram_tensor(k, sh, dt, kind="ExternalInput").ap()
           for k, (sh, dt) in CONST_SHAPES.items()}
    wb = {k: nc.dram_tensor(k, sh, dt, kind="ExternalInput").ap()
          for k, (sh, dt) in B_W_SHAPES.items()}
    with stack:
        with nc.Block() as block:
            p = Prog(nc, block, stack)
            consts = emit_consts(p, cin)
            s_out, n_out = emit_phase_b(p, consts, xin, xout, wb)
            @block.sync
            def _(sy):
                sy.wait_ge(p.sem(s_out), n_out)
    return nc


def host_prep_a(inp):
    return {
        "Wqd_blk": blk(np.asarray(inp["L0_attn_Wq_down"])),
        "Wkvd_blk": blk(np.asarray(inp["L0_attn_Wkv_down"])),
        "Wqup_blk": blk(np.asarray(inp["L0_attn_Wq_up"])),
        "Wkup_blk": blk(np.asarray(inp["L0_attn_Wk_up"])),
        "Wvup_blk": blk(np.asarray(inp["L0_attn_Wv_up"])),
        "Wo_blk": blk(np.asarray(inp["L0_attn_Wo"])),
        "bo_T": vecT(np.asarray(inp["L0_attn_bo"])),
    }

A_W_SHAPES = {
    "Wqd_blk": ([16, 8, 128, 128], F32),
    "Wkvd_blk": ([16, 4, 128, 128], F32),
    "Wqup_blk": ([8, 16, 128, 128], F32),
    "Wkup_blk": ([4, 16, 128, 128], F32),
    "Wvup_blk": ([4, 16, 128, 128], F32),
    "Wo_blk": ([16, 16, 128, 128], F32),
    "bo_T": ([128, 16], F32),
}


def build_a_test():
    nc = bass.Bass("TRN2", target_bir_lowering=False)
    stack = contextlib.ExitStack()
    xin = nc.dram_tensor("xin", [D, T], F32, kind="ExternalInput").ap()
    xout = nc.dram_tensor("xout", [D, T], F32, kind="ExternalOutput").ap()
    ctx_hbm = nc.dram_tensor("ctx_hbm", [D, T], BF16, kind="Internal").ap()
    cin = {k: nc.dram_tensor(k, sh, dt, kind="ExternalInput").ap()
           for k, (sh, dt) in CONST_SHAPES.items()}
    wa = {k: nc.dram_tensor(k, sh, dt, kind="ExternalInput").ap()
          for k, (sh, dt) in A_W_SHAPES.items()}
    wa["c_mask_a"] = cin["c_mask_a"]
    with stack:
        with nc.Block() as block:
            p = Prog(nc, block, stack)
            consts = emit_consts(p, cin)
            s_out, n_out = emit_phase_a(p, consts, xin, xout, ctx_hbm, wa)
            @block.sync
            def _(sy):
                sy.wait_ge(p.sem(s_out), n_out)
    return nc



ALL_W_SHAPES = {}
ALL_W_SHAPES.update(A_W_SHAPES)
ALL_W_SHAPES.update(B_W_SHAPES)
ALL_W_SHAPES.update(C_W_SHAPES)
ALL_W_SHAPES.update(D_W_SHAPES)


def build_full():
    nc = bass.Bass("TRN2", target_bir_lowering=False)
    stack = contextlib.ExitStack()
    x0 = nc.dram_tensor("x0", [D, T], F32, kind="ExternalInput").ap()
    yT = nc.dram_tensor("yT", [D, T], F32, kind="ExternalOutput").ap()
    xc1 = nc.dram_tensor("xc1", [D, T], F32, kind="Internal").ap()
    xc2 = nc.dram_tensor("xc2", [D, T], F32, kind="Internal").ap()
    ctx_hbm = nc.dram_tensor("ctx_hbm", [D, T], BF16, kind="Internal").ap()
    moe_acc = nc.dram_tensor("moe_acc", [D, T], F32, kind="Internal").ap()
    cin = {k: nc.dram_tensor(k, sh, dt, kind="ExternalInput").ap()
           for k, (sh, dt) in CONST_SHAPES.items()}
    wall = {k: nc.dram_tensor(k, sh, dt, kind="ExternalInput").ap()
            for k, (sh, dt) in ALL_W_SHAPES.items()}
    wall["c_mask_a"] = cin["c_mask_a"]
    wall["c_mask_c"] = cin["c_mask_c"]
    with stack:
        with nc.Block() as block:
            p = Prog(nc, block, stack)
            consts = emit_consts(p, cin)
            sA, nA = emit_phase_a(p, consts, x0, xc1, ctx_hbm, wall)
            sB, nB = emit_phase_b(p, consts, xc1, xc2, wall, s_prev=sA, n_prev=nA)
            sC, nC = emit_phase_c(p, consts, xc2, xc1, ctx_hbm, wall, s_prev=sB, n_prev=nB)
            sD, nD = emit_phase_d(p, consts, xc1, yT, moe_acc, wall, s_prev=sC, n_prev=nC)
            @block.sync
            def _(sy):
                sy.wait_ge(p.sem(sD), nD)
    return nc


_NC_CACHE = [None]

def kernel(**inputs):
    if _NC_CACHE[0] is None:
        _NC_CACHE[0] = build_full()
    nc = _NC_CACHE[0]
    shared = {}
    shared.update(host_consts())
    shared.update(host_prep_a(inputs))
    shared.update(host_prep_b(inputs))
    shared.update(host_prep_c(inputs))
    shared.update(host_prep_d(inputs))
    x = np.asarray(inputs["L0_attn_x"])
    in_maps = []
    for b in range(8):
        m = {"x0": np.ascontiguousarray(x[b].T)}
        m.update(shared)
        in_maps.append(m)
    res = run_bass_kernel_spmd(nc, in_maps, core_ids=list(range(8)))
    out = np.stack([np.asarray(res.results[b]["yT"]).T for b in range(8)])
    return np.ascontiguousarray(out.astype(np.float32))
